# version 12
# speedup vs baseline: 1.4573x; 1.1413x over previous
import math
import numpy as np
import concourse.bass as bass
import concourse.mybir as mybir
from concourse.bass_utils import run_bass_kernel_spmd

F32 = mybir.dt.float32
BF16 = mybir.dt.bfloat16
AF = mybir.ActivationFunctionType
ALU = mybir.AluOpType
AX = mybir.AxisListType

S = 2048
D = 1024
NCH = 8
NTH = 4
H = 16
DH = 64
NE = 32
NI = 512
FFN_H = 4096
RANK = 16
LN_EPS = 1e-5

ENGS = ("g", "t", "s", "v", "sy")


class Sched:
    def __init__(self, nc):
        self.nc = nc
        self.q = {k: [] for k in ENGS}
        self.c = dict(ldc=0, wa=0, wb=0, ma=0, mb=0, x=0, st=0, wst=0, pe=0, act=0, dve=0, pool=0)
        self.sem = {}
        self._lastself = {}

    OWNSEM = dict(t="pe", s="act", v="dve", g="pool")

    def em(self, eng, method, *args, inc=None, incv=1, **kw):
        own = self.OWNSEM.get(eng)
        if own is not None and eng != "t":
            last = self._lastself.get(eng, -1)
            if self.c[own] > last:
                self.wait(eng, own, self.c[own])
                self._lastself[eng] = self.c[own]
        sem = self.sem[inc] if inc else None
        if inc:
            self.c[inc] += incv

        def fn(e, method=method, args=args, kw=kw, sem=sem, incv=incv):
            ins = getattr(e, method)(*args, **kw)
            if sem is not None:
                ins.then_inc(sem, incv)
        self.q[eng].append(fn)
        return self.c[inc] if inc else None

    OWN = dict(t="pe", s="act", v="dve", g="pool")

    def selfwait(self, eng):
        self.wait(eng, self.OWN[eng], self.c[self.OWN[eng]])

    def wait(self, eng, semkey, val):
        if val <= 0:
            return
        sem = self.sem[semkey]
        self.q[eng].append(lambda e, sem=sem, val=val: e.wait_ge(sem, val))


class Prog:
    def __init__(self, nc, dev_io=False, stage=None):
        self.nc = nc
        self.S = Sched(nc)
        self.dev_io = dev_io
        self.stage = stage
        self._declare_dram()
        self._alloc_sbuf()
        self.PS = [nc.alloc_psum_tensor(f"PS{i}", [128, 512], F32) for i in range(8)]

    def _declare_dram(self):
        nc = self.nc
        dd = {}

        def din(name, shape, dt=F32):
            dd[name] = nc.dram_tensor(name, shape, dt, kind="ExternalInput").ap()
        din("xin", [2, D, S])
        din("L0_attn_Wqkv", [D, 3 * D])
        din("L0_attn_Wo", [D, D])
        din("L0_ffn_Wr", [D, NE])
        din("L0_ffn_br", [NE, 1])
        din("L0_ffn_Wg_s", [D, NI])
        din("L0_ffn_bg_s", [128, 4])
        din("L0_ffn_Wu_s", [D, NI])
        din("L0_ffn_bu_s", [128, 4])
        din("L0_ffn_Wd_s", [NI, D])
        din("L0_ffn_bd_s", [128, 8])
        din("L0_ffn_Wg", [NE, D, NI])
        din("L0_ffn_bg", [128, NE * 4])
        din("L0_ffn_Wu", [NE, D, NI])
        din("L0_ffn_bu", [128, NE * 4])
        din("L0_ffn_Wd", [NE, NI, D])
        din("L0_ffn_bd", [NE, D])
        din("L1_attn_filt", [128, 128])
        din("L1_attn_gamma", [128, 8])
        din("L1_attn_beta", [128, 8])
        din("L1_ffn_fast_bT", [RANK, RANK])
        din("L1_ffn_w_fc", [FFN_H, D])
        din("L1_ffn_w_proj", [D, FFN_H])
        din("L1_ffn_uT", [RANK, FFN_H])
        din("L1_ffn_vT", [D, RANK])
        din("L1_ffn_gate", [1, 1])
        din("c_cos", [128, S])
        din("c_sin", [128, S])
        din("c_oh", [32, NE * 128])
        din("c_p2", [128, 128])
        din("c_id", [128, 128])
        din("c_ones", [128, 128])
        dd["xout"] = nc.dram_tensor("xout", [2, D, S], F32, kind="ExternalOutput").ap()
        dd["xscr"] = nc.dram_tensor("xscr", [2, D, S], F32, kind="Internal").ap()
        dd["weff_fc"] = nc.dram_tensor("weff_fc", [NE, 128, 1024], BF16, kind="Internal").ap()
        dd["weff_proj"] = nc.dram_tensor("weff_proj", [NCH, 128, 4096], BF16, kind="Internal").ap()
        if self.stage:
            if self.stage != "att":
                dd["s_y"] = nc.dram_tensor("s_y", [2, D, S], BF16, kind="ExternalInput").ap()
                dd["s_xprev"] = nc.dram_tensor("s_xprev", [2, D, S], F32, kind="ExternalInput").ap()
            if self.stage != "ffn":
                dd["s_yout"] = nc.dram_tensor("s_yout", [2, D, S], BF16, kind="ExternalOutput").ap()
                dd["s_xn"] = nc.dram_tensor("s_xn", [2, D, S], F32, kind="ExternalOutput").ap()
        if self.dev_io:
            dd["dev_y"] = nc.dram_tensor("dev_y", [D, S], F32, kind="ExternalInput").ap()
            dd["dev_o"] = nc.dram_tensor("dev_o", [D, S], F32, kind="ExternalOutput").ap()
            dd["dev_dump"] = nc.dram_tensor("dev_dump", [128, 16384], F32, kind="ExternalOutput").ap()
        self.d = dd

    def _alloc_sbuf(self):
        nc = self.nc
        t = {}
        t["y"] = nc.alloc_sbuf_tensor("A_y", [128, NCH * S], BF16)
        t["b1"] = nc.alloc_sbuf_tensor("A_b1", [128, 2 * NCH * S], BF16)
        t["v"] = nc.alloc_sbuf_tensor("A_v", [128, 16 * 1040], BF16)
        t["w"] = nc.alloc_sbuf_tensor("A_w", [128, 20480], BF16)
        t["epix"] = nc.alloc_sbuf_tensor("A_epix", [128, S], F32)
        t["sq"] = nc.alloc_sbuf_tensor("A_sq", [128, 4096], BF16)
        t["misc"] = nc.alloc_sbuf_tensor("A_misc", [128, 1024], F32)
        t["cst"] = nc.alloc_sbuf_tensor("A_cst", [128, 4096], BF16)
        t["p2"] = nc.alloc_sbuf_tensor("c_p2s", [128, 128], BF16)
        t["id"] = nc.alloc_sbuf_tensor("c_ids", [128, 128], BF16)
        t["ones"] = nc.alloc_sbuf_tensor("c_oness", [128, 128], BF16)
        t["mb"] = nc.alloc_sbuf_tensor("c_mb", [128, 16], F32)
        t["stat"] = nc.alloc_sbuf_tensor("c_stat", [128, 128], F32)
        t["eb"] = nc.alloc_sbuf_tensor("c_eb", [128, NE * 8 + 16], F32)
        t["bdT"] = nc.alloc_sbuf_tensor("c_bdT", [32, 1024], BF16)
        t["wr"] = nc.alloc_sbuf_tensor("c_wr", [128, NCH * NE], BF16)
        t["br"] = nc.alloc_sbuf_tensor("c_br", [32, 1], F32)
        t["filt"] = nc.alloc_sbuf_tensor("c_filt", [128, 176], F32)
        t["gb"] = nc.alloc_sbuf_tensor("c_gb", [128, 16], F32)
        t["idf"] = nc.alloc_sbuf_tensor("c_idf", [128, 128], F32)
        t["g1"] = nc.alloc_sbuf_tensor("c_g1", [16, 16], BF16)
        self.t = t
        self.b1f = t["b1"].ap().bitcast(F32)
        self.vf = t["v"].ap().bitcast(F32)
        self.wf = t["w"].ap().bitcast(F32)
        self.sqf = t["sq"].ap().bitcast(F32)
        self.miscb = t["misc"].ap().bitcast(BF16)

    def ych(self, c, t0=0, n=512):
        return self.t["y"][:, c * S + t0: c * S + t0 + n]



def load_constants(P):
    S_, d, t = P.S, P.d, P.t
    for dst, src in [
        (t["p2"][:, :], d["c_p2"]),
        (t["id"][:, :], d["c_id"]),
        (t["ones"][:, :], d["c_ones"]),
        (t["eb"][:, 0:NE * 4], d["L0_ffn_bg"]),
        (t["eb"][:, NE * 4:NE * 8], d["L0_ffn_bu"]),
        (t["eb"][:, NE * 8:NE * 8 + 4], d["L0_ffn_bg_s"]),
        (t["eb"][:, NE * 8 + 4:NE * 8 + 8], d["L0_ffn_bu_s"]),
        (t["eb"][:, NE * 8 + 8:NE * 8 + 16], d["L0_ffn_bd_s"]),
        (t["bdT"][:, :], d["L0_ffn_bd"]),
        (t["wr"].ap().rearrange("p (c e) -> p c e", c=NCH), d["L0_ffn_Wr"].rearrange("(c p) e -> p c e", p=128)),
        (t["br"][:, :], d["L0_ffn_br"]),
        (t["filt"][:, 0:128], d["L1_attn_filt"]),
        (t["gb"][:, 0:8], d["L1_attn_gamma"]),
        (t["gb"][:, 8:16], d["L1_attn_beta"]),
        (t["idf"][:, :], d["c_id"]),
    ]:
        S_.em("g", "dma_start", dst, src, inc="ldc", incv=16)
    return S_.c["ldc"]


def load_cossin(P):
    S_, d, t = P.S, P.d, P.t
    S_.em("g", "dma_start", t["cst"][:, 0:S], d["c_cos"], inc="ldc", incv=16)
    S_.em("g", "dma_start", t["cst"][:, S:2 * S], d["c_sin"], inc="ldc", incv=16)
    return S_.c["ldc"]



def emit_att(P):
    S_, t, d, PS = P.S, P.t, P.d, P.PS
    AY, AB1, AV, AW, ASQ = t["y"], t["b1"], t["v"], t["w"], t["sq"]
    miscb, stat = P.miscb, t["stat"]
    WQK = [16384, 17408]
    WO = [18432, 19456]
    WVOFF = 8192

    cs_ld = load_cossin(P)

    for i in range(NCH):
        S_.em("g", "dma_start", AW[:, WVOFF + i * 1024: WVOFF + (i + 1) * 1024],
              d["L0_attn_Wqkv"][i * 128:(i + 1) * 128, 2 * D:3 * D], inc="ldc", incv=16)
    ldw_wv = S_.c["ldc"]

    def qkv_band_dma(jc):
        b = WQK[jc % 2]
        sem = "wa" if jc % 2 == 0 else "wb"
        S_.wait("g", sem, S_.c[sem])
        S_.em("g", "dma_start",
              AW[:, b:b + 1024].rearrange("p (ci j) -> p ci j", ci=8),
              d["L0_attn_Wqkv"][:, jc * 128:(jc + 1) * 128].rearrange("(ci p) j -> p ci j", p=128),
              inc=sem, incv=16)
        return (sem, S_.c[sem])

    ldw_band = [qkv_band_dma(0), qkv_band_dma(1)] + [None] * 14
    mmstop, rotid, rawev, dve3 = [], [], [], []

    def emit_rope_tail(gi):
        jc0, th0_ = gi // 4, gi % 4
        ps = PS[gi % 2]
        psr = PS[2 + gi % 2]
        sqslot = ASQ[:, (gi % 4) * 512:(gi % 4) * 512 + 512]
        S_.wait("s", "pe", mmstop[gi])
        if gi >= 4:
            S_.wait("s", "dve", dve3[gi - 4] - 2)
            S_.wait("s", "pe", rotid[gi - 4])
        S_.em("s", "activation", sqslot, ps[:, :], AF.Copy, inc="act")
        rawev.append(S_.c["act"])
        S_.wait("t", "act", rawev[-1])
        if gi >= 2:
            S_.wait("t", "dve", dve3[gi - 2] - 1)
        S_.em("t", "matmul", psr[:, :], t["p2"][:, :], sqslot, start=True, stop=True, inc="pe")
        rotid.append(S_.c["pe"])
        dst = AB1[:, jc0 * S + th0_ * 512: jc0 * S + (th0_ + 1) * 512]
        coss = t["cst"][:, th0_ * 512:(th0_ + 1) * 512]
        sins = t["cst"][:, S + th0_ * 512: S + (th0_ + 1) * 512]
        m1 = ASQ[:, 2048 + (gi % 4) * 512:2048 + (gi % 4) * 512 + 512]
        m2 = miscb[:, (gi % 4) * 512:(gi % 4) * 512 + 512]
        S_.wait("v", "act", rawev[-1])
        S_.em("v", "tensor_tensor", m1, sqslot, coss, ALU.mult, inc="dve")
        S_.wait("v", "pe", rotid[-1])
        S_.em("v", "tensor_tensor", m2, psr[:, :], sins, ALU.mult, inc="dve")
        S_.wait("v", "dve", S_.c["dve"])
        S_.em("v", "tensor_tensor", dst, m1, m2, ALU.add, inc="dve")
        dve3.append(S_.c["dve"])

    for jc in range(16):
        b = WQK[jc % 2]
        S_.wait("t", ldw_band[jc][0], ldw_band[jc][1])
        if jc == 0:
            S_.wait("t", "ldc", ldw_wv)
        for th in range(NTH):
            gi = jc * 4 + th
            ps = PS[gi % 2]
            if gi >= 2:
                S_.wait("t", "act", rawev[gi - 2])
            for i in range(NCH):
                S_.em("t", "matmul", ps[:, :], AW[:, b + i * 128: b + (i + 1) * 128],
                      P.ych(i, th * 512), start=(i == 0), stop=(i == 7),
                      inc="pe" if i == 7 else None)
            mmstop.append(S_.c["pe"])
            if gi >= 1:
                emit_rope_tail(gi - 1)
        if jc + 2 < 16:
            S_.wait("g", "pe", mmstop[-1])
            ldw_band[jc + 2] = qkv_band_dma(jc + 2)
    emit_rope_tail(63)
    dve_qk_end = S_.c["dve"]

    vview = AV.ap().rearrange("p (tc hd c) -> p tc hd c", tc=16, hd=16)
    S_.em("v", "memset", vview[:, :, :, 64:65], 1.0, inc="dve")
    dve_vones = S_.c["dve"]
    S_.wait("t", "ldc", ldw_wv)
    S_.wait("t", "dve", dve_vones)
    S_.wait("s", "dve", dve_qk_end)
    sq2 = ASQ[:, 2048:4096]
    vstop, vev = [], []
    mh_sq_act = None
    mh_mm_ids = []
    mh_red = []
    nv = 0
    for g in range(32):
        tc, half = g // 2, g % 2
        ps = PS[4 + (nv % 2)]
        if nv >= 2:
            S_.wait("t", "act", vev[nv - 2])
        for i in range(NCH):
            S_.em("t", "matmul", ps[:, :], AY[:, i * S + tc * 128:i * S + (tc + 1) * 128],
                  AW[:, WVOFF + i * 1024 + half * 512: WVOFF + i * 1024 + (half + 1) * 512],
                  start=(i == 0), stop=(i == 7), inc="pe" if i == 7 else None)
        vstop.append(S_.c["pe"])
        S_.wait("s", "pe", vstop[-1])
        for hh in range(8):
            hd = half * 8 + hh
            S_.em("s", "activation", AV[:, tc * 1040 + hd * 65: tc * 1040 + hd * 65 + 64],
                  ps[:, hh * 64:(hh + 1) * 64], AF.Copy, inc="act" if hh == 7 else None)
        vev.append(S_.c["act"])
        nv += 1
        srcq = g // 16
        cq = (g % 16) // 2
        hh = g % 2
        jc = srcq * 8 + cq
        col = srcq * 16 + 2 * cq + hh
        base = hh * 64
        if hh == 0:
            if mh_mm_ids:
                S_.wait("s", "pe", mh_mm_ids[-1])
            S_.em("s", "activation", sq2, AB1[:, jc * S:(jc + 1) * S], AF.Square, inc="act")
            mh_sq_act = S_.c["act"]
        S_.wait("t", "act", mh_sq_act)
        for q4 in range(4):
            bank = PS[6 + q4 % 2]
            if len(mh_red) >= 2:
                S_.wait("t", "dve", mh_red[-2])
            S_.em("t", "matmul", bank[0:64, :], t["ones"][base:base + 64, 0:64],
                  sq2[base:base + 64, q4 * 512:(q4 + 1) * 512], start=True, stop=True, inc="pe")
            mh_mm_ids.append(S_.c["pe"])
            S_.wait("v", "pe", mh_mm_ids[-1])
            if q4 == 0:
                S_.em("v", "reduce_max", stat[0:1, col:col + 1], bank[0:1, :], axis=AX.X, inc="dve")
            else:
                S_.em("v", "reduce_max", stat[0:1, 32:33], bank[0:1, :], axis=AX.X, inc="dve")
                S_.wait("v", "dve", S_.c["dve"])
                S_.em("v", "tensor_tensor", stat[0:1, col:col + 1], stat[0:1, col:col + 1],
                      stat[0:1, 32:33], ALU.max, inc="dve")
            mh_red.append(S_.c["dve"])
    S_.wait("s", "dve", mh_red[-1])
    S_.em("s", "activation", stat[0:1, 33:49], stat[0:1, 0:16], AF.Sqrt, inc="act")
    S_.em("s", "activation", stat[0:1, 49:65], stat[0:1, 16:32], AF.Sqrt, inc="act")
    act_sq = S_.c["act"]
    S_.wait("v", "act", act_sq)
    S_.em("v", "tensor_tensor", stat[0:1, 65:81], stat[0:1, 33:49], stat[0:1, 49:65], ALU.mult, inc="dve")
    S_.wait("v", "dve", S_.c["dve"])
    S_.em("v", "tensor_scalar_mul", stat[0:1, 81:97], stat[0:1, 65:81], -1.0 / 64.0, inc="dve")
    dve_mh = S_.c["dve"]
    S_.wait("s", "dve", dve_mh)
    S_.em("s", "activation", miscb[0:1, 0:97], stat[0:1, 0:97], AF.Copy, inc="act")
    act_mhc = S_.c["act"]
    S_.wait("t", "act", act_mhc)
    S_.wait("t", "dve", mh_red[-1])
    for h in range(H):
        S_.em("t", "matmul", PS[6][:, h:h + 1], t["ones"][0:1, 0:128],
              miscb[0:1, 81 + h:82 + h], start=True, stop=True,
              inc="pe" if h == H - 1 else None)
    pe_mb = S_.c["pe"]
    S_.wait("s", "pe", pe_mb)
    S_.em("s", "activation", t["mb"][:, :], PS[6][:, 0:16], AF.Copy, inc="act")
    act_mb = S_.c["act"]

    scid = [[None] * 16 for _ in range(64)]
    avid = [[None] * 16 for _ in range(64)]
    expid = [[None] * 16 for _ in range(64)]
    denid = [None] * 64
    bcid = [None] * 64
    rbev = [None] * 64
    recid = [None] * 64
    mulid = [None] * 64

    def emit_bcast(n):
        h, qb = divmod(n, 4)
        pbase = (h % 2) * 64
        db = 64 if h % 2 == 0 else 0
        S_.wait("t", recid[n][0], recid[n][1])
        if n >= 2:
            S_.wait("t", "act", rbev[n - 2])
        elif n == 0:
            S_.wait("t", "act", act_mb)
        S_.em("t", "matmul", PS[6 + n % 2][pbase:pbase + 64, :], t["ones"][db:db + 1, 0:64],
              ASQ[db:db + 1, 2048 + (n % 2) * 512:2048 + (n % 2) * 512 + 512],
              start=True, stop=True, inc="pe")
        bcid[n] = S_.c["pe"]
        S_.wait("s", "pe", bcid[n])
        if n >= 2:
            S_.wait("s", "dve", mulid[n - 2])
        S_.em("s", "activation", ASQ[pbase:pbase + 64, 2048 + (n % 2) * 512:2048 + (n % 2) * 512 + 512],
              PS[6 + n % 2][pbase:pbase + 64, :], AF.Copy, inc="act")
        rbev[n] = S_.c["act"]

    def emit_mul(n):
        h, qb = divmod(n, 4)
        pbase = (h % 2) * 64
        S_.wait("v", "act", rbev[n])
        dst = AY[pbase:pbase + 64, (h // 2) * S + qb * 512:(h // 2) * S + (qb + 1) * 512]
        S_.em("v", "tensor_tensor", dst, PS[4 + n % 2][pbase:pbase + 64, :],
              ASQ[pbase:pbase + 64, 2048 + (n % 2) * 512:2048 + (n % 2) * 512 + 512],
              ALU.mult, inc="dve")
        mulid[n] = S_.c["dve"]

    for n in range(64):
        h, qb = divmod(n, 4)
        hb = (h % 2) * 64
        even = (h % 2 == 0)
        qs = AB1[hb:hb + 64, (h // 2) * S + qb * 512:(h // 2) * S + (qb + 1) * 512]

        def emit_av(kt):
            S_.wait("t", "act", expid[n][kt])
            if kt == 0:
                if n >= 2:
                    S_.wait("t", "dve", mulid[n - 2])
                else:
                    S_.wait("t", "act", vev[30 + n])
            pslot = miscb[:, (kt % 4) * 512:(kt % 4) * 512 + 512]
            if even:
                S_.em("t", "matmul", PS[4 + n % 2][0:65, :],
                      AV[:, kt * 1040 + h * 65: kt * 1040 + h * 65 + 65], pslot,
                      start=(kt == 0), stop=(kt == 15), inc="pe")
            else:
                S_.em("t", "matmul", PS[4 + n % 2][64:128, :],
                      AV[:, kt * 1040 + h * 65: kt * 1040 + h * 65 + 64], pslot,
                      start=(kt == 0), stop=(kt == 15), inc="pe")
            avid[n][kt] = S_.c["pe"]
            if not even:
                if kt == 0 and n >= 2:
                    S_.wait("t", "act", rbev[n - 2])
                S_.em("t", "matmul", PS[6 + n % 2][0:64, :], t["ones"][:, 0:64], pslot,
                      start=(kt == 0), stop=(kt == 15), inc="pe")
                denid[n] = S_.c["pe"]
            if kt == 7 and n >= 1:
                emit_bcast(n - 1)
                emit_mul(n - 1)

        for pos in range(18):
            if pos < 16:
                kt = pos
                ps = PS[kt % 4]
                if kt >= 4:
                    S_.wait("t", "act", expid[n][kt - 4])
                elif n >= 1:
                    S_.wait("t", "act", expid[n - 1][12 + kt])
                else:
                    if kt < 2:
                        S_.wait("t", "act", rawev[60 + kt])
                    else:
                        S_.wait("t", "dve", dve3[60 + kt] - 1)
                ks = AB1[hb:hb + 64, (8 + h // 2) * S + kt * 128:(8 + h // 2) * S + (kt + 1) * 128]
                S_.em("t", "matmul", ps[:, :], ks, qs, start=True, stop=True, inc="pe")
                scid[n][kt] = S_.c["pe"]
                S_.wait("s", "pe", scid[n][kt])
                if kt >= 4:
                    S_.wait("s", "pe", avid[n][kt - 4])
                elif n >= 1:
                    S_.wait("s", "pe", avid[n - 1][12 + kt])
                else:
                    S_.wait("s", "dve", dve3[60 + kt])
                S_.em("s", "activation", miscb[:, (kt % 4) * 512:(kt % 4) * 512 + 512],
                      ps[:, :], AF.Exp, bias=t["mb"][:, h:h + 1], scale=0.125, inc="act")
                expid[n][kt] = S_.c["act"]
            if pos >= 2:
                emit_av(pos - 2)
        last_pe = denid[n] if not even else avid[n][15]
        db = 64 if even else 0
        S_.wait("s", "pe", last_pe)
        denrow_ps = PS[4 + n % 2][64:65, :] if even else PS[6 + n % 2][0:1, :]
        if n >= 2:
            S_.wait("s", "act", rbev[n - 2])
        S_.em("s", "activation", t["epix"][db:db + 1, (n % 2) * 512:(n % 2) * 512 + 512],
              denrow_ps, AF.Ln, inc="act")
        S_.em("s", "activation", ASQ[db:db + 1, 2048 + (n % 2) * 512:2048 + (n % 2) * 512 + 512],
              t["epix"][db:db + 1, (n % 2) * 512:(n % 2) * 512 + 512], AF.Exp, scale=-1.0, inc="act")
        recid[n] = ("act", S_.c["act"])
    emit_bcast(63)
    emit_mul(63)

    def wo_band_dma(j):
        b = WO[j % 2]
        sem = "wa" if j % 2 == 0 else "wb"
        S_.wait("g", sem, S_.c[sem])
        S_.em("g", "dma_start",
              AW[:, b:b + 1024].rearrange("p (ci j) -> p ci j", ci=8),
              d["L0_attn_Wo"][:, j * 128:(j + 1) * 128].rearrange("(ci p) j -> p ci j", p=128),
              inc=sem, incv=16)
        return (sem, S_.c[sem])

    S_.wait("g", "pe", mmstop[-1])
    wo_ld = [wo_band_dma(0), wo_band_dma(1)] + [None] * 6
    S_.wait("t", "dve", mulid[63])
    oev = []
    wo_stop = []
    ng = 0
    for j in range(NCH):
        b = WO[j % 2]
        S_.wait("t", wo_ld[j][0], wo_ld[j][1])
        for th in range(NTH):
            ps = PS[ng % 2]
            if ng >= 2:
                S_.wait("t", "act", oev[ng - 2])
            elif ng < 2:
                S_.wait("t", "act", expid[63][12 + ng])
            for i in range(NCH):
                S_.em("t", "matmul", ps[:, :], AW[:, b + i * 128: b + (i + 1) * 128],
                      AY[:, i * S + th * 512:i * S + (th + 1) * 512],
                      start=(i == 0), stop=(i == 7), inc="pe" if i == 7 else None)
            wo_stop.append(S_.c["pe"])
            S_.wait("s", "pe", wo_stop[-1])
            S_.em("s", "activation", AV[:, j * S + th * 512: j * S + (th + 1) * 512],
                  ps[:, :], AF.Copy, inc="act")
            oev.append(S_.c["act"])
            ng += 1
        if j + 2 < NCH:
            S_.wait("g", "pe", wo_stop[-1])
            wo_ld[j + 2] = wo_band_dma(j + 2)
    return dict(o_done_act=oev[-1], o_last_dve=mulid[63], rope_dve=dve_qk_end)



def emit_moe(P, gates=()):
    S_, t, d, PS = P.S, P.t, P.d, P.PS
    AY, AV, AW, ASQ, AM = t["y"], t["v"], t["w"], t["sq"], t["misc"]
    miscb, stat, eb, cst = P.miscb, t["stat"], t["eb"], t["cst"]
    b1f, sqf = P.b1f, P.sqf
    racc = b1f
    WG = [0, 4096]
    WU = [8192, 12288]
    WDB = [16384, 16896, 17408, 17920]

    for eng, sk, val in gates:
        S_.wait(eng, sk, val)
    S_.em("g", "dma_start", cst[0:32, :], d["c_oh"], inc="ldc", incv=16)
    idf = t["idf"]
    ldc_moe = S_.c["ldc"]

    def wguw_dma(which, e):
        if which == -1:
            slot, sem = 0, "ma"
            wg_src = d["L0_ffn_Wg_s"]
            wu_src = d["L0_ffn_Wu_s"]
        else:
            slot = (which + 1) % 2
            sem = "mb" if which % 2 == 0 else "ma"
            wg_src = d["L0_ffn_Wg"][which]
            wu_src = d["L0_ffn_Wu"][which]
        S_.em("g", "dma_start", AW[:, WG[slot]:WG[slot] + 4096].rearrange("p (i c) -> p i c", i=8),
              wg_src.rearrange("(i p) c -> p i c", p=128), inc=sem, incv=16)
        S_.em("g", "dma_start", AW[:, WU[slot]:WU[slot] + 4096].rearrange("p (i c) -> p i c", i=8),
              wu_src.rearrange("(i p) c -> p i c", p=128), inc=sem, incv=16)
        return (sem, S_.c[sem], slot)

    def wd_band_dma(which, j, slotk):
        sem = "wa" if j % 2 == 0 else "wb"
        src = d["L0_ffn_Wd_s"] if which == -1 else d["L0_ffn_Wd"][which]
        S_.em("g", "dma_start",
              AW[:, WDB[slotk]:WDB[slotk] + 512].rearrange("p (ii c) -> p ii c", ii=4),
              src[:, j * 128:(j + 1) * 128].rearrange("(ii p) c -> p ii c", p=128),
              inc=sem, incv=16)
        return (sem, S_.c[sem])

    ld_sh = wguw_dma(-1, 0)

    S_.wait("t", "ldc", ldc_moe)
    rt_stop = []
    sig_ev = []
    for th in range(NTH):
        ps = PS[2 + th % 2]
        if th >= 2:
            S_.wait("t", "act", sig_ev[th - 2])
        for i in range(NCH):
            S_.em("t", "matmul", ps[0:32, :], t["wr"][:, i * NE:(i + 1) * NE],
                  P.ych(i, th * 512), start=(i == 0), stop=(i == 7),
                  inc="pe" if i == 7 else None)
        rt_stop.append(S_.c["pe"])
        S_.wait("s", "pe", rt_stop[-1])
        S_.em("s", "activation", sqf[0:32, th * 512:(th + 1) * 512], ps[0:32, :],
              AF.Sigmoid, bias=t["br"][:, :], inc="act")
        sig_ev.append(S_.c["act"])

    afft_ev = []
    tr_pe = []
    for c in range(16):
        S_.wait("t", "act", sig_ev[min(3, c // 4)])
        if len(tr_pe) >= 2:
            S_.wait("t", "dve", afft_ev[c - 2])
        if c < 2:
            S_.wait("t", "act", sig_ev[2 + c % 2])
        S_.em("t", "transpose", PS[2 + c % 2][:, 0:32], sqf[0:32, c * 128:(c + 1) * 128],
              idf[0:32, 0:32], inc="pe")
        tr_pe.append(S_.c["pe"])
        S_.wait("v", "pe", tr_pe[-1])
        S_.em("v", "tensor_copy", AM[:, 512 + c * 32:512 + (c + 1) * 32],
              PS[2 + c % 2][:, 0:32], inc="dve")
        afft_ev.append(S_.c["dve"])

    wtr_pe = []
    wT_ev = []
    for c in range(16):
        afftc = AM[:, 512 + c * 32:512 + (c + 1) * 32]
        work = stat[:, 0:32]
        sel = stat[:, 32:64]
        msk = stat[:, 64:96]
        if wtr_pe:
            S_.wait("v", "pe", wtr_pe[-1])
        S_.em("v", "tensor_copy", work, afftc, inc="dve")
        S_.em("v", "memset", sel, 0.0, inc="dve")
        for it in range(8):
            S_.em("v", "reduce_max", stat[:, 96:97], work, axis=AX.X, inc="dve")
            S_.em("v", "tensor_scalar", msk, work, stat[:, 96:97], None, ALU.is_ge, inc="dve")
            S_.em("v", "tensor_tensor", sel, sel, msk, ALU.max, inc="dve")
            S_.em("v", "scalar_tensor_tensor", work, msk, -2.0, work, ALU.mult, ALU.add, inc="dve")
        S_.em("v", "tensor_tensor", msk, afftc, sel, ALU.mult, inc="dve")
        S_.em("v", "reduce_sum", stat[:, 96:97], msk, axis=AX.X, inc="dve")
        S_.em("v", "tensor_scalar_add", stat[:, 97:98], stat[:, 96:97], 1e-9, inc="dve")
        S_.em("v", "reciprocal", stat[:, 98:99], stat[:, 97:98], inc="dve")
        S_.em("v", "tensor_scalar", stat[:, 0:32], msk, stat[:, 98:99], None, ALU.mult, inc="dve")
        dve_w = S_.c["dve"]
        S_.wait("t", "dve", dve_w)
        if len(wtr_pe) >= 2:
            S_.wait("t", "act", wT_ev[c - 2])
        else:
            S_.wait("t", "dve", afft_ev[14 + c])
        S_.em("t", "transpose", PS[2 + c % 2][0:32, 0:128], stat[:, 0:32], idf[:, :], inc="pe")
        wtr_pe.append(S_.c["pe"])
        S_.wait("s", "pe", wtr_pe[-1])
        S_.em("s", "activation", ASQ[0:32, 2048 + c * 128:2048 + (c + 1) * 128],
              PS[2 + c % 2][0:32, 0:128], AF.Copy, inc="act")
        wT_ev.append(S_.c["act"])
    wT_done = wT_ev[-1]
    wT = ASQ[0:32, 2048:4096]

    SG = [0, 512]
    HT = [1024, 1536]
    HREG = [0, 8192]
    gu_idx = 0
    d_idx = 0
    wb_idx = 0
    h_done = {}
    dn_add = []
    racc_init = {}
    sil_ev = []
    upcons = []
    wb_ev = []
    wb_cons = []
    dn_stop = []
    last_dn_pe = 0
    ld_exp = [None] * NE
    ld_wd = {}

    def gate_up(which):
        nonlocal gu_idx, wb_idx
        slot = 0 if which == -1 else (which + 1) % 2
        sem, cnt, _ = ld_sh if which == -1 else ld_exp[which]
        S_.wait("t", sem, cnt)
        hreg = HREG[0 if which == -1 else which % 2]
        for th in range(NTH):
            if which >= 0:
                S_.wait("t", "act", wT_done)
                if wb_idx >= 2:
                    S_.wait("t", "dve", wb_cons[wb_idx - 2])
                S_.em("t", "matmul", PS[6 + wb_idx % 2][:, :], cst[0:32, which * 128:(which + 1) * 128],
                      wT[:, th * 512:(th + 1) * 512], start=True, stop=True, inc="pe")
                pe_wb = S_.c["pe"]
                S_.wait("s", "pe", pe_wb)
                if wb_idx >= 2:
                    S_.wait("s", "dve", wb_cons[wb_idx - 2])
                S_.em("s", "activation", miscb[:, (wb_idx % 2) * 512:(wb_idx % 2) * 512 + 512],
                      PS[6 + wb_idx % 2][:, :], AF.Copy, inc="act")
                wb_ev.append(S_.c["act"])
            for jj in range(4):
                psg = PS[0 + gu_idx % 2]
                psu = PS[2 + gu_idx % 2]
                if gu_idx >= 2:
                    S_.wait("t", "act", sil_ev[gu_idx - 2])
                    S_.wait("t", "dve", upcons[gu_idx - 2])
                elif which == -1 and gu_idx < 2:
                    S_.wait("t", "act", wT_ev[14 + gu_idx])
                for i in range(NCH):
                    S_.em("t", "matmul", psg[:, :], AW[:, WG[slot] + i * 512 + jj * 128: WG[slot] + i * 512 + (jj + 1) * 128],
                          P.ych(i, th * 512), start=(i == 0), stop=(i == 7),
                          inc="pe" if i == 7 else None)
                peg = S_.c["pe"]
                for i in range(NCH):
                    S_.em("t", "matmul", psu[:, :], AW[:, WU[slot] + i * 512 + jj * 128: WU[slot] + i * 512 + (jj + 1) * 128],
                          P.ych(i, th * 512), start=(i == 0), stop=(i == 7),
                          inc="pe" if i == 7 else None)
                peu = S_.c["pe"]
                bgc = eb[:, NE * 8 + jj:NE * 8 + jj + 1] if which == -1 else eb[:, which * 4 + jj:which * 4 + jj + 1]
                buc = eb[:, NE * 8 + 4 + jj:NE * 8 + 4 + jj + 1] if which == -1 else eb[:, NE * 4 + which * 4 + jj:NE * 4 + which * 4 + jj + 1]
                S_.wait("s", "pe", peg)
                if gu_idx >= 2:
                    S_.wait("s", "dve", upcons[gu_idx - 2])
                S_.em("s", "activation", ASQ[:, SG[gu_idx % 2]:SG[gu_idx % 2] + 512], psg[:, :],
                      AF.Silu, bias=bgc, inc="act")
                sil_ev.append(S_.c["act"])
                S_.wait("v", "pe", peu)
                S_.wait("v", "act", sil_ev[-1])
                hdst = AV[:, hreg + jj * S + th * 512: hreg + jj * S + (th + 1) * 512]
                if which == -1:
                    S_.em("v", "scalar_tensor_tensor", hdst, psu[:, :], buc,
                          ASQ[:, SG[gu_idx % 2]:SG[gu_idx % 2] + 512], ALU.add, ALU.mult, inc="dve")
                else:
                    S_.em("v", "scalar_tensor_tensor", ASQ[:, HT[gu_idx % 2]:HT[gu_idx % 2] + 512],
                          psu[:, :], buc, ASQ[:, SG[gu_idx % 2]:SG[gu_idx % 2] + 512],
                          ALU.add, ALU.mult, inc="dve")
                    S_.wait("v", "act", wb_ev[-1])
                    S_.em("v", "tensor_tensor", hdst,
                          ASQ[:, HT[gu_idx % 2]:HT[gu_idx % 2] + 512],
                          miscb[:, (wb_idx % 2) * 512:(wb_idx % 2) * 512 + 512], ALU.mult, inc="dve")
                upcons.append(S_.c["dve"])
                h_done[(0 if which == -1 else which % 2, th)] = S_.c["dve"]
                gu_idx += 1
            if which >= 0:
                wb_cons.append(S_.c["dve"])
                wb_idx += 1

    def downs(members):
        nonlocal d_idx, last_dn_pe
        for j in range(NCH):
            if j < 2:
                for mi, m in enumerate(members):
                    wd_band_dma(m, j, (j % 2) * 2 + mi)
                ld_wd[j] = ("wa" if j % 2 == 0 else "wb", S_.c["wa" if j % 2 == 0 else "wb"])
            for th in range(NTH):
                psd = PS[4 + d_idx % 2]
                if d_idx >= 2:
                    S_.wait("t", "dve", dn_add[d_idx - 2])
                first = True
                sem, cnt = ld_wd[j]
                S_.wait("t", sem, cnt)
                for mi, m in enumerate(members):
                    hreg = HREG[0 if m == -1 else m % 2]
                    S_.wait("t", "dve", h_done[(0 if m == -1 else m % 2, th)])
                    for ii in range(4):
                        S_.em("t", "matmul", psd[:, :],
                              AW[:, WDB[(j % 2) * 2 + mi] + ii * 128: WDB[(j % 2) * 2 + mi] + (ii + 1) * 128],
                              AV[:, hreg + ii * S + th * 512: hreg + ii * S + (th + 1) * 512],
                              start=first, stop=(mi == len(members) - 1 and ii == 3),
                              inc="pe" if (mi == len(members) - 1 and ii == 3) else None)
                        first = False
                dn_stop.append(S_.c["pe"])
                last_dn_pe = S_.c["pe"]
                rdst = racc[:, j * S + th * 512: j * S + (th + 1) * 512]
                if members == [-1]:
                    S_.wait("v", "pe", last_dn_pe)
                    S_.em("v", "tensor_scalar_add", rdst, psd[:, :],
                          eb[:, NE * 8 + 8 + j:NE * 8 + 8 + j + 1], inc="dve")
                    racc_init[(j, th)] = S_.c["dve"]
                    dn_add.append(S_.c["dve"])
                else:
                    S_.wait("v", "pe", last_dn_pe)
                    S_.em("v", "tensor_tensor", rdst, rdst, psd[:, :], ALU.add, inc="dve")
                    dn_add.append(S_.c["dve"])
                d_idx += 1
            if j + 2 < NCH:
                S_.wait("g", "pe", last_dn_pe)
                for mi, m in enumerate(members):
                    wd_band_dma(m, j + 2, ((j + 2) % 2) * 2 + mi)
                ld_wd[j + 2] = ("wa" if j % 2 == 0 else "wb", S_.c["wa" if j % 2 == 0 else "wb"])

    gate_up(-1)
    downs([-1])
    for e in range(NE):
        if e == 0:
            ld_exp[0] = wguw_dma(0, 0)
            S_.wait("g", "pe", S_.c["pe"])
            ld_exp[1] = wguw_dma(1, 0)
        elif e + 1 < NE:
            S_.wait("g", "pe", S_.c["pe"])
            ld_exp[e + 1] = wguw_dma(e + 1, 0)
        gate_up(e)
        if e % 2 == 1:
            downs([e - 1, e])
    for j in range(NCH):
        for th in range(NTH):
            psd = PS[4 + d_idx % 2]
            if d_idx >= 2:
                S_.wait("t", "dve", dn_add[d_idx - 2])
            S_.em("t", "matmul", psd[:, :], t["bdT"][:, j * 128:(j + 1) * 128],
                  wT[:, th * 512:(th + 1) * 512], start=True, stop=True, inc="pe")
            pj = S_.c["pe"]
            rdst = racc[:, j * S + th * 512: j * S + (th + 1) * 512]
            S_.wait("v", "pe", pj)
            S_.em("v", "tensor_tensor", rdst, rdst, psd[:, :], ALU.add, inc="dve")
            dn_add.append(S_.c["dve"])
            d_idx += 1
    dve_racc = S_.c["dve"]
    oev = []
    S_.wait("s", "dve", dve_racc)
    for c in range(NCH):
        S_.em("s", "activation", AV[:, c * S:(c + 1) * S], racc[:, c * S:(c + 1) * S], AF.Copy, inc="act")
        oev.append(S_.c["act"])
    return dict(o_done_act=oev[-1])


def build_dev_moe():
    nc = bass.Bass("TRN2", target_bir_lowering=False)
    P = Prog(nc, dev_io=True)
    S_ = P.S
    with contextlib.ExitStack() as stack:
        _make_sems(nc, S_, stack)
        block = stack.enter_context(nc.Block())
        load_constants(P)
        S_.em("g", "dma_start",
              P.t["y"].ap().rearrange("p (c t) -> p c t", c=NCH),
              P.d["dev_y"].rearrange("(c p) t -> p c t", p=128), inc="ldc", incv=16)
        ret = emit_moe(P)
        S_.wait("g", "act", ret["o_done_act"])
        S_.em("g", "dma_start",
              P.d["dev_o"].rearrange("(c p) t -> p c t", p=128),
              P.t["v"][:, 0:NCH * S].rearrange("p (c t) -> p c t", c=NCH),
              inc="st", incv=16)
        S_.wait("g", "st", S_.c["st"])
        _finish_queues(P, block)
    return nc



ROW_MU = (0, 512)
ROW_R = (1024, 1536)
ROW_B, ROW_C, ROW_V = 2048, 2560, 3072
ROW_BF = (3584, 3840)


def ln_rows(P, s_ps, q_ps, sl, pe_sum_id):
    S_ = P.S
    b1f = P.b1f
    mu = b1f[0:1, ROW_MU[sl % 2]:ROW_MU[sl % 2] + 512]
    rr = b1f[0:1, ROW_R[sl % 2]:ROW_R[sl % 2] + 512]
    B = b1f[0:1, ROW_B:ROW_B + 512]
    C = b1f[0:1, ROW_C:ROW_C + 512]
    V = b1f[0:1, ROW_V:ROW_V + 512]
    S_.wait("v", "pe", pe_sum_id)
    S_.em("v", "tensor_scalar_mul", mu, s_ps, 1.0 / 1024.0, inc="dve")
    dve_mu = S_.c["dve"]
    S_.em("v", "tensor_scalar_mul", B, q_ps, 1.0 / 1024.0, inc="dve")
    S_.em("v", "tensor_tensor", C, mu, mu, ALU.mult, inc="dve")
    S_.em("v", "tensor_tensor", V, B, C, ALU.subtract, inc="dve")
    S_.em("v", "tensor_scalar_add", C, V, LN_EPS, inc="dve")
    dve_v = S_.c["dve"]
    S_.wait("s", "dve", dve_v)
    S_.em("s", "activation", B, C, AF.Ln, inc="act")
    S_.wait("v", "act", S_.c["act"])
    S_.em("v", "tensor_scalar_mul", V, B, -0.5, inc="dve")
    S_.wait("s", "dve", S_.c["dve"])
    S_.em("s", "activation", rr, V, AF.Exp, inc="act")
    return dve_mu, S_.c["act"], mu, rr


def ln_rows_bf(P, sl, mu, rr, act_r):
    S_ = P.S
    mb = P.b1f.bitcast(BF16)
    mu_bf = mb[0:1, 2 * ROW_BF[0] + (sl % 2) * 1024: 2 * ROW_BF[0] + (sl % 2) * 1024 + 512]
    r_bf = mb[0:1, 2 * ROW_BF[0] + (sl % 2) * 1024 + 512: 2 * ROW_BF[0] + (sl % 2) * 1024 + 1024]
    S_.wait("s", "act", act_r)
    S_.em("s", "activation", mu_bf, mu, AF.Copy, inc="act")
    S_.em("s", "activation", r_bf, rr, AF.Copy, inc="act")
    return S_.c["act"], mu_bf, r_bf



def emit_wav(P, entry=(), dump=None):
    S_, t, d, PS = P.S, P.t, P.d, P.PS
    AY, AV, ASQ, AM, AX_ = t["y"], t["v"], t["sq"], t["misc"], t["epix"]
    miscb, filt = P.miscb, t["filt"]
    b1f, vf = P.b1f, P.vf
    for eng, sk, val in entry:
        S_.wait(eng, sk, val)

    def fcol(f, c, k):
        return filt[:, f * 32 + c * 4 + k: f * 32 + c * 4 + k + 1]

    def csyn(fs, c, s):
        return filt[:, 128 + fs * 24 + c * 3 + s: 128 + fs * 24 + c * 3 + s + 1]

    for fs in range(2):
        f = 2 + fs
        for c in range(8):
            base = 128 + fs * 24 + c * 3
            S_.em("v", "tensor_tensor", filt[:, base + 0:base + 1], fcol(f, c, 1), fcol(f, c, 2), ALU.add, inc="dve")
            S_.em("v", "tensor_tensor", filt[:, base + 1:base + 2], fcol(f, c, 0), fcol(f, c, 1), ALU.add, inc="dve")
            S_.em("v", "tensor_tensor", filt[:, base + 2:base + 3], fcol(f, c, 2), fcol(f, c, 3), ALU.add, inc="dve")
    dve_taps = S_.c["dve"]
    S_.wait("g", "dve", dve_taps)

    eng_inc = dict(v="dve", g="pool")
    pool_tmp = t["epix"][:, 1024:2048]

    def mac(eng, dst, srcap, f):
        if eng == "v":
            S_.em(eng, "scalar_tensor_tensor", dst, srcap, f, dst, ALU.mult, ALU.add, inc="dve")
        else:
            n = dst.ap[-1][1]
            S_.em(eng, "tensor_scalar_mul", pool_tmp[:, 0:n], srcap, f, inc="pool")
            S_.em(eng, "tensor_tensor", dst, dst, pool_tmp[:, 0:n], ALU.add, inc="pool")

    def conv_down(eng, src, dst, fidx, n_out):
        S_.em(eng, "tensor_scalar_mul", dst(0, n_out), src(0, n_out, 2), fcol(fidx, c, 1), inc=eng_inc[eng])
        mac(eng, dst(0, n_out), src(1, n_out, 2), fcol(fidx, c, 2))
        mac(eng, dst(1, n_out - 1), src(1, n_out - 1, 2), fcol(fidx, c, 0))
        mac(eng, dst(0, n_out - 1), src(2, n_out - 1, 2), fcol(fidx, c, 3))
        return S_.c[eng_inc[eng]]

    l0_ids, h0_ids = [None] * 8, [None] * 8
    for c in range(8):
        ysrc = lambda s0, n, st, c=c: AY[:, c * S + s0: c * S + s0 + (n - 1) * st + 1: st]
        l0d = lambda s0, n, c=c: b1f[:, c * 1024 + s0: c * 1024 + s0 + n]
        h0d = lambda s0, n, c=c: b1f[:, 8192 + c * 1024 + s0: 8192 + c * 1024 + s0 + n]
        l0_ids[c] = conv_down("v", ysrc, l0d, 0, 1024)
        h0_ids[c] = conv_down("v", ysrc, h0d, 1, 1024)
    l1_ids, h1_ids = [None] * 8, [None] * 8
    for c in range(8):
        lsrc = lambda s0, n, st, c=c: b1f[:, c * 1024 + s0: c * 1024 + s0 + (n - 1) * st + 1: st]
        l1d = lambda s0, n, c=c: vf[:, c * 512 + s0: c * 512 + s0 + n]
        h1d = lambda s0, n, c=c: vf[:, 4096 + c * 512 + s0: 4096 + c * 512 + s0 + n]
        l1_ids[c] = conv_down("v", lsrc, l1d, 0, 512)
        h1_ids[c] = conv_down("v", lsrc, h1d, 1, 512)
    if dump == "l1":
        return dict(dump_src=vf[:, 0:8192], done=("dve", l1_ids[7]), done2=("pool", h1_ids[7]))
    q_ids = [None] * 8
    qbf_ids = [None] * 8
    exp_prev = None
    pass
    for c in range(8):
        l1c = vf[:, c * 512:(c + 1) * 512]
        m0 = AM[:, 0:512]
        m1 = AM[:, 512:1024]
        if exp_prev is not None:
            S_.wait("v", "act", exp_prev)
        S_.em("v", "tensor_scalar_min", m0, l1c, 0.0, inc="dve")
        d0 = S_.c["dve"]
        S_.wait("s", "dve", d0)
        S_.em("s", "activation", m1, m0, AF.Exp, inc="act")
        exp_prev = S_.c["act"]
        S_.wait("v", "act", S_.c["act"])
        S_.em("v", "scalar_tensor_tensor", b1f[:, 4096 + c * 512:4096 + (c + 1) * 512], l1c, 0.0, m1,
              ALU.max, ALU.add, inc="dve")
        q_ids[c] = S_.c["dve"]
        S_.wait("s", "dve", q_ids[c])
        S_.em("s", "activation", ASQ[:, c * 512:(c + 1) * 512],
              b1f[:, 4096 + c * 512:4096 + (c + 1) * 512], AF.Copy, inc="act")
        qbf_ids[c] = S_.c["act"]

    S_.em("v", "memset", miscb[:, 256 + 64:256 + 260:65], 1.0, inc="dve")
    dve_vones = S_.c["dve"]
    la_done = []
    kvz_evs = []
    rb_evs = []
    prev_evs = {}
    import os as _os
    _nh = int(_os.environ.get('WAVH', str(H)))
    for h in range(_nh):
        hb = (h % 2) * 64
        hc = h // 2
        even = (h % 2 == 0)
        qh = ASQ[hb:hb + 64, hc * 512:(hc + 1) * 512]
        qhf = b1f[hb:hb + 64, 4096 + hc * 512:4096 + (hc + 1) * 512]
        l1h = vf[hb:hb + 64, hc * 512:(hc + 1) * 512]
        S_.wait("t", "dve", max(q_ids[hc], dve_vones))
        S_.wait("t", "act", qbf_ids[hc])
        ktm_ev, va_ev = [], []
        for ncc in range(4):
            psb = PS[ncc % 2][:, 0:64]
            if h > 0 or ncc >= 2:
                S_.wait("t", "act", prev_evs.get(("k", (h * 4 + ncc - 2)), 0))
            S_.em("t", "transpose", psb, qhf[:, ncc * 128:(ncc + 1) * 128],
                  t["idf"][hb:hb + 64, hb:hb + 64], inc="pe")
            pk = S_.c["pe"]
            S_.wait("s", "pe", pk)
            S_.em("s", "activation", miscb[:, ncc * 64:(ncc + 1) * 64], psb, AF.Copy, inc="act")
            ktm_ev.append(S_.c["act"])
            prev_evs[("k", h * 4 + ncc)] = S_.c["act"]
            psf = PS[2 + ncc % 2][:, 0:64]
            if h > 0 or ncc >= 2:
                S_.wait("t", "act", prev_evs.get(("v", (h * 4 + ncc - 2)), 0))
            S_.em("t", "transpose", psf, l1h[:, ncc * 128:(ncc + 1) * 128],
                  t["idf"][hb:hb + 64, hb:hb + 64], inc="pe")
            pv = S_.c["pe"]
            S_.wait("s", "pe", pv)
            S_.em("s", "activation", miscb[:, 256 + ncc * 65:256 + ncc * 65 + 64], psf, AF.Copy, inc="act")
            va_ev.append(S_.c["act"])
            prev_evs[("v", h * 4 + ncc)] = S_.c["act"]
        S_.wait("t", "act", va_ev[-1])
        S_.wait("t", "act", ktm_ev[-1])
        if h >= 2:
            S_.wait("t", "act", kvz_evs[h - 2])
        for ncc in range(4):
            S_.em("t", "matmul", PS[4][hb:hb + 64, 0:65], miscb[:, ncc * 64:(ncc + 1) * 64],
                  miscb[:, 256 + ncc * 65:256 + ncc * 65 + 65], start=(ncc == 0), stop=(ncc == 3),
                  inc="pe" if ncc == 3 else None)
        pe_kv = S_.c["pe"]
        S_.wait("s", "pe", pe_kv)
        S_.em("s", "activation", miscb[hb:hb + 64, 516:581], PS[4][hb:hb + 64, 0:65], AF.Copy, inc="act")
        act_kvz = S_.c["act"]
        kvz_evs.append(act_kvz)
        S_.wait("t", "act", act_kvz)
        if h >= 1:
            S_.wait("t", "dve", la_done[h - 1])
        if even:
            S_.em("t", "matmul", PS[5][0:65, :], miscb[0:64, 516:581], qh, start=True, stop=True, inc="pe")
            pe_nd = S_.c["pe"]
            denrow_ps = PS[5][64:65, :]
            db = 64
        else:
            S_.em("t", "matmul", PS[5][64:128, :], miscb[64:128, 516:580], qh, start=True, stop=True, inc="pe")
            S_.em("t", "matmul", PS[5][0:1, :], miscb[64:128, 580:581], qh, start=True, stop=True, inc="pe")
            pe_nd = S_.c["pe"]
            denrow_ps = PS[5][0:1, :]
            db = 0
        S_.wait("v", "pe", pe_nd)
        S_.em("v", "tensor_scalar_add", AX_[db:db + 1, 1024:1536], denrow_ps, 1e-6, inc="dve")
        S_.em("v", "reciprocal", miscb[db:db + 1, 581:1093], AX_[db:db + 1, 1024:1536], inc="dve")
        dve_rec = S_.c["dve"]
        S_.wait("t", "dve", dve_rec)
        if h >= 2:
            S_.wait("t", "act", rb_evs[h - 2])
        S_.em("t", "matmul", PS[6][hb:hb + 64, :], t["ones"][db:db + 1, 0:64],
              miscb[db:db + 1, 581:1093], start=True, stop=True, inc="pe")
        pe_bc = S_.c["pe"]
        S_.wait("s", "pe", pe_bc)
        S_.em("s", "activation", miscb[hb:hb + 64, 1093:1605], PS[6][hb:hb + 64, :], AF.Copy, inc="act")
        act_rb = S_.c["act"]
        rb_evs.append(act_rb)
        S_.wait("v", "act", act_rb)
        S_.em("v", "tensor_tensor", l1h, PS[5][hb:hb + 64, :], miscb[hb:hb + 64, 1093:1605],
              ALU.mult, inc="dve")
        la_done.append(S_.c["dve"])
    dve_la = S_.c["dve"]
    if dump == "lowa":
        return dict(dump_src=P.vf[:, 0:8192], done=("dve", dve_la))
    S_.wait("g", "dve", dve_la)

    syn1_v, syn1_g = [None] * 8, [None] * 8
    for c in range(8):
        la = lambda s0, n, c=c: vf[:, c * 512 + s0: c * 512 + s0 + n]
        hh = lambda s0, n, c=c: vf[:, 4096 + c * 512 + s0: 4096 + c * 512 + s0 + n]
        ev = lambda s0, n, c=c: b1f[:, c * 1024 + 2 * s0: c * 1024 + 2 * s0 + 2 * (n - 1) + 1: 2]
        od = lambda s0, n, c=c: b1f[:, c * 1024 + 1 + 2 * s0: c * 1024 + 1 + 2 * s0 + 2 * (n - 1) + 1: 2]
        S_.em("v", "tensor_scalar_mul", ev(0, 512), la(0, 512), csyn(0, c, 0), inc="dve")
        S_.em("v", "scalar_tensor_tensor", ev(1, 511), la(0, 511), fcol(2, c, 0), ev(1, 511), ALU.mult, ALU.add, inc="dve")
        S_.em("v", "scalar_tensor_tensor", ev(0, 511), la(1, 511), fcol(2, c, 3), ev(0, 511), ALU.mult, ALU.add, inc="dve")
        S_.em("v", "scalar_tensor_tensor", ev(0, 512), hh(0, 512), csyn(1, c, 0), ev(0, 512), ALU.mult, ALU.add, inc="dve")
        S_.em("v", "scalar_tensor_tensor", ev(1, 511), hh(0, 511), fcol(3, c, 0), ev(1, 511), ALU.mult, ALU.add, inc="dve")
        S_.em("v", "scalar_tensor_tensor", ev(0, 511), hh(1, 511), fcol(3, c, 3), ev(0, 511), ALU.mult, ALU.add, inc="dve")
        syn1_v[c] = S_.c["dve"]
        S_.em("v", "tensor_scalar_mul", od(0, 512), la(0, 512), csyn(0, c, 1), inc="dve")
        mac("v", od(0, 511), la(1, 511), csyn(0, c, 2))
        mac("v", od(0, 512), hh(0, 512), csyn(1, c, 1))
        mac("v", od(0, 511), hh(1, 511), csyn(1, c, 2))
        syn1_g[c] = S_.c["dve"]

    if dump == "low1":
        return dict(dump_src=b1f[:, 0:8192], done=("dve", syn1_v[7]), done2=("pool", syn1_g[7]))
    z_ids = [None] * 8
    pool_z = [None] * 8
    S_.wait("g", "dve", max(syn1_v[7], syn1_g[7]))
    for c in range(8):
        l1s = lambda s0, n, st=1, c=c: b1f[:, c * 1024 + s0: c * 1024 + s0 + (n - 1) * st + 1: st]
        h0s = lambda s0, n, st=1, c=c: b1f[:, 8192 + c * 1024 + s0: 8192 + c * 1024 + s0 + (n - 1) * st + 1: st]
        acc_e = AM[:, 0:1024]
        acc_o = AX_[:, 0:1024]
        if c > 0 and z_ids[c - 1] is not None:
            S_.wait("v", "act", z_ids[c - 1])
        S_.em("v", "tensor_copy", acc_e, AY[:, c * S: c * S + 2047: 2], inc="dve")
        S_.em("v", "scalar_tensor_tensor", acc_e, l1s(0, 1024), csyn(0, c, 0), acc_e, ALU.mult, ALU.add, inc="dve")
        S_.em("v", "scalar_tensor_tensor", acc_e[:, 1:1024], l1s(0, 1023), fcol(2, c, 0), acc_e[:, 1:1024], ALU.mult, ALU.add, inc="dve")
        S_.em("v", "scalar_tensor_tensor", acc_e[:, 0:1023], l1s(1, 1023), fcol(2, c, 3), acc_e[:, 0:1023], ALU.mult, ALU.add, inc="dve")
        S_.em("v", "scalar_tensor_tensor", acc_e, h0s(0, 1024), csyn(1, c, 0), acc_e, ALU.mult, ALU.add, inc="dve")
        S_.em("v", "scalar_tensor_tensor", acc_e[:, 1:1024], h0s(0, 1023), fcol(3, c, 0), acc_e[:, 1:1024], ALU.mult, ALU.add, inc="dve")
        S_.em("v", "scalar_tensor_tensor", acc_e[:, 0:1023], h0s(1, 1023), fcol(3, c, 3), acc_e[:, 0:1023], ALU.mult, ALU.add, inc="dve")
        dve_acc = S_.c["dve"]
        S_.wait("s", "dve", dve_acc)
        S_.em("s", "activation", AV[:, c * S: c * S + 2047: 2], acc_e, AF.Copy, inc="act")
        z_ids[c] = S_.c["act"]
        S_.em("g", "tensor_copy", acc_o, AY[:, c * S + 1: c * S + 2048: 2], inc="pool")
        S_.wait("v", "pool", S_.c["pool"])
        mac("v", acc_o, l1s(0, 1024), csyn(0, c, 1))
        mac("v", acc_o[:, 0:1023], l1s(1, 1023), csyn(0, c, 2))
        mac("v", acc_o, h0s(0, 1024), csyn(1, c, 1))
        mac("v", acc_o[:, 0:1023], h0s(1, 1023), csyn(1, c, 2))
        S_.wait("g", "dve", S_.c["dve"])
        S_.em("g", "tensor_copy", AV[:, c * S + 1: c * S + 2048: 2], acc_o, inc="pool")
        pool_z[c] = S_.c["pool"]
    act_z = z_ids[-1]
    pool_zl = pool_z[-1]
    if dump == "z":
        return dict(dump_src=AV[:, 0:16384], done=("act", act_z), done2=("pool", pool_zl), cast=True)

    oev = []
    sqs_ev = []
    sqmm_ids = []
    sum_pe = []
    rows_info = [None] * 4
    S_.wait("v", "pool", pool_zl)
    for sl in range(4):
        t0 = sl * 512
        for c in range(8):
            sqslot = ASQ[:, (c % 2) * 512:(c % 2) * 512 + 512]
            S_.wait("s", "act", act_z)
            S_.wait("s", "pool", pool_zl)
            if len(sqmm_ids) >= 2:
                S_.wait("s", "pe", sqmm_ids[-2])
            S_.em("s", "activation", sqslot, AV[:, c * S + t0: c * S + t0 + 512], AF.Square, inc="act")
            sqs_ev.append(S_.c["act"])
            S_.wait("t", "act", sqs_ev[-1])
            S_.wait("t", "pool", pool_zl)
            if sl > 0 and c == 0:
                S_.wait("t", "dve", rows_info[sl - 1][0])
            S_.em("t", "matmul", PS[0][:, :], t["ones"][:, :],
                  AV[:, c * S + t0: c * S + t0 + 512], start=(c == 0), stop=(c == 7), inc="pe")
            S_.em("t", "matmul", PS[1][:, :], t["ones"][:, :], sqslot,
                  start=(c == 0), stop=(c == 7), inc="pe")
            sqmm_ids.append(S_.c["pe"])
            if c == 7:
                sum_pe.append(S_.c["pe"])
        dve_mu, act_r, mu, rr = ln_rows(P, PS[0][0:1, :], PS[1][0:1, :], sl, sum_pe[-1])
        act_bf, mu_bf, r_bf = ln_rows_bf(P, sl, mu, rr, act_r)
        rows_info[sl] = (S_.c["dve"], act_bf)
        bmu = PS[2 + (sl % 2) * 2][:, :]
        brr = PS[3 + (sl % 2) * 2][:, :]
        S_.wait("t", "act", act_bf)
        if sl >= 2:
            S_.wait("t", "dve", oev[(sl - 2) * 8 + 7])
        S_.em("t", "matmul", bmu, t["ones"][0:1, :], mu_bf, start=True, stop=True, inc="pe")
        S_.em("t", "matmul", brr, t["ones"][0:1, :], r_bf, start=True, stop=True, inc="pe")
        pe_bc = S_.c["pe"]
        for c in range(8):
            zsl = AV[:, c * S + t0: c * S + t0 + 512]
            t1 = ASQ[:, 1024 + (c % 2) * 512:1024 + (c % 2) * 512 + 512]
            t2 = ASQ[:, 2048 + (c % 2) * 512:2048 + (c % 2) * 512 + 512]
            S_.wait("v", "pe", pe_bc)
            S_.em("v", "tensor_tensor", t1, zsl, bmu, ALU.subtract, inc="dve")
            S_.em("v", "tensor_tensor", t2, t1, brr, ALU.mult, inc="dve")
            S_.em("v", "tensor_scalar", zsl, t2, t["gb"][:, c:c + 1], t["gb"][:, 8 + c:8 + c + 1],
                  ALU.mult, ALU.add, inc="dve")
            oev.append(S_.c["dve"])
    return dict(o_done_dve=oev[-1])


def build_dev_wav(dump=None):
    nc = bass.Bass("TRN2", target_bir_lowering=False)
    P = Prog(nc, dev_io=True)
    S_ = P.S
    with contextlib.ExitStack() as stack:
        _make_sems(nc, S_, stack)
        block = stack.enter_context(nc.Block())
        load_constants(P)
        S_.em("g", "dma_start",
              P.t["y"].ap().rearrange("p (c t) -> p c t", c=NCH),
              P.d["dev_y"].rearrange("(c p) t -> p c t", p=128), inc="ldc", incv=16)
        n = S_.c["ldc"]
        ret = emit_wav(P, entry=[(e, "ldc", n) for e in ("v", "g", "s", "t")], dump=dump)
        if dump is not None:
            S_.wait("g", ret["done"][0], ret["done"][1])
            if "done2" in ret:
                S_.wait("g", ret["done2"][0], ret["done2"][1])
            n16 = ret["dump_src"].ap[1][1] if hasattr(ret["dump_src"], "ap") else 16384
            S_.em("g", "dma_start", P.d["dev_dump"][:, 0:(8192 if not ret.get("cast") else 16384)],
                  ret["dump_src"], inc="st", incv=16)
        else:
            S_.wait("g", "dve", ret["o_done_dve"])
            S_.em("g", "dma_start",
                  P.d["dev_o"].rearrange("(c p) t -> p c t", p=128),
                  P.t["v"][:, 0:NCH * S].rearrange("p (c t) -> p c t", c=NCH),
                  inc="st", incv=16)
        S_.wait("g", "st", S_.c["st"])
        _finish_queues(P, block)
    return nc



def emit_ffn(P, entry=(), tern=True):
    S_, t, d, PS = P.S, P.t, P.d, P.PS
    AY, AV, ASQ, AM, AX_ = t["y"], t["v"], t["sq"], t["misc"], t["epix"]
    b1f, wf = P.b1f, P.wf
    epixf = t["epix"]
    miscb = P.miscb
    stat = t["stat"]
    for eng, sk, val in entry:
        S_.wait(eng, sk, val)
    rr = t["b1"]

    S_.em("g", "dma_start", t["w"][:, 18432:18560].rearrange("p (c r) -> p c r", c=8),
          d["L1_ffn_vT"].rearrange("(c p) r -> p c r", p=128), inc="ldc", incv=16)
    S_.em("g", "dma_start", t["w"][0:16, 18816:18832], d["L1_ffn_fast_bT"], inc="ldc", incv=16)
    S_.em("g", "dma_start", t["g1"][0:1, 0:1], d["L1_ffn_gate"], inc="ldc", incv=16)
    ldc_ffn = S_.c["ldc"]

    S_.wait("t", "ldc", ldc_ffn)
    S_.em("t", "matmul", PS[3][0:16, 0:1], t["ones"][0:1, 0:16], t["g1"][0:1, 0:1],
          start=True, stop=True, inc="pe")
    S_.wait("s", "pe", S_.c["pe"])
    S_.em("s", "activation", stat[0:16, 99:100], PS[3][0:16, 0:1], AF.Copy, inc="act")
    act_g = S_.c["act"]

    oev = []
    last_abs_act = None
    stage_act = [None, None]
    last_mask_dve = None
    last_tr_pe = None
    trev_last = [None, None]
    lhsT_fc_pe = [None, None]
    last_rel_dve = [None, None]
    last_rel_act = [None, None]
    last_o_act = [None, None]
    last_proj_pe = None
    ld_fc = {}
    ld_uT = {}
    prev_t1ev = None
    wst_j = {}

    def fc_band_dma(j, cached):
        sem = "wa" if j % 2 == 0 else "wb"
        if cached:
            S_.wait("g", sem, S_.c[sem])
            S_.em("g", "dma_start", t["w"][:, 4096 + (j % 2) * 1024:4096 + (j % 2) * 1024 + 1024],
                  d["weff_fc"][j], inc=sem, incv=16)
        else:
            S_.em("g", "dma_start", wf[:, (j % 2) * 1024:(j % 2) * 1024 + 1024],
                  d["L1_ffn_w_fc"][j * 128:(j + 1) * 128, :], inc=sem, incv=16)
        ld_fc[j % 2] = (sem, S_.c[sem])

    def uT_dma(j):
        sem = "ma" if j % 2 == 0 else "mb"
        S_.em("g", "dma_start", t["w"][0:16, 18560 + (j % 2) * 128:18560 + (j % 2) * 128 + 128],
              d["L1_ffn_uT"][:, j * 128:(j + 1) * 128], inc=sem, incv=16)
        ld_uT[j % 2] = (sem, S_.c[sem])

    wst_total = [S_.c["wst"]]
    for half in range(2):
        use_tern = tern and half == 0
        th0 = half * 1024
        t2sb = miscb[0:16, 1024:2048]
        t1ev = []
        for th in range(2):
            if prev_t1ev is not None:
                S_.wait("t", "act", prev_t1ev)
            for i in range(NCH):
                S_.em("t", "matmul", PS[2][0:16, :], t["w"][:, 18432 + i * 16:18432 + (i + 1) * 16],
                      P.ych(i, th0 + th * 512), start=(i == 0), stop=(i == 7),
                      inc="pe" if i == 7 else None)
            S_.wait("s", "pe", S_.c["pe"])
            S_.em("s", "activation", ASQ[0:16, (th % 2) * 512:(th % 2) * 512 + 512],
                  PS[2][0:16, :], AF.Copy, inc="act")
            act_t1 = S_.c["act"]
            S_.wait("t", "act", act_t1)
            S_.em("t", "matmul", PS[3][0:16, :], t["w"][0:16, 18816:18832],
                  ASQ[0:16, (th % 2) * 512:(th % 2) * 512 + 512], start=True, stop=True, inc="pe")
            S_.wait("s", "pe", S_.c["pe"])
            S_.em("s", "activation", t2sb[:, th * 512:(th + 1) * 512], PS[3][0:16, :], AF.Copy,
                  scale=stat[0:16, 99:100], inc="act")
            t1ev.append(S_.c["act"])
        act_t2 = S_.c["act"]
        prev_t1ev = act_t2
        S_.wait("t", "act", act_t2)

        if not use_tern:
            S_.wait("g", "wst", wst_total[0])
        for j in range(NE):
            if j < 2:
                if half == 1:
                    S_.wait("g", "act", prev_abs_act if prev_abs_act is not None else 0)
                    S_.wait("g", "pe", last_proj_pe if last_proj_pe is not None else 0)
                if j == 0 and not use_tern and lhsT_fc_pe[0] is not None:
                    S_.wait("g", "pe", lhsT_fc_pe[0])
                    S_.wait("g", "pe", lhsT_fc_pe[1] or 0)
                fc_band_dma(j, not use_tern)
                uT_dma(j)
            sem, cnt = ld_fc[j % 2]
            if use_tern:
                S_.wait("s", sem, cnt)
                stage = wf[:, (j % 2) * 1024:(j % 2) * 1024 + 1024]
                absb = epixf[:, 0:1024]
                sgn = ASQ[:, 1024:2048]
                msk = AM.ap().bitcast(BF16)[:, 0:1024]
                weff = epixf[:, 1024:2048]
                if last_mask_dve is not None:
                    S_.wait("s", "dve", last_mask_dve)
                S_.em("s", "activation", absb, stage, AF.Abs, inc="act")
                S_.em("s", "activation", sgn, stage, AF.Sign, inc="act")
                act_abs = S_.c["act"]
                stage_act[j % 2] = act_abs
                S_.wait("v", "act", act_abs)
                S_.em("v", "reduce_sum", stat[:, 96:97], absb, axis=AX.X, inc="dve")
                S_.em("v", "tensor_scalar", stat[:, 97:98], stat[:, 96:97], 1.0 / 1024.0, 1e-5,
                      ALU.mult, ALU.max, inc="dve")
                S_.em("v", "tensor_scalar_mul", stat[:, 98:99], stat[:, 97:98], 0.5, inc="dve")
                S_.em("v", "tensor_scalar", msk, absb, stat[:, 98:99], None, ALU.is_ge, inc="dve")
                if last_tr_pe is not None:
                    S_.wait("v", "pe", last_tr_pe)
                S_.em("v", "scalar_tensor_tensor", weff, msk, stat[:, 97:98], sgn,
                      ALU.mult, ALU.mult, inc="dve")
                last_mask_dve = S_.c["dve"]
                nj = j + 2
                if nj < NE:
                    S_.wait("g", "act", act_abs)
                    fc_band_dma(nj, False)
                for i in range(NCH):
                    psx = PS[4 + i % 2][:, 0:128]
                    S_.wait("t", "dve", last_mask_dve)
                    if trev_last[i % 2] is not None:
                        S_.wait("t", "act", trev_last[i % 2])
                    S_.em("t", "transpose", psx, weff[:, i * 128:(i + 1) * 128], t["idf"][:, :], inc="pe")
                    last_tr_pe = S_.c["pe"]
                    S_.wait("s", "pe", last_tr_pe)
                    if lhsT_fc_pe[j % 2] is not None:
                        S_.wait("s", "pe", lhsT_fc_pe[j % 2])
                    if wst_j.get((0, j % 2)) is not None:
                        S_.wait("s", "wst", S_.c["wst"])
                    S_.em("s", "activation",
                          t["w"][:, 4096 + (j % 2) * 1024 + i * 128:4096 + (j % 2) * 1024 + (i + 1) * 128],
                          psx, AF.Copy, inc="act")
                    trev_last[i % 2] = S_.c["act"]
                S_.wait("sy", "act", trev_last[0])
                S_.wait("sy", "act", trev_last[1])
                S_.em("sy", "dma_start", d["weff_fc"][j],
                      t["w"][:, 4096 + (j % 2) * 1024:4096 + (j % 2) * 1024 + 1024], inc="wst", incv=16)
                wst_j[(0, j % 2)] = S_.c["wst"]
                wst_total[0] = S_.c["wst"]
            else:
                pass
            semu, cntu = ld_uT[j % 2]
            S_.wait("t", semu, cntu)
            if use_tern:
                S_.wait("t", "act", trev_last[0])
                S_.wait("t", "act", trev_last[1])
            else:
                S_.wait("t", sem, cnt)
            for th in range(2):
                ps = PS[th % 2]
                if last_rel_dve[th] is not None:
                    S_.wait("t", "dve", last_rel_dve[th])
                if j == 0 and last_o_act[th] is not None:
                    S_.wait("t", "act", last_o_act[th])
                for i in range(NCH):
                    S_.em("t", "matmul", ps[:, :],
                          t["w"][:, 4096 + (j % 2) * 1024 + i * 128:4096 + (j % 2) * 1024 + (i + 1) * 128],
                          P.ych(i, th0 + th * 512), start=(i == 0), stop=False)
                S_.em("t", "matmul", ps[:, :],
                      t["w"][0:16, 18560 + (j % 2) * 128:18560 + (j % 2) * 128 + 128],
                      t2sb[:, th * 512:(th + 1) * 512], start=False, stop=True, inc="pe")
                pe_h = S_.c["pe"]
                S_.wait("v", "pe", pe_h)
                S_.wait("v", "act", act_t2)
                rslot = ASQ[:, (th % 2) * 512:(th % 2) * 512 + 512]
                if last_rel_act[th] is not None:
                    S_.wait("v", "act", last_rel_act[th])
                S_.em("v", "tensor_scalar_max", rslot, ps[:, :], 0.0, inc="dve")
                last_rel_dve[th] = S_.c["dve"]
                S_.wait("s", "dve", last_rel_dve[th])
                S_.em("s", "activation", rr[:, j * 1024 + th * 512:j * 1024 + (th + 1) * 512],
                      rslot, AF.Square, inc="act")
                last_rel_act[th] = S_.c["act"]
            lhsT_fc_pe[j % 2] = S_.c["pe"]
            nj = j + 2
            if nj < NE:
                S_.wait("g", "pe", lhsT_fc_pe[j % 2])
                uT_dma(nj)
                if not use_tern:
                    fc_band_dma(nj, True)
        S_.wait("t", "act", last_rel_act[0])
        S_.wait("t", "act", last_rel_act[1])
        prev_tern_dve = None
        prev_abs_act = None
        tr_last_p = [None, None]
        PJ = [14336, 6144]
        for jd in range(NCH):
            sem = "wa" if jd % 2 == 0 else "wb"
            if not use_tern:
                if jd == 0:
                    S_.wait("g", "wst", wst_total[0])
                    for th_ in range(2):
                        if last_proj_pe is not None:
                            S_.wait("g", "pe", last_proj_pe)
                    S_.wait("g", "wa", S_.c["wa"])
                    S_.wait("g", "wb", S_.c["wb"])
                    S_.em("g", "dma_start", t["w"][:, PJ[0]:PJ[0] + 4096], d["weff_proj"][0], inc="wa", incv=16)
                    ld_pj0 = ("wa", S_.c["wa"])
                    S_.em("g", "dma_start", t["w"][:, PJ[1]:PJ[1] + 4096], d["weff_proj"][1], inc="wb", incv=16)
                    ld_pj1 = ("wb", S_.c["wb"])
                    ld_pj = [ld_pj0, ld_pj1]
                S_.wait("t", ld_pj[jd % 2][0], ld_pj[jd % 2][1])
                lhsT_base = PJ[jd % 2]
                for th in range(2):
                    ps = PS[th % 2]
                    if last_o_act[th] is not None:
                        S_.wait("t", "act", last_o_act[th])
                    elif last_rel_dve[th] is not None:
                        S_.wait("t", "dve", last_rel_dve[th])
                    for o in range(NE):
                        S_.em("t", "matmul", ps[:, :], t["w"][:, lhsT_base + o * 128:lhsT_base + (o + 1) * 128],
                              rr[:, o * 1024 + th * 512:o * 1024 + (th + 1) * 512],
                              start=(o == 0), stop=(o == NE - 1), inc="pe" if o == NE - 1 else None)
                    last_proj_pe = S_.c["pe"]
                    S_.wait("s", "pe", last_proj_pe)
                    S_.em("s", "activation", AV[:, jd * S + th0 + th * 512: jd * S + th0 + (th + 1) * 512],
                          ps[:, :], AF.Copy, inc="act")
                    oev.append(S_.c["act"])
                    last_o_act[th] = S_.c["act"]
                if jd + 2 < NCH:
                    S_.wait("g", "pe", last_proj_pe)
                    bsem = "wa" if jd % 2 == 0 else "wb"
                    S_.wait("g", bsem, S_.c[bsem])
                    S_.em("g", "dma_start", t["w"][:, PJ[jd % 2]:PJ[jd % 2] + 4096],
                          d["weff_proj"][jd + 2], inc=bsem, incv=16)
                    ld_pj[jd % 2] = (bsem, S_.c[bsem])
                continue
            if prev_abs_act is not None:
                S_.wait("g", "act", prev_abs_act)
            S_.wait("g", sem, S_.c[sem])
            S_.em("g", "dma_start", wf[:, 3072:7168], d["L1_ffn_w_proj"][jd * 128:(jd + 1) * 128, :],
                  inc=sem, incv=16)
            ldp = (sem, S_.c[sem])
            stage = wf[:, 3072:7168]
            S_.wait("s", ldp[0], ldp[1])
            if prev_tern_dve is not None:
                S_.wait("s", "dve", prev_tern_dve)
            S_.em("s", "activation", epixf[:, 0:2048], stage[:, 0:2048], AF.Abs, inc="act")
            S_.wait("v", "act", S_.c["act"])
            S_.em("v", "reduce_sum", stat[:, 96:97], epixf[:, 0:2048], axis=AX.X, inc="dve")
            S_.wait("s", "dve", S_.c["dve"])
            S_.em("s", "activation", epixf[:, 0:2048], stage[:, 2048:4096], AF.Abs, inc="act")
            S_.wait("v", "act", S_.c["act"])
            S_.em("v", "reduce_sum", stat[:, 95:96], epixf[:, 0:2048], axis=AX.X, inc="dve")
            S_.em("v", "tensor_tensor", stat[:, 96:97], stat[:, 96:97], stat[:, 95:96], ALU.add, inc="dve")
            S_.em("v", "tensor_scalar", stat[:, 97:98], stat[:, 96:97], 1.0 / 4096.0, 1e-5,
                  ALU.mult, ALU.max, inc="dve")
            S_.em("v", "tensor_scalar_mul", stat[:, 98:99], stat[:, 97:98], 0.5, inc="dve")
            dve_sc = S_.c["dve"]
            for pc in range(2):
                S_.wait("s", "dve", dve_sc)
                S_.em("s", "activation", epixf[:, 0:2048], stage[:, pc * 2048:(pc + 1) * 2048], AF.Abs, inc="act")
                S_.em("s", "activation", ASQ[:, 2048:4096], stage[:, pc * 2048:(pc + 1) * 2048], AF.Sign, inc="act")
                a3 = S_.c["act"]
                prev_abs_act = a3
                S_.wait("v", "act", a3)
                S_.em("v", "tensor_scalar", ASQ[:, 0:2048], epixf[:, 0:2048], stat[:, 98:99], None,
                      ALU.is_ge, inc="dve")
                if last_tr_pe is not None:
                    S_.wait("v", "pe", last_tr_pe)
                S_.em("v", "scalar_tensor_tensor", epixf[:, 0:2048], ASQ[:, 0:2048], stat[:, 97:98],
                      ASQ[:, 2048:4096], ALU.mult, ALU.mult, inc="dve")
                dve_we = S_.c["dve"]
                prev_tern_dve = dve_we
                for o2 in range(16):
                    o = pc * 16 + o2
                    psx = PS[4 + o % 2][:, 0:128]
                    S_.wait("t", "dve", dve_we)
                    if tr_last_p[o % 2] is not None:
                        S_.wait("t", "act", tr_last_p[o % 2])
                    elif trev_last[o % 2] is not None:
                        S_.wait("t", "act", trev_last[o % 2])
                    S_.em("t", "transpose", psx, epixf[:, o2 * 128:(o2 + 1) * 128], t["idf"][:, :], inc="pe")
                    last_tr_pe = S_.c["pe"]
                    S_.wait("s", "pe", last_tr_pe)
                    if last_proj_pe is not None and o < 2 and jd > 0:
                        S_.wait("s", "pe", last_proj_pe)
                    if o < 2 and jd > 0:
                        S_.wait("s", "wst", S_.c["wst"])
                    S_.em("s", "activation", t["w"][:, 14336 + o * 128:14336 + (o + 1) * 128], psx,
                          AF.Copy, inc="act")
                    tr_last_p[o % 2] = S_.c["act"]
            S_.wait("t", "act", tr_last_p[0])
            S_.wait("t", "act", tr_last_p[1])
            S_.wait("sy", "act", tr_last_p[0])
            S_.wait("sy", "act", tr_last_p[1])
            S_.em("sy", "dma_start", d["weff_proj"][jd], t["w"][:, 14336:14336 + 4096], inc="wst", incv=16)
            wst_total[0] = S_.c["wst"]
            wst_pj = S_.c["wst"]
            for th in range(2):
                ps = PS[th % 2]
                if last_o_act[th] is not None:
                    S_.wait("t", "act", last_o_act[th])
                elif last_rel_dve[th] is not None:
                    S_.wait("t", "dve", last_rel_dve[th])
                for o in range(NE):
                    S_.em("t", "matmul", ps[:, :], t["w"][:, 14336 + o * 128:14336 + (o + 1) * 128],
                          rr[:, o * 1024 + th * 512:o * 1024 + (th + 1) * 512],
                          start=(o == 0), stop=(o == NE - 1), inc="pe" if o == NE - 1 else None)
                last_proj_pe = S_.c["pe"]
                S_.wait("s", "pe", last_proj_pe)
                S_.em("s", "activation", AV[:, jd * S + th0 + th * 512: jd * S + th0 + (th + 1) * 512],
                      ps[:, :], AF.Copy, inc="act")
                oev.append(S_.c["act"])
                last_o_act[th] = S_.c["act"]
    return dict(o_done_act=oev[-1])


def build_dev_ffn():
    nc = bass.Bass("TRN2", target_bir_lowering=False)
    P = Prog(nc, dev_io=True)
    S_ = P.S
    with contextlib.ExitStack() as stack:
        _make_sems(nc, S_, stack)
        block = stack.enter_context(nc.Block())
        load_constants(P)
        S_.em("g", "dma_start",
              P.t["y"].ap().rearrange("p (c t) -> p c t", c=NCH),
              P.d["dev_y"].rearrange("(c p) t -> p c t", p=128), inc="ldc", incv=16)
        ret = emit_ffn(P)
        S_.wait("g", "act", ret["o_done_act"])
        S_.em("g", "dma_start",
              P.d["dev_o"].rearrange("(c p) t -> p c t", p=128),
              P.t["v"][:, 0:NCH * S].rearrange("p (c t) -> p c t", c=NCH),
              inc="st", incv=16)
        S_.wait("g", "st", S_.c["st"])
        _finish_queues(P, block)
    return nc



def phase_barrier(P, engines=("t", "s", "v", "g")):
    S_ = P.S
    snap = {k: S_.c[k] for k in ("pe", "act", "dve", "pool", "st", "x")}
    own = dict(t="pe", s="act", v="dve", g="pool")
    for eng in engines:
        for key, val in snap.items():
            if key != own[eng]:
                S_.wait(eng, key, val)
    return snap


def emit_prologue(P, x_src):
    S_, t, PS = P.S, P.t, P.PS
    AY, ASQ = t["y"], t["sq"]
    S_.em("g", "dma_start", AY.ap().rearrange("p (c t) -> p c t", c=NCH),
          x_src.rearrange("(c p) t -> p c t", p=128), inc="ldc", incv=16)
    ldx = S_.c["ldc"]
    S_.wait("s", "ldc", ldx)
    S_.wait("t", "ldc", ldx)
    S_.wait("v", "ldc", ldx)
    sqev = [None, None]
    mmlast = [None, None]
    rows_dve = None
    norm_ids = []
    for sl in range(4):
        t0 = sl * 512
        for c in range(NCH):
            slot = ASQ[:, (c % 2) * 512:(c % 2) * 512 + 512]
            if mmlast[c % 2] is not None:
                S_.wait("s", "pe", mmlast[c % 2])
            S_.em("s", "activation", slot, AY[:, c * S + t0:c * S + t0 + 512], AF.Square, inc="act")
            sqev[c % 2] = S_.c["act"]
            S_.wait("t", "act", sqev[c % 2])
            if sl > 0 and c == 0:
                S_.wait("t", "dve", rows_dve)
            S_.em("t", "matmul", PS[0][:, :], t["ones"][:, :], AY[:, c * S + t0:c * S + t0 + 512],
                  start=(c == 0), stop=(c == 7), inc="pe")
            S_.em("t", "matmul", PS[1][:, :], t["ones"][:, :], slot,
                  start=(c == 0), stop=(c == 7), inc="pe")
            mmlast[c % 2] = S_.c["pe"]
        dve_mu, act_r, mu, rr_ = ln_rows(P, PS[0][0:1, :], PS[1][0:1, :], sl, mmlast[1])
        rows_dve = S_.c["dve"]
        act_bf, mu_bf, r_bf = ln_rows_bf(P, sl, mu, rr_, act_r)
        bmu = PS[2 + (sl % 2) * 2][:, :]
        brr = PS[3 + (sl % 2) * 2][:, :]
        S_.wait("t", "act", act_bf)
        if sl >= 2:
            S_.wait("t", "dve", norm_ids[-1])
        S_.em("t", "matmul", bmu, t["ones"][0:1, :], mu_bf, start=True, stop=True, inc="pe")
        S_.em("t", "matmul", brr, t["ones"][0:1, :], r_bf, start=True, stop=True, inc="pe")
        pe_bc = S_.c["pe"]
        for c in range(NCH):
            ysl = AY[:, c * S + t0:c * S + t0 + 512]
            t1 = ASQ[:, 1024 + (c % 2) * 512:1024 + (c % 2) * 512 + 512]
            S_.wait("v", "pe", pe_bc)
            S_.em("v", "tensor_tensor", t1, ysl, bmu, ALU.subtract, inc="dve")
            S_.em("v", "tensor_tensor", ysl, t1, brr, ALU.mult, inc="dve")
            norm_ids.append(S_.c["dve"])
    return dict(y_done_dve=S_.c["dve"])


def emit_epilogue(P, x_src, xn_dst, want_y, y_hbm=None):
    S_, t, PS = P.S, P.t, P.PS
    AY, AV, ASQ = t["y"], t["v"], t["sq"]
    b1b = P.b1f.bitcast(BF16)
    epixf = t["epix"]
    b1f = P.b1f
    for key in ("pe", "act", "dve", "pool"):
        S_.wait("sy", key, S_.c[key])
    sqev = [None, None]
    mmlast = [None, None]
    rows_dve = None
    bc_ev = []
    for sl in range(4):
        t0 = sl * 512
        for c in range(NCH):
            slot = ASQ[:, (c % 2) * 512:(c % 2) * 512 + 512]
            if mmlast[c % 2] is not None:
                S_.wait("s", "pe", mmlast[c % 2])
            S_.em("s", "activation", slot, AV[:, c * S + t0:c * S + t0 + 512], AF.Square, inc="act")
            sqev[c % 2] = S_.c["act"]
            S_.wait("t", "act", sqev[c % 2])
            if sl > 0 and c == 0:
                S_.wait("t", "dve", rows_dve)
            S_.em("t", "matmul", PS[0][:, :], t["ones"][:, :], AV[:, c * S + t0:c * S + t0 + 512],
                  start=(c == 0), stop=(c == 7), inc="pe")
            S_.em("t", "matmul", PS[1][:, :], t["ones"][:, :], slot,
                  start=(c == 0), stop=(c == 7), inc="pe")
            mmlast[c % 2] = S_.c["pe"]
        dve_mu, act_r, mu, rr_ = ln_rows(P, PS[0][0:1, :], PS[1][0:1, :], sl, mmlast[1])
        rows_dve = S_.c["dve"]
        act_bf, mu_bf, r_bf = ln_rows_bf(P, sl, mu, rr_, act_r)
        bmu = PS[2 + (sl % 2) * 2][:, :]
        brr = PS[3 + (sl % 2) * 2][:, :]
        S_.wait("t", "act", act_bf)
        if sl >= 2:
            S_.wait("t", "act", bc_ev[sl - 2])
        S_.em("t", "matmul", bmu, t["ones"][0:1, :], mu_bf, start=True, stop=True, inc="pe")
        S_.em("t", "matmul", brr, t["ones"][0:1, :], r_bf, start=True, stop=True, inc="pe")
        pe_bc = S_.c["pe"]
        S_.wait("s", "pe", pe_bc)
        S_.em("s", "activation", b1b[:, 16384 + sl * 512:16384 + (sl + 1) * 512], bmu, AF.Copy, inc="act")
        S_.em("s", "activation", b1b[:, 20480 + sl * 512:20480 + (sl + 1) * 512], brr, AF.Copy, inc="act")
        bc_ev.append(S_.c["act"])
    act_bcast = S_.c["act"]
    S_.wait("t", "dve", rows_dve)
    S_.wait("t", "act", act_bcast)
    xn_adds = []
    s2stop = [None] * 4
    xn_bf_acts = []
    s2mm = [None, None]
    st_ids = []
    for c in range(NCH):
        if c >= 1:
            S_.wait("sy", "dve", xn_adds[-1])
        S_.em("sy", "dma_start", epixf[:, :], x_src[c * 128:(c + 1) * 128, :], inc="x", incv=16)
        ldx = S_.c["x"]
        S_.wait("v", "x", ldx)
        S_.wait("v", "act", act_bcast)
        if st_ids:
            S_.wait("v", "st", st_ids[-1])
        if xn_bf_acts:
            S_.wait("v", "act", xn_bf_acts[-1])
        for sl in range(4):
            t0 = sl * 512
            osl = AV[:, c * S + t0:c * S + t0 + 512]
            t1 = ASQ[:, 1024 + (sl % 2) * 512:1024 + (sl % 2) * 512 + 512]
            t2 = ASQ[:, 2048 + (sl % 2) * 512:2048 + (sl % 2) * 512 + 512]
            S_.em("v", "tensor_tensor", t1, osl, b1b[:, 16384 + sl * 512:16384 + (sl + 1) * 512],
                  ALU.subtract, inc="dve")
            S_.em("v", "tensor_tensor", t2, t1, b1b[:, 20480 + sl * 512:20480 + (sl + 1) * 512],
                  ALU.mult, inc="dve")
            S_.em("v", "tensor_tensor", b1f[:, 6144 + t0:6144 + t0 + 512], epixf[:, t0:t0 + 512],
                  t2, ALU.add, inc="dve")
            xn_adds.append(S_.c["dve"])
            if want_y:
                S_.wait("s", "dve", xn_adds[-1])
                S_.em("s", "activation", AY[:, c * S + t0:c * S + t0 + 512],
                      b1f[:, 6144 + t0:6144 + t0 + 512], AF.Copy, inc="act")
                if s2mm[sl % 2] is not None:
                    S_.wait("s", "pe", s2mm[sl % 2])
                S_.em("s", "activation", ASQ[:, (sl % 2) * 512:(sl % 2) * 512 + 512],
                      AY[:, c * S + t0:c * S + t0 + 512], AF.Square, inc="act")
                xn_bf_acts.append(S_.c["act"])
                S_.wait("t", "act", xn_bf_acts[-1])
                S_.em("t", "matmul", PS[sl][:, :], t["ones"][:, :],
                      AY[:, c * S + t0:c * S + t0 + 512], start=(c == 0), stop=(c == 7), inc="pe")
                S_.em("t", "matmul", PS[4 + sl][:, :], t["ones"][:, :],
                      ASQ[:, (sl % 2) * 512:(sl % 2) * 512 + 512], start=(c == 0), stop=(c == 7), inc="pe")
                s2mm[sl % 2] = S_.c["pe"]
                if c == 7:
                    s2stop[sl] = S_.c["pe"]
        S_.wait("sy", "dve", xn_adds[-1])
        S_.em("sy", "dma_start", xn_dst[c * 128:(c + 1) * 128, :], b1f[:, 6144:8192], inc="st", incv=16)
        st_ids.append(S_.c["st"])
    st_xn = S_.c["st"]
    if not want_y:
        return dict(st_done=st_xn)
    ydone = []
    for sl in range(4):
        t0 = sl * 512
        dve_mu, act_r, mu, rr_ = ln_rows(P, PS[sl][0:1, :], PS[4 + sl][0:1, :], sl, s2stop[sl])
        act_bf, mu_bf, r_bf = ln_rows_bf(P, sl, mu, rr_, act_r)
        bmu = PS[sl][:, :]
        brr = PS[4 + sl][:, :]
        S_.wait("t", "act", act_bf)
        S_.wait("t", "dve", S_.c["dve"])
        S_.em("t", "matmul", bmu, t["ones"][0:1, :], mu_bf, start=True, stop=True, inc="pe")
        S_.em("t", "matmul", brr, t["ones"][0:1, :], r_bf, start=True, stop=True, inc="pe")
        pe_bc = S_.c["pe"]
        for c in range(NCH):
            ysl = AY[:, c * S + t0:c * S + t0 + 512]
            t1 = ASQ[:, 1024 + (c % 2) * 512:1024 + (c % 2) * 512 + 512]
            S_.wait("v", "pe", pe_bc)
            S_.em("v", "tensor_tensor", t1, ysl, bmu, ALU.subtract, inc="dve")
            S_.em("v", "tensor_tensor", ysl, t1, brr, ALU.mult, inc="dve")
            ydone.append(S_.c["dve"])
    ret = dict(st_done=st_xn, y_done_dve=ydone[-1])
    if y_hbm is not None:
        S_.wait("sy", "dve", ydone[-1])
        S_.em("sy", "dma_start", y_hbm.rearrange("(c p) t -> p c t", p=128),
              AY.ap().rearrange("p (c t) -> p c t", c=NCH), inc="st", incv=16)
        ret["st_done"] = S_.c["st"]
    return ret



BLOCK_EMIT = dict(att=None, moe=None, wav=None, ffn=None)


def build_stage(stage):
    nc = bass.Bass("TRN2", target_bir_lowering=False)
    P = Prog(nc, stage=stage)
    S_ = P.S
    d = P.d
    with contextlib.ExitStack() as stack:
        _make_sems(nc, S_, stack)
        block = stack.enter_context(nc.Block())
        load_constants(P)
        for b in range(2):
            if stage == "att":
                emit_prologue(P, d["xin"][b])
                phase_barrier(P)
                emit_att(P)
                phase_barrier(P)
                ret = emit_epilogue(P, d["xin"][b], d["s_xn"][b], True, y_hbm=d["s_yout"][b])
            else:
                gate = S_.c["dve"]
                S_.wait("g", "dve", gate)
                S_.em("g", "dma_start", P.t["y"].ap().rearrange("p (c t) -> p c t", c=NCH),
                      d["s_y"][b].rearrange("(c p) t -> p c t", p=128), inc="ldc", incv=16)
                ldy = S_.c["ldc"]
                if stage == "moe":
                    emit_moe(P)
                elif stage == "wav":
                    ret0 = emit_wav(P, entry=[(e, "ldc", ldy) for e in ("v", "g", "s", "t")])
                    phase_barrier(P)
                elif stage == "ffn":
                    emit_ffn(P)
                phase_barrier(P)
                if stage == "ffn":
                    ret = emit_epilogue(P, d["s_xprev"][b], d["xout"][b], False)
                else:
                    ret = emit_epilogue(P, d["s_xprev"][b], d["s_xn"][b], True, y_hbm=d["s_yout"][b])
            phase_barrier(P)
            S_.wait("g", "st", S_.c["st"])
            S_.wait("g", "x", S_.c["x"])
        S_.wait("sy", "st", S_.c["st"])
        _finish_queues(P, block)
    return nc


def build_full():
    nc = bass.Bass("TRN2", target_bir_lowering=False)
    P = Prog(nc)
    S_ = P.S
    d = P.d
    with contextlib.ExitStack() as stack:
        _make_sems(nc, S_, stack)
        block = stack.enter_context(nc.Block())
        load_constants(P)
        for b in range(2):
            emit_prologue(P, d["xin"][b])
            phase_barrier(P)
            ratt = emit_att(P)
            phase_barrier(P)
            emit_epilogue(P, d["xin"][b], d["xscr"][0], True)
            phase_barrier(P)
            emit_moe(P, gates=[("g", "dve", ratt["rope_dve"])])
            phase_barrier(P)
            emit_epilogue(P, d["xscr"][0], d["xscr"][1], True)
            phase_barrier(P)
            emit_wav(P)
            phase_barrier(P)
            emit_epilogue(P, d["xscr"][1], d["xscr"][0], True)
            phase_barrier(P)
            emit_ffn(P, tern=(b == 0))
            phase_barrier(P)
            emit_epilogue(P, d["xscr"][0], d["xout"][b], False)
            phase_barrier(P)
        S_.wait("sy", "st", S_.c["st"])
        S_.wait("sy", "x", S_.c["x"])
        _finish_queues(P, block)
    return nc


def kernel_fused(**inputs):
    import ml_dtypes
    consts = _host_constants()
    w = _pack_weights(inputs)
    maps = [_core_inputs(inputs, c, consts, w) for c in range(_CORES)]
    nc = build_full()
    res = run_bass_kernel_spmd(nc, maps, core_ids=list(range(_CORES)))
    out = np.empty((16, S, D), dtype=np.float32)
    for c in range(_CORES):
        xo = np.asarray(res.results[c]["xout"])
        out[2 * c:2 * c + 2] = xo.transpose(0, 2, 1)
    return out


_CORES = 8


def _core_inputs(inputs, core, consts, w):
    x = np.asarray(inputs["L0_attn_x"])
    xb = np.ascontiguousarray(x[2 * core:2 * core + 2].transpose(0, 2, 1)).astype(np.float32)
    m = {"xin": xb}
    m.update(consts)
    m.update(w)
    return m


def kernel(**inputs):
    return kernel_fused(**inputs)


def kernel_staged(**inputs):
    import ml_dtypes
    consts = _host_constants()
    w = _pack_weights(inputs)
    maps = [_core_inputs(inputs, c, consts, w) for c in range(_CORES)]
    state = [dict() for _ in range(_CORES)]
    for stage in ("att", "moe", "wav", "ffn"):
        nc = build_stage(stage)
        ins = []
        for c in range(_CORES):
            mm = dict(maps[c])
            if stage != "att":
                mm["s_y"] = state[c]["s_yout"]
                mm["s_xprev"] = state[c]["s_xn"]
            ins.append(mm)
        res = run_bass_kernel_spmd(nc, ins, core_ids=list(range(_CORES)))
        for c in range(_CORES):
            state[c] = {k: np.asarray(v) for k, v in res.results[c].items()}
    out = np.empty((16, S, D), dtype=np.float32)
    for c in range(_CORES):
        xo = state[c]["xout"]
        out[2 * c:2 * c + 2] = xo.transpose(0, 2, 1)
    return out



def _host_constants():
    c = {}
    t = np.arange(S, dtype=np.float64)
    invf = 1.0 / (10000.0 ** (np.arange(0, DH, 2, dtype=np.float64) / DH))
    ang = np.outer(np.concatenate([invf, invf]), t)
    cos = np.cos(ang).astype(np.float32)
    sin = np.sin(ang).astype(np.float32)
    c["c_cos"] = np.concatenate([cos, cos], 0)
    c["c_sin"] = np.concatenate([sin, sin], 0)
    p2 = np.zeros((128, 128), dtype=np.float32)
    for half in range(2):
        o = half * 64
        for p in range(32):
            p2[o + p + 32, o + p] = -1.0
            p2[o + p, o + p + 32] = 1.0
    c["c_p2"] = p2
    c["c_id"] = np.eye(128, dtype=np.float32)
    c["c_ones"] = np.ones((128, 128), dtype=np.float32)
    oh = np.zeros((32, NE * 128), dtype=np.float32)
    for e in range(NE):
        oh[e, e * 128:(e + 1) * 128] = 1.0
    c["c_oh"] = oh
    return c


def _pack_weights(inputs):
    w = {}
    w["L0_attn_Wqkv"] = np.ascontiguousarray(inputs["L0_attn_Wqkv"])
    w["L0_attn_Wo"] = np.ascontiguousarray(inputs["L0_attn_Wo"])
    w["L0_ffn_Wr"] = np.ascontiguousarray(inputs["L0_ffn_Wr"])
    w["L0_ffn_br"] = np.ascontiguousarray(inputs["L0_ffn_br"].reshape(NE, 1))
    for k in ("Wg_s", "Wu_s", "Wd_s"):
        w[f"L0_ffn_{k}"] = np.ascontiguousarray(inputs[f"L0_ffn_{k}"])
    for k in ("bg_s", "bu_s"):
        w[f"L0_ffn_{k}"] = np.ascontiguousarray(inputs[f"L0_ffn_{k}"].reshape(4, 128).T)
    w["L0_ffn_bd_s"] = np.ascontiguousarray(inputs["L0_ffn_bd_s"].reshape(8, 128).T)
    for k in ("Wg", "Wu", "Wd"):
        w[f"L0_ffn_{k}"] = np.ascontiguousarray(inputs[f"L0_ffn_{k}"])
    for k in ("bg", "bu"):
        b = inputs[f"L0_ffn_{k}"].reshape(NE, 4, 128)
        w[f"L0_ffn_{k}"] = np.ascontiguousarray(b.transpose(2, 0, 1).reshape(128, NE * 4))
    w["L0_ffn_bd"] = np.ascontiguousarray(inputs["L0_ffn_bd"])
    filt = np.zeros((128, 128), dtype=np.float32)
    for fi, key in enumerate(["L1_attn_h_filter", "L1_attn_g_filter", "L1_attn_h_synth", "L1_attn_g_synth"]):
        f = inputs[key]
        for c in range(8):
            for k in range(4):
                filt[:, fi * 32 + c * 4 + k] = f[k, c * 128:(c + 1) * 128]
    w["L1_attn_filt"] = filt
    w["L1_attn_gamma"] = np.ascontiguousarray(inputs["L1_attn_norm_gamma"].reshape(8, 128).T)
    w["L1_attn_beta"] = np.ascontiguousarray(inputs["L1_attn_norm_beta"].reshape(8, 128).T)
    w["L1_ffn_fast_bT"] = np.ascontiguousarray(inputs["L1_ffn_fast_b"].T)
    w["L1_ffn_w_fc"] = np.ascontiguousarray(inputs["L1_ffn_w_fc"])
    w["L1_ffn_w_proj"] = np.ascontiguousarray(inputs["L1_ffn_w_proj"])
    w["L1_ffn_uT"] = np.ascontiguousarray(inputs["L1_ffn_u"].T)
    w["L1_ffn_vT"] = np.ascontiguousarray(inputs["L1_ffn_v"].T)
    w["L1_ffn_gate"] = np.asarray(inputs["L1_ffn_gate"], dtype=np.float32).reshape(1, 1)
    return w


def _finish_queues(P, block):
    S_ = P.S
    import contextlib
    lp = P.nc.allow_low_precision(reason="bf16 operands with f32 psum accumulation by design")
    lp.__enter__()
    if S_.q["g"]:
        @block.gpsimd
        def _(g):
            for fn in S_.q["g"]:
                fn(g)
    if S_.q["t"]:
        @block.tensor
        def _(t_):
            for fn in S_.q["t"]:
                fn(t_)
    if S_.q["s"]:
        @block.scalar
        def _(s_):
            for fn in S_.q["s"]:
                fn(s_)
    if S_.q["v"]:
        @block.vector
        def _(v_):
            for fn in S_.q["v"]:
                fn(v_)
    if S_.q["sy"]:
        @block.sync
        def _(sy_):
            for fn in S_.q["sy"]:
                fn(sy_)
    lp.__exit__(None, None, None)


import contextlib


def _make_sems(nc, S_, stack):
    keys = ("ldc", "wa", "wb", "ma", "mb", "x", "st", "wst", "pe", "act", "dve", "pool")
    S_.sem = {k: stack.enter_context(nc.semaphore(f"s_{k}")) for k in keys}


def build_dev_att():
    nc = bass.Bass("TRN2", target_bir_lowering=False)
    P = Prog(nc, dev_io=True)
    S_ = P.S
    with contextlib.ExitStack() as stack:
        _make_sems(nc, S_, stack)
        block = stack.enter_context(nc.Block())
        load_constants(P)
        S_.em("g", "dma_start",
              P.t["y"].ap().rearrange("p (c t) -> p c t", c=NCH),
              P.d["dev_y"].rearrange("(c p) t -> p c t", p=128), inc="ldc", incv=16)
        ldy = S_.c["ldc"]
        ret = emit_att(P)
        S_.wait("g", "act", ret["o_done_act"])
        S_.em("g", "dma_start",
              P.d["dev_o"].rearrange("(c p) t -> p c t", p=128),
              P.t["v"][:, 0:NCH * S].rearrange("p (c t) -> p c t", c=NCH),
              inc="st", incv=16)
        S_.wait("g", "st", S_.c["st"])
        _finish_queues(P, block)
    return nc





# version 13
# speedup vs baseline: 1.5278x; 1.0484x over previous
import math
import numpy as np
import concourse.bass as bass
import concourse.mybir as mybir
from concourse.bass_utils import run_bass_kernel_spmd

F32 = mybir.dt.float32
BF16 = mybir.dt.bfloat16
AF = mybir.ActivationFunctionType
ALU = mybir.AluOpType
AX = mybir.AxisListType

S = 2048
D = 1024
NCH = 8
NTH = 4
H = 16
DH = 64
NE = 32
NI = 512
FFN_H = 4096
RANK = 16
LN_EPS = 1e-5

ENGS = ("g", "t", "s", "v", "sy")


class Sched:
    def __init__(self, nc):
        self.nc = nc
        self.q = {k: [] for k in ENGS}
        self.c = dict(ldc=0, wa=0, wb=0, ma=0, mb=0, x=0, st=0, wst=0, pe=0, act=0, dve=0, pool=0)
        self.sem = {}
        self._lastself = {}

    OWNSEM = dict(t="pe", s="act", v="dve", g="pool")

    def em(self, eng, method, *args, inc=None, incv=1, **kw):
        own = self.OWNSEM.get(eng)
        if own is not None and eng != "t":
            last = self._lastself.get(eng, -1)
            if self.c[own] > last:
                self.wait(eng, own, self.c[own])
                self._lastself[eng] = self.c[own]
        sem = self.sem[inc] if inc else None
        if inc:
            self.c[inc] += incv

        def fn(e, method=method, args=args, kw=kw, sem=sem, incv=incv):
            ins = getattr(e, method)(*args, **kw)
            if sem is not None:
                ins.then_inc(sem, incv)
        self.q[eng].append(fn)
        return self.c[inc] if inc else None

    OWN = dict(t="pe", s="act", v="dve", g="pool")

    def selfwait(self, eng):
        self.wait(eng, self.OWN[eng], self.c[self.OWN[eng]])

    def wait(self, eng, semkey, val):
        if val <= 0:
            return
        sem = self.sem[semkey]
        self.q[eng].append(lambda e, sem=sem, val=val: e.wait_ge(sem, val))


class Prog:
    def __init__(self, nc, dev_io=False, stage=None):
        self.nc = nc
        self.S = Sched(nc)
        self.dev_io = dev_io
        self.stage = stage
        self._declare_dram()
        self._alloc_sbuf()
        self.PS = [nc.alloc_psum_tensor(f"PS{i}", [128, 512], F32) for i in range(8)]

    def _declare_dram(self):
        nc = self.nc
        dd = {}

        def din(name, shape, dt=F32):
            dd[name] = nc.dram_tensor(name, shape, dt, kind="ExternalInput").ap()
        din("xin", [2, D, S])
        din("L0_attn_Wqkv", [D, 3 * D])
        din("L0_attn_Wo", [D, D])
        din("L0_ffn_Wr", [D, NE])
        din("L0_ffn_br", [NE, 1])
        din("L0_ffn_Wg_s", [D, NI])
        din("L0_ffn_bg_s", [128, 4])
        din("L0_ffn_Wu_s", [D, NI])
        din("L0_ffn_bu_s", [128, 4])
        din("L0_ffn_Wd_s", [NI, D])
        din("L0_ffn_bd_s", [128, 8])
        din("L0_ffn_Wg", [NE, D, NI])
        din("L0_ffn_bg", [128, NE * 4])
        din("L0_ffn_Wu", [NE, D, NI])
        din("L0_ffn_bu", [128, NE * 4])
        din("L0_ffn_Wd", [NE, NI, D])
        din("L0_ffn_bd", [NE, D])
        din("L1_attn_filt", [128, 128])
        din("L1_attn_gamma", [128, 8])
        din("L1_attn_beta", [128, 8])
        din("L1_ffn_fast_bT", [RANK, RANK])
        din("L1_ffn_w_fc", [FFN_H, D])
        din("L1_ffn_w_proj", [D, FFN_H])
        din("L1_ffn_uT", [RANK, FFN_H])
        din("L1_ffn_vT", [D, RANK])
        din("L1_ffn_gate", [1, 1])
        din("c_cos", [128, S])
        din("c_sin", [128, S])
        din("c_oh", [32, NE * 128])
        din("c_p2", [128, 128])
        din("c_id", [128, 128])
        din("c_ones", [128, 128])
        dd["xout"] = nc.dram_tensor("xout", [2, D, S], F32, kind="ExternalOutput").ap()
        dd["xscr"] = nc.dram_tensor("xscr", [2, D, S], F32, kind="Internal").ap()
        dd["weff_fc"] = nc.dram_tensor("weff_fc", [NE, 128, 1024], BF16, kind="Internal").ap()
        dd["weff_proj"] = nc.dram_tensor("weff_proj", [NCH, 128, 4096], BF16, kind="Internal").ap()
        if self.stage:
            if self.stage != "att":
                dd["s_y"] = nc.dram_tensor("s_y", [2, D, S], BF16, kind="ExternalInput").ap()
                dd["s_xprev"] = nc.dram_tensor("s_xprev", [2, D, S], F32, kind="ExternalInput").ap()
            if self.stage != "ffn":
                dd["s_yout"] = nc.dram_tensor("s_yout", [2, D, S], BF16, kind="ExternalOutput").ap()
                dd["s_xn"] = nc.dram_tensor("s_xn", [2, D, S], F32, kind="ExternalOutput").ap()
        if self.dev_io:
            dd["dev_y"] = nc.dram_tensor("dev_y", [D, S], F32, kind="ExternalInput").ap()
            dd["dev_o"] = nc.dram_tensor("dev_o", [D, S], F32, kind="ExternalOutput").ap()
            dd["dev_dump"] = nc.dram_tensor("dev_dump", [128, 16384], F32, kind="ExternalOutput").ap()
        self.d = dd

    def _alloc_sbuf(self):
        nc = self.nc
        t = {}
        t["y"] = nc.alloc_sbuf_tensor("A_y", [128, NCH * S], BF16)
        t["b1"] = nc.alloc_sbuf_tensor("A_b1", [128, 2 * NCH * S], BF16)
        t["v"] = nc.alloc_sbuf_tensor("A_v", [128, 16 * 1040], BF16)
        t["w"] = nc.alloc_sbuf_tensor("A_w", [128, 20480], BF16)
        t["epix"] = nc.alloc_sbuf_tensor("A_epix", [128, S], F32)
        t["sq"] = nc.alloc_sbuf_tensor("A_sq", [128, 4096], BF16)
        t["misc"] = nc.alloc_sbuf_tensor("A_misc", [128, 1024], F32)
        t["cst"] = nc.alloc_sbuf_tensor("A_cst", [128, 4096], BF16)
        t["p2"] = nc.alloc_sbuf_tensor("c_p2s", [128, 128], BF16)
        t["id"] = nc.alloc_sbuf_tensor("c_ids", [128, 128], BF16)
        t["ones"] = nc.alloc_sbuf_tensor("c_oness", [128, 128], BF16)
        t["mb"] = nc.alloc_sbuf_tensor("c_mb", [128, 16], F32)
        t["stat"] = nc.alloc_sbuf_tensor("c_stat", [128, 128], F32)
        t["eb"] = nc.alloc_sbuf_tensor("c_eb", [128, NE * 8 + 16], F32)
        t["bdT"] = nc.alloc_sbuf_tensor("c_bdT", [32, 1024], BF16)
        t["wr"] = nc.alloc_sbuf_tensor("c_wr", [128, NCH * NE], BF16)
        t["br"] = nc.alloc_sbuf_tensor("c_br", [32, 1], F32)
        t["filt"] = nc.alloc_sbuf_tensor("c_filt", [128, 176], F32)
        t["gb"] = nc.alloc_sbuf_tensor("c_gb", [128, 16], F32)
        t["idf"] = nc.alloc_sbuf_tensor("c_idf", [128, 128], F32)
        t["g1"] = nc.alloc_sbuf_tensor("c_g1", [16, 16], BF16)
        self.t = t
        self.b1f = t["b1"].ap().bitcast(F32)
        self.vf = t["v"].ap().bitcast(F32)
        self.wf = t["w"].ap().bitcast(F32)
        self.sqf = t["sq"].ap().bitcast(F32)
        self.miscb = t["misc"].ap().bitcast(BF16)

    def ych(self, c, t0=0, n=512):
        return self.t["y"][:, c * S + t0: c * S + t0 + n]



def load_constants(P):
    S_, d, t = P.S, P.d, P.t
    for dst, src in [
        (t["p2"][:, :], d["c_p2"]),
        (t["id"][:, :], d["c_id"]),
        (t["ones"][:, :], d["c_ones"]),
        (t["eb"][:, 0:NE * 4], d["L0_ffn_bg"]),
        (t["eb"][:, NE * 4:NE * 8], d["L0_ffn_bu"]),
        (t["eb"][:, NE * 8:NE * 8 + 4], d["L0_ffn_bg_s"]),
        (t["eb"][:, NE * 8 + 4:NE * 8 + 8], d["L0_ffn_bu_s"]),
        (t["eb"][:, NE * 8 + 8:NE * 8 + 16], d["L0_ffn_bd_s"]),
        (t["bdT"][:, :], d["L0_ffn_bd"]),
        (t["wr"].ap().rearrange("p (c e) -> p c e", c=NCH), d["L0_ffn_Wr"].rearrange("(c p) e -> p c e", p=128)),
        (t["br"][:, :], d["L0_ffn_br"]),
        (t["filt"][:, 0:128], d["L1_attn_filt"]),
        (t["gb"][:, 0:8], d["L1_attn_gamma"]),
        (t["gb"][:, 8:16], d["L1_attn_beta"]),
        (t["idf"][:, :], d["c_id"]),
    ]:
        S_.em("g", "dma_start", dst, src, inc="ldc", incv=16)
    return S_.c["ldc"]


def load_cossin(P):
    S_, d, t = P.S, P.d, P.t
    S_.em("g", "dma_start", t["cst"][:, 0:S], d["c_cos"], inc="ldc", incv=16)
    S_.em("g", "dma_start", t["cst"][:, S:2 * S], d["c_sin"], inc="ldc", incv=16)
    return S_.c["ldc"]



def emit_att(P):
    S_, t, d, PS = P.S, P.t, P.d, P.PS
    AY, AB1, AV, AW, ASQ = t["y"], t["b1"], t["v"], t["w"], t["sq"]
    miscb, stat = P.miscb, t["stat"]
    WQK = [16384, 17408]
    WO = [18432, 19456]
    WVOFF = 8192

    cs_ld = load_cossin(P)

    for i in range(NCH):
        S_.em("g", "dma_start", AW[:, WVOFF + i * 1024: WVOFF + (i + 1) * 1024],
              d["L0_attn_Wqkv"][i * 128:(i + 1) * 128, 2 * D:3 * D], inc="ldc", incv=16)
    ldw_wv = S_.c["ldc"]

    def qkv_band_dma(jc):
        b = WQK[jc % 2]
        sem = "wa" if jc % 2 == 0 else "wb"
        S_.wait("g", sem, S_.c[sem])
        S_.em("g", "dma_start",
              AW[:, b:b + 1024].rearrange("p (ci j) -> p ci j", ci=8),
              d["L0_attn_Wqkv"][:, jc * 128:(jc + 1) * 128].rearrange("(ci p) j -> p ci j", p=128),
              inc=sem, incv=16)
        return (sem, S_.c[sem])

    ldw_band = [qkv_band_dma(0), qkv_band_dma(1)] + [None] * 14
    mmstop, rotid, rawev, dve3 = [], [], [], []

    def emit_rope_tail(gi):
        jc0, th0_ = gi // 4, gi % 4
        ps = PS[gi % 2]
        psr = PS[2 + gi % 2]
        sqslot = ASQ[:, (gi % 4) * 512:(gi % 4) * 512 + 512]
        S_.wait("s", "pe", mmstop[gi])
        if gi >= 4:
            S_.wait("s", "dve", dve3[gi - 4] - 2)
            S_.wait("s", "pe", rotid[gi - 4])
        S_.em("s", "activation", sqslot, ps[:, :], AF.Copy, inc="act")
        rawev.append(S_.c["act"])
        S_.wait("t", "act", rawev[-1])
        if gi >= 2:
            S_.wait("t", "dve", dve3[gi - 2] - 1)
        S_.em("t", "matmul", psr[:, :], t["p2"][:, :], sqslot, start=True, stop=True, inc="pe")
        rotid.append(S_.c["pe"])
        dst = AB1[:, jc0 * S + th0_ * 512: jc0 * S + (th0_ + 1) * 512]
        coss = t["cst"][:, th0_ * 512:(th0_ + 1) * 512]
        sins = t["cst"][:, S + th0_ * 512: S + (th0_ + 1) * 512]
        m1 = ASQ[:, 2048 + (gi % 4) * 512:2048 + (gi % 4) * 512 + 512]
        m2 = miscb[:, (gi % 4) * 512:(gi % 4) * 512 + 512]
        S_.wait("v", "act", rawev[-1])
        S_.em("v", "tensor_tensor", m1, sqslot, coss, ALU.mult, inc="dve")
        S_.wait("v", "pe", rotid[-1])
        S_.em("v", "tensor_tensor", m2, psr[:, :], sins, ALU.mult, inc="dve")
        S_.wait("v", "dve", S_.c["dve"])
        S_.em("v", "tensor_tensor", dst, m1, m2, ALU.add, inc="dve")
        dve3.append(S_.c["dve"])

    for jc in range(16):
        b = WQK[jc % 2]
        S_.wait("t", ldw_band[jc][0], ldw_band[jc][1])
        if jc == 0:
            S_.wait("t", "ldc", ldw_wv)
        for th in range(NTH):
            gi = jc * 4 + th
            ps = PS[gi % 2]
            if gi >= 2:
                S_.wait("t", "act", rawev[gi - 2])
            for i in range(NCH):
                S_.em("t", "matmul", ps[:, :], AW[:, b + i * 128: b + (i + 1) * 128],
                      P.ych(i, th * 512), start=(i == 0), stop=(i == 7),
                      inc="pe" if i == 7 else None)
            mmstop.append(S_.c["pe"])
            if gi >= 1:
                emit_rope_tail(gi - 1)
        if jc + 2 < 16:
            S_.wait("g", "pe", mmstop[-1])
            ldw_band[jc + 2] = qkv_band_dma(jc + 2)
    emit_rope_tail(63)
    dve_qk_end = S_.c["dve"]

    vview = AV.ap().rearrange("p (tc hd c) -> p tc hd c", tc=16, hd=16)
    S_.em("v", "memset", vview[:, :, :, 64:65], 1.0, inc="dve")
    dve_vones = S_.c["dve"]
    S_.wait("t", "ldc", ldw_wv)
    S_.wait("t", "dve", dve_vones)
    S_.wait("s", "dve", dve_qk_end)
    sq2 = ASQ[:, 2048:4096]
    vstop, vev = [], []
    mh_sq_act = None
    mh_mm_ids = []
    mh_red = []
    nv = 0
    for g in range(32):
        tc, half = g // 2, g % 2
        ps = PS[4 + (nv % 2)]
        if nv >= 2:
            S_.wait("t", "act", vev[nv - 2])
        for i in range(NCH):
            S_.em("t", "matmul", ps[:, :], AY[:, i * S + tc * 128:i * S + (tc + 1) * 128],
                  AW[:, WVOFF + i * 1024 + half * 512: WVOFF + i * 1024 + (half + 1) * 512],
                  start=(i == 0), stop=(i == 7), inc="pe" if i == 7 else None)
        vstop.append(S_.c["pe"])
        S_.wait("s", "pe", vstop[-1])
        for hh in range(8):
            hd = half * 8 + hh
            S_.em("s", "activation", AV[:, tc * 1040 + hd * 65: tc * 1040 + hd * 65 + 64],
                  ps[:, hh * 64:(hh + 1) * 64], AF.Copy, inc="act" if hh == 7 else None)
        vev.append(S_.c["act"])
        nv += 1
        srcq = g // 16
        cq = (g % 16) // 2
        hh = g % 2
        jc = srcq * 8 + cq
        col = srcq * 16 + 2 * cq + hh
        base = hh * 64
        if hh == 0:
            if mh_mm_ids:
                S_.wait("s", "pe", mh_mm_ids[-1])
            S_.em("s", "activation", sq2, AB1[:, jc * S:(jc + 1) * S], AF.Square, inc="act")
            mh_sq_act = S_.c["act"]
        S_.wait("t", "act", mh_sq_act)
        for q4 in range(4):
            bank = PS[6 + q4 % 2]
            if len(mh_red) >= 2:
                S_.wait("t", "dve", mh_red[-2])
            S_.em("t", "matmul", bank[0:64, :], t["ones"][base:base + 64, 0:64],
                  sq2[base:base + 64, q4 * 512:(q4 + 1) * 512], start=True, stop=True, inc="pe")
            mh_mm_ids.append(S_.c["pe"])
            S_.wait("v", "pe", mh_mm_ids[-1])
            if q4 == 0:
                S_.em("v", "reduce_max", stat[0:1, col:col + 1], bank[0:1, :], axis=AX.X, inc="dve")
            else:
                S_.em("v", "reduce_max", stat[0:1, 32:33], bank[0:1, :], axis=AX.X, inc="dve")
                S_.wait("v", "dve", S_.c["dve"])
                S_.em("v", "tensor_tensor", stat[0:1, col:col + 1], stat[0:1, col:col + 1],
                      stat[0:1, 32:33], ALU.max, inc="dve")
            mh_red.append(S_.c["dve"])
    S_.wait("s", "dve", mh_red[-1])
    S_.em("s", "activation", stat[0:1, 33:49], stat[0:1, 0:16], AF.Sqrt, inc="act")
    S_.em("s", "activation", stat[0:1, 49:65], stat[0:1, 16:32], AF.Sqrt, inc="act")
    act_sq = S_.c["act"]
    S_.wait("v", "act", act_sq)
    S_.em("v", "tensor_tensor", stat[0:1, 65:81], stat[0:1, 33:49], stat[0:1, 49:65], ALU.mult, inc="dve")
    S_.wait("v", "dve", S_.c["dve"])
    S_.em("v", "tensor_scalar_mul", stat[0:1, 81:97], stat[0:1, 65:81], -1.0 / 64.0, inc="dve")
    dve_mh = S_.c["dve"]
    S_.wait("s", "dve", dve_mh)
    S_.em("s", "activation", miscb[0:1, 0:97], stat[0:1, 0:97], AF.Copy, inc="act")
    act_mhc = S_.c["act"]
    S_.wait("t", "act", act_mhc)
    S_.wait("t", "dve", mh_red[-1])
    for h in range(H):
        S_.em("t", "matmul", PS[6][:, h:h + 1], t["ones"][0:1, 0:128],
              miscb[0:1, 81 + h:82 + h], start=True, stop=True,
              inc="pe" if h == H - 1 else None)
    pe_mb = S_.c["pe"]
    S_.wait("s", "pe", pe_mb)
    S_.em("s", "activation", t["mb"][:, :], PS[6][:, 0:16], AF.Copy, inc="act")
    act_mb = S_.c["act"]

    scid = [[None] * 16 for _ in range(64)]
    avid = [[None] * 16 for _ in range(64)]
    expid = [[None] * 16 for _ in range(64)]
    denid = [None] * 64
    bcid = [None] * 64
    rbev = [None] * 64
    recid = [None] * 64
    mulid = [None] * 64

    def emit_bcast(n):
        h, qb = divmod(n, 4)
        pbase = (h % 2) * 64
        db = 64 if h % 2 == 0 else 0
        S_.wait("t", recid[n][0], recid[n][1])
        if n >= 2:
            S_.wait("t", "act", rbev[n - 2])
        elif n == 0:
            S_.wait("t", "act", act_mb)
        S_.em("t", "matmul", PS[6 + n % 2][pbase:pbase + 64, :], t["ones"][db:db + 1, 0:64],
              ASQ[db:db + 1, 2048 + (n % 2) * 512:2048 + (n % 2) * 512 + 512],
              start=True, stop=True, inc="pe")
        bcid[n] = S_.c["pe"]
        S_.wait("s", "pe", bcid[n])
        if n >= 2:
            S_.wait("s", "dve", mulid[n - 2])
        S_.em("s", "activation", ASQ[pbase:pbase + 64, 2048 + (n % 2) * 512:2048 + (n % 2) * 512 + 512],
              PS[6 + n % 2][pbase:pbase + 64, :], AF.Copy, inc="act")
        rbev[n] = S_.c["act"]

    def emit_mul(n):
        h, qb = divmod(n, 4)
        pbase = (h % 2) * 64
        S_.wait("v", "act", rbev[n])
        dst = AY[pbase:pbase + 64, (h // 2) * S + qb * 512:(h // 2) * S + (qb + 1) * 512]
        S_.em("v", "tensor_tensor", dst, PS[4 + n % 2][pbase:pbase + 64, :],
              ASQ[pbase:pbase + 64, 2048 + (n % 2) * 512:2048 + (n % 2) * 512 + 512],
              ALU.mult, inc="dve")
        mulid[n] = S_.c["dve"]

    for n in range(64):
        h, qb = divmod(n, 4)
        hb = (h % 2) * 64
        even = (h % 2 == 0)
        qs = AB1[hb:hb + 64, (h // 2) * S + qb * 512:(h // 2) * S + (qb + 1) * 512]

        def emit_av(kt):
            S_.wait("t", "act", expid[n][kt])
            if kt == 0:
                if n >= 2:
                    S_.wait("t", "dve", mulid[n - 2])
                else:
                    S_.wait("t", "act", vev[30 + n])
            pslot = miscb[:, (kt % 4) * 512:(kt % 4) * 512 + 512]
            if even:
                S_.em("t", "matmul", PS[4 + n % 2][0:65, :],
                      AV[:, kt * 1040 + h * 65: kt * 1040 + h * 65 + 65], pslot,
                      start=(kt == 0), stop=(kt == 15), inc="pe")
            else:
                S_.em("t", "matmul", PS[4 + n % 2][64:128, :],
                      AV[:, kt * 1040 + h * 65: kt * 1040 + h * 65 + 64], pslot,
                      start=(kt == 0), stop=(kt == 15), inc="pe")
            avid[n][kt] = S_.c["pe"]
            if not even:
                if kt == 0 and n >= 2:
                    S_.wait("t", "act", rbev[n - 2])
                S_.em("t", "matmul", PS[6 + n % 2][0:64, :], t["ones"][:, 0:64], pslot,
                      start=(kt == 0), stop=(kt == 15), inc="pe")
                denid[n] = S_.c["pe"]
            if kt == 7 and n >= 1:
                emit_bcast(n - 1)
                emit_mul(n - 1)

        for pos in range(18):
            if pos < 16:
                kt = pos
                ps = PS[kt % 4]
                if kt >= 4:
                    S_.wait("t", "act", expid[n][kt - 4])
                elif n >= 1:
                    S_.wait("t", "act", expid[n - 1][12 + kt])
                else:
                    if kt < 2:
                        S_.wait("t", "act", rawev[60 + kt])
                    else:
                        S_.wait("t", "dve", dve3[60 + kt] - 1)
                ks = AB1[hb:hb + 64, (8 + h // 2) * S + kt * 128:(8 + h // 2) * S + (kt + 1) * 128]
                S_.em("t", "matmul", ps[:, :], ks, qs, start=True, stop=True, inc="pe")
                scid[n][kt] = S_.c["pe"]
                S_.wait("s", "pe", scid[n][kt])
                if kt >= 4:
                    S_.wait("s", "pe", avid[n][kt - 4])
                elif n >= 1:
                    S_.wait("s", "pe", avid[n - 1][12 + kt])
                else:
                    S_.wait("s", "dve", dve3[60 + kt])
                S_.em("s", "activation", miscb[:, (kt % 4) * 512:(kt % 4) * 512 + 512],
                      ps[:, :], AF.Exp, bias=t["mb"][:, h:h + 1], scale=0.125, inc="act")
                expid[n][kt] = S_.c["act"]
            if pos >= 2:
                emit_av(pos - 2)
        last_pe = denid[n] if not even else avid[n][15]
        db = 64 if even else 0
        S_.wait("s", "pe", last_pe)
        denrow_ps = PS[4 + n % 2][64:65, :] if even else PS[6 + n % 2][0:1, :]
        if n >= 2:
            S_.wait("s", "act", rbev[n - 2])
        S_.em("s", "activation", t["epix"][db:db + 1, (n % 2) * 512:(n % 2) * 512 + 512],
              denrow_ps, AF.Ln, inc="act")
        S_.em("s", "activation", ASQ[db:db + 1, 2048 + (n % 2) * 512:2048 + (n % 2) * 512 + 512],
              t["epix"][db:db + 1, (n % 2) * 512:(n % 2) * 512 + 512], AF.Exp, scale=-1.0, inc="act")
        recid[n] = ("act", S_.c["act"])
    emit_bcast(63)
    emit_mul(63)

    def wo_band_dma(j):
        b = WO[j % 2]
        sem = "wa" if j % 2 == 0 else "wb"
        S_.wait("g", sem, S_.c[sem])
        S_.em("g", "dma_start",
              AW[:, b:b + 1024].rearrange("p (ci j) -> p ci j", ci=8),
              d["L0_attn_Wo"][:, j * 128:(j + 1) * 128].rearrange("(ci p) j -> p ci j", p=128),
              inc=sem, incv=16)
        return (sem, S_.c[sem])

    S_.wait("g", "pe", mmstop[-1])
    wo_ld = [wo_band_dma(0), wo_band_dma(1)] + [None] * 6
    S_.wait("t", "dve", mulid[63])
    oev = []
    wo_stop = []
    ng = 0
    for j in range(NCH):
        b = WO[j % 2]
        S_.wait("t", wo_ld[j][0], wo_ld[j][1])
        for th in range(NTH):
            ps = PS[ng % 2]
            if ng >= 2:
                S_.wait("t", "act", oev[ng - 2])
            elif ng < 2:
                S_.wait("t", "act", expid[63][12 + ng])
            for i in range(NCH):
                S_.em("t", "matmul", ps[:, :], AW[:, b + i * 128: b + (i + 1) * 128],
                      AY[:, i * S + th * 512:i * S + (th + 1) * 512],
                      start=(i == 0), stop=(i == 7), inc="pe" if i == 7 else None)
            wo_stop.append(S_.c["pe"])
            S_.wait("s", "pe", wo_stop[-1])
            S_.em("s", "activation", AV[:, j * S + th * 512: j * S + (th + 1) * 512],
                  ps[:, :], AF.Copy, inc="act")
            oev.append(S_.c["act"])
            ng += 1
        if j + 2 < NCH:
            S_.wait("g", "pe", wo_stop[-1])
            wo_ld[j + 2] = wo_band_dma(j + 2)
    return dict(o_done_act=oev[-1], o_last_dve=mulid[63], rope_dve=dve_qk_end)



def emit_moe(P, gates=()):
    S_, t, d, PS = P.S, P.t, P.d, P.PS
    AY, AV, AW, ASQ, AM = t["y"], t["v"], t["w"], t["sq"], t["misc"]
    miscb, stat, eb, cst = P.miscb, t["stat"], t["eb"], t["cst"]
    b1f, sqf = P.b1f, P.sqf
    racc = b1f
    WG = [0, 4096]
    WU = [8192, 12288]
    WDB = [16384, 16896, 17408, 17920]

    for eng, sk, val in gates:
        S_.wait(eng, sk, val)
    S_.em("g", "dma_start", cst[0:32, :], d["c_oh"], inc="ldc", incv=16)
    idf = t["idf"]
    ldc_moe = S_.c["ldc"]

    def wguw_dma(which, e):
        if which == -1:
            slot, sem = 0, "ma"
            wg_src = d["L0_ffn_Wg_s"]
            wu_src = d["L0_ffn_Wu_s"]
        else:
            slot = (which + 1) % 2
            sem = "mb" if which % 2 == 0 else "ma"
            wg_src = d["L0_ffn_Wg"][which]
            wu_src = d["L0_ffn_Wu"][which]
        S_.em("g", "dma_start", AW[:, WG[slot]:WG[slot] + 4096].rearrange("p (i c) -> p i c", i=8),
              wg_src.rearrange("(i p) c -> p i c", p=128), inc=sem, incv=16)
        S_.em("g", "dma_start", AW[:, WU[slot]:WU[slot] + 4096].rearrange("p (i c) -> p i c", i=8),
              wu_src.rearrange("(i p) c -> p i c", p=128), inc=sem, incv=16)
        return (sem, S_.c[sem], slot)

    def wd_band_dma(which, j, slotk):
        sem = "wa" if j % 2 == 0 else "wb"
        src = d["L0_ffn_Wd_s"] if which == -1 else d["L0_ffn_Wd"][which]
        S_.em("g", "dma_start",
              AW[:, WDB[slotk]:WDB[slotk] + 512].rearrange("p (ii c) -> p ii c", ii=4),
              src[:, j * 128:(j + 1) * 128].rearrange("(ii p) c -> p ii c", p=128),
              inc=sem, incv=16)
        return (sem, S_.c[sem])

    ld_sh = wguw_dma(-1, 0)

    S_.wait("t", "ldc", ldc_moe)
    rt_stop = []
    sig_ev = []
    for th in range(NTH):
        ps = PS[2 + th % 2]
        if th >= 2:
            S_.wait("t", "act", sig_ev[th - 2])
        for i in range(NCH):
            S_.em("t", "matmul", ps[0:32, :], t["wr"][:, i * NE:(i + 1) * NE],
                  P.ych(i, th * 512), start=(i == 0), stop=(i == 7),
                  inc="pe" if i == 7 else None)
        rt_stop.append(S_.c["pe"])
        S_.wait("s", "pe", rt_stop[-1])
        S_.em("s", "activation", sqf[0:32, th * 512:(th + 1) * 512], ps[0:32, :],
              AF.Sigmoid, bias=t["br"][:, :], inc="act")
        sig_ev.append(S_.c["act"])

    afft_ev = []
    tr_pe = []
    for c in range(16):
        S_.wait("t", "act", sig_ev[min(3, c // 4)])
        if len(tr_pe) >= 2:
            S_.wait("t", "dve", afft_ev[c - 2])
        if c < 2:
            S_.wait("t", "act", sig_ev[2 + c % 2])
        S_.em("t", "transpose", PS[2 + c % 2][:, 0:32], sqf[0:32, c * 128:(c + 1) * 128],
              idf[0:32, 0:32], inc="pe")
        tr_pe.append(S_.c["pe"])
        S_.wait("v", "pe", tr_pe[-1])
        S_.em("v", "tensor_copy", AM[:, 512 + c * 32:512 + (c + 1) * 32],
              PS[2 + c % 2][:, 0:32], inc="dve")
        afft_ev.append(S_.c["dve"])

    wtr_pe = []
    wT_ev = []
    for c in range(16):
        afftc = AM[:, 512 + c * 32:512 + (c + 1) * 32]
        work = stat[:, 0:32]
        sel = stat[:, 32:64]
        msk = stat[:, 64:96]
        if wtr_pe:
            S_.wait("v", "pe", wtr_pe[-1])
        S_.em("v", "tensor_copy", work, afftc, inc="dve")
        S_.em("v", "memset", sel, 0.0, inc="dve")
        for it in range(8):
            S_.em("v", "reduce_max", stat[:, 96:97], work, axis=AX.X, inc="dve")
            S_.em("v", "tensor_scalar", msk, work, stat[:, 96:97], None, ALU.is_ge, inc="dve")
            S_.em("v", "tensor_tensor", sel, sel, msk, ALU.max, inc="dve")
            S_.em("v", "scalar_tensor_tensor", work, msk, -2.0, work, ALU.mult, ALU.add, inc="dve")
        S_.em("v", "tensor_tensor", msk, afftc, sel, ALU.mult, inc="dve")
        S_.em("v", "reduce_sum", stat[:, 96:97], msk, axis=AX.X, inc="dve")
        S_.em("v", "tensor_scalar_add", stat[:, 97:98], stat[:, 96:97], 1e-9, inc="dve")
        S_.em("v", "reciprocal", stat[:, 98:99], stat[:, 97:98], inc="dve")
        S_.em("v", "tensor_scalar", stat[:, 0:32], msk, stat[:, 98:99], None, ALU.mult, inc="dve")
        dve_w = S_.c["dve"]
        S_.wait("t", "dve", dve_w)
        if len(wtr_pe) >= 2:
            S_.wait("t", "act", wT_ev[c - 2])
        else:
            S_.wait("t", "dve", afft_ev[14 + c])
        S_.em("t", "transpose", PS[2 + c % 2][0:32, 0:128], stat[:, 0:32], idf[:, :], inc="pe")
        wtr_pe.append(S_.c["pe"])
        S_.wait("s", "pe", wtr_pe[-1])
        S_.em("s", "activation", ASQ[0:32, 2048 + c * 128:2048 + (c + 1) * 128],
              PS[2 + c % 2][0:32, 0:128], AF.Copy, inc="act")
        wT_ev.append(S_.c["act"])
    wT_done = wT_ev[-1]
    wT = ASQ[0:32, 2048:4096]

    SG = [0, 512]
    HT = [1024, 1536]
    HREG = [0, 8192]
    gu_idx = 0
    d_idx = 0
    wb_idx = 0
    h_done = {}
    dn_add = []
    racc_init = {}
    sil_ev = []
    upcons = []
    wb_ev = []
    wb_cons = []
    dn_stop = []
    last_dn_pe = 0
    ld_exp = [None] * NE
    ld_wd = {}

    def gate_up(which):
        nonlocal gu_idx, wb_idx
        slot = 0 if which == -1 else (which + 1) % 2
        sem, cnt, _ = ld_sh if which == -1 else ld_exp[which]
        S_.wait("t", sem, cnt)
        hreg = HREG[0 if which == -1 else which % 2]
        for th in range(NTH):
            if which >= 0:
                S_.wait("t", "act", wT_done)
                if wb_idx >= 2:
                    S_.wait("t", "dve", wb_cons[wb_idx - 2])
                S_.em("t", "matmul", PS[6 + wb_idx % 2][:, :], cst[0:32, which * 128:(which + 1) * 128],
                      wT[:, th * 512:(th + 1) * 512], start=True, stop=True, inc="pe")
                pe_wb = S_.c["pe"]
                S_.wait("s", "pe", pe_wb)
                if wb_idx >= 2:
                    S_.wait("s", "dve", wb_cons[wb_idx - 2])
                S_.em("s", "activation", miscb[:, (wb_idx % 2) * 512:(wb_idx % 2) * 512 + 512],
                      PS[6 + wb_idx % 2][:, :], AF.Copy, inc="act")
                wb_ev.append(S_.c["act"])
            for jj in range(4):
                psg = PS[0 + gu_idx % 2]
                psu = PS[2 + gu_idx % 2]
                if gu_idx >= 2:
                    S_.wait("t", "act", sil_ev[gu_idx - 2])
                    S_.wait("t", "dve", upcons[gu_idx - 2])
                elif which == -1 and gu_idx < 2:
                    S_.wait("t", "act", wT_ev[14 + gu_idx])
                for i in range(NCH):
                    S_.em("t", "matmul", psg[:, :], AW[:, WG[slot] + i * 512 + jj * 128: WG[slot] + i * 512 + (jj + 1) * 128],
                          P.ych(i, th * 512), start=(i == 0), stop=(i == 7),
                          inc="pe" if i == 7 else None)
                peg = S_.c["pe"]
                for i in range(NCH):
                    S_.em("t", "matmul", psu[:, :], AW[:, WU[slot] + i * 512 + jj * 128: WU[slot] + i * 512 + (jj + 1) * 128],
                          P.ych(i, th * 512), start=(i == 0), stop=(i == 7),
                          inc="pe" if i == 7 else None)
                peu = S_.c["pe"]
                bgc = eb[:, NE * 8 + jj:NE * 8 + jj + 1] if which == -1 else eb[:, which * 4 + jj:which * 4 + jj + 1]
                buc = eb[:, NE * 8 + 4 + jj:NE * 8 + 4 + jj + 1] if which == -1 else eb[:, NE * 4 + which * 4 + jj:NE * 4 + which * 4 + jj + 1]
                S_.wait("s", "pe", peg)
                if gu_idx >= 2:
                    S_.wait("s", "dve", upcons[gu_idx - 2])
                S_.em("s", "activation", ASQ[:, SG[gu_idx % 2]:SG[gu_idx % 2] + 512], psg[:, :],
                      AF.Silu, bias=bgc, inc="act")
                sil_ev.append(S_.c["act"])
                S_.wait("v", "pe", peu)
                S_.wait("v", "act", sil_ev[-1])
                hdst = AV[:, hreg + jj * S + th * 512: hreg + jj * S + (th + 1) * 512]
                if which == -1:
                    S_.em("v", "scalar_tensor_tensor", hdst, psu[:, :], buc,
                          ASQ[:, SG[gu_idx % 2]:SG[gu_idx % 2] + 512], ALU.add, ALU.mult, inc="dve")
                else:
                    S_.em("v", "scalar_tensor_tensor", ASQ[:, HT[gu_idx % 2]:HT[gu_idx % 2] + 512],
                          psu[:, :], buc, ASQ[:, SG[gu_idx % 2]:SG[gu_idx % 2] + 512],
                          ALU.add, ALU.mult, inc="dve")
                    S_.wait("v", "act", wb_ev[-1])
                    S_.em("v", "tensor_tensor", hdst,
                          ASQ[:, HT[gu_idx % 2]:HT[gu_idx % 2] + 512],
                          miscb[:, (wb_idx % 2) * 512:(wb_idx % 2) * 512 + 512], ALU.mult, inc="dve")
                upcons.append(S_.c["dve"])
                h_done[(0 if which == -1 else which % 2, th)] = S_.c["dve"]
                gu_idx += 1
            if which >= 0:
                wb_cons.append(S_.c["dve"])
                wb_idx += 1

    def downs(members):
        nonlocal d_idx, last_dn_pe
        for j in range(NCH):
            if j < 2:
                for mi, m in enumerate(members):
                    wd_band_dma(m, j, (j % 2) * 2 + mi)
                ld_wd[j] = ("wa" if j % 2 == 0 else "wb", S_.c["wa" if j % 2 == 0 else "wb"])
            for th in range(NTH):
                psd = PS[4 + d_idx % 2]
                if d_idx >= 2:
                    S_.wait("t", "dve", dn_add[d_idx - 2])
                first = True
                sem, cnt = ld_wd[j]
                S_.wait("t", sem, cnt)
                for mi, m in enumerate(members):
                    hreg = HREG[0 if m == -1 else m % 2]
                    S_.wait("t", "dve", h_done[(0 if m == -1 else m % 2, th)])
                    for ii in range(4):
                        S_.em("t", "matmul", psd[:, :],
                              AW[:, WDB[(j % 2) * 2 + mi] + ii * 128: WDB[(j % 2) * 2 + mi] + (ii + 1) * 128],
                              AV[:, hreg + ii * S + th * 512: hreg + ii * S + (th + 1) * 512],
                              start=first, stop=(mi == len(members) - 1 and ii == 3),
                              inc="pe" if (mi == len(members) - 1 and ii == 3) else None)
                        first = False
                dn_stop.append(S_.c["pe"])
                last_dn_pe = S_.c["pe"]
                rdst = racc[:, j * S + th * 512: j * S + (th + 1) * 512]
                if members == [-1]:
                    S_.wait("v", "pe", last_dn_pe)
                    S_.em("v", "tensor_scalar_add", rdst, psd[:, :],
                          eb[:, NE * 8 + 8 + j:NE * 8 + 8 + j + 1], inc="dve")
                    racc_init[(j, th)] = S_.c["dve"]
                    dn_add.append(S_.c["dve"])
                else:
                    S_.wait("v", "pe", last_dn_pe)
                    S_.em("v", "tensor_tensor", rdst, rdst, psd[:, :], ALU.add, inc="dve")
                    dn_add.append(S_.c["dve"])
                d_idx += 1
            if j + 2 < NCH:
                S_.wait("g", "pe", last_dn_pe)
                for mi, m in enumerate(members):
                    wd_band_dma(m, j + 2, ((j + 2) % 2) * 2 + mi)
                ld_wd[j + 2] = ("wa" if j % 2 == 0 else "wb", S_.c["wa" if j % 2 == 0 else "wb"])

    gate_up(-1)
    downs([-1])
    for e in range(NE):
        if e == 0:
            ld_exp[0] = wguw_dma(0, 0)
            S_.wait("g", "pe", S_.c["pe"])
            ld_exp[1] = wguw_dma(1, 0)
        elif e + 1 < NE:
            S_.wait("g", "pe", S_.c["pe"])
            ld_exp[e + 1] = wguw_dma(e + 1, 0)
        gate_up(e)
        if e % 2 == 1:
            downs([e - 1, e])
    for j in range(NCH):
        for th in range(NTH):
            psd = PS[4 + d_idx % 2]
            if d_idx >= 2:
                S_.wait("t", "dve", dn_add[d_idx - 2])
            S_.em("t", "matmul", psd[:, :], t["bdT"][:, j * 128:(j + 1) * 128],
                  wT[:, th * 512:(th + 1) * 512], start=True, stop=True, inc="pe")
            pj = S_.c["pe"]
            rdst = racc[:, j * S + th * 512: j * S + (th + 1) * 512]
            S_.wait("v", "pe", pj)
            S_.em("v", "tensor_tensor", rdst, rdst, psd[:, :], ALU.add, inc="dve")
            dn_add.append(S_.c["dve"])
            d_idx += 1
    dve_racc = S_.c["dve"]
    oev = []
    S_.wait("s", "dve", dve_racc)
    for c in range(NCH):
        S_.em("s", "activation", AV[:, c * S:(c + 1) * S], racc[:, c * S:(c + 1) * S], AF.Copy, inc="act")
        oev.append(S_.c["act"])
    return dict(o_done_act=oev[-1])


def build_dev_moe():
    nc = bass.Bass("TRN2", target_bir_lowering=False)
    P = Prog(nc, dev_io=True)
    S_ = P.S
    with contextlib.ExitStack() as stack:
        _make_sems(nc, S_, stack)
        block = stack.enter_context(nc.Block())
        load_constants(P)
        S_.em("g", "dma_start",
              P.t["y"].ap().rearrange("p (c t) -> p c t", c=NCH),
              P.d["dev_y"].rearrange("(c p) t -> p c t", p=128), inc="ldc", incv=16)
        ret = emit_moe(P)
        S_.wait("g", "act", ret["o_done_act"])
        S_.em("g", "dma_start",
              P.d["dev_o"].rearrange("(c p) t -> p c t", p=128),
              P.t["v"][:, 0:NCH * S].rearrange("p (c t) -> p c t", c=NCH),
              inc="st", incv=16)
        S_.wait("g", "st", S_.c["st"])
        _finish_queues(P, block)
    return nc



ROW_MU = (0, 512)
ROW_R = (1024, 1536)
ROW_B, ROW_C, ROW_V = 2048, 2560, 3072
ROW_BF = (3584, 3840)


def ln_rows(P, s_ps, q_ps, sl, pe_sum_id):
    S_ = P.S
    b1f = P.b1f
    mu = b1f[0:1, ROW_MU[sl % 2]:ROW_MU[sl % 2] + 512]
    rr = b1f[0:1, ROW_R[sl % 2]:ROW_R[sl % 2] + 512]
    B = b1f[0:1, ROW_B:ROW_B + 512]
    C = b1f[0:1, ROW_C:ROW_C + 512]
    V = b1f[0:1, ROW_V:ROW_V + 512]
    S_.wait("v", "pe", pe_sum_id)
    S_.em("v", "tensor_scalar_mul", mu, s_ps, 1.0 / 1024.0, inc="dve")
    dve_mu = S_.c["dve"]
    S_.em("v", "tensor_scalar_mul", B, q_ps, 1.0 / 1024.0, inc="dve")
    S_.em("v", "tensor_tensor", C, mu, mu, ALU.mult, inc="dve")
    S_.em("v", "tensor_tensor", V, B, C, ALU.subtract, inc="dve")
    S_.em("v", "tensor_scalar_add", C, V, LN_EPS, inc="dve")
    dve_v = S_.c["dve"]
    S_.wait("s", "dve", dve_v)
    S_.em("s", "activation", B, C, AF.Ln, inc="act")
    S_.wait("v", "act", S_.c["act"])
    S_.em("v", "tensor_scalar_mul", V, B, -0.5, inc="dve")
    S_.wait("s", "dve", S_.c["dve"])
    S_.em("s", "activation", rr, V, AF.Exp, inc="act")
    return dve_mu, S_.c["act"], mu, rr


def ln_rows_bf(P, sl, mu, rr, act_r):
    S_ = P.S
    mb = P.b1f.bitcast(BF16)
    mu_bf = mb[0:1, 2 * ROW_BF[0] + (sl % 2) * 1024: 2 * ROW_BF[0] + (sl % 2) * 1024 + 512]
    r_bf = mb[0:1, 2 * ROW_BF[0] + (sl % 2) * 1024 + 512: 2 * ROW_BF[0] + (sl % 2) * 1024 + 1024]
    S_.wait("s", "act", act_r)
    S_.em("s", "activation", mu_bf, mu, AF.Copy, inc="act")
    S_.em("s", "activation", r_bf, rr, AF.Copy, inc="act")
    return S_.c["act"], mu_bf, r_bf



def emit_wav(P, entry=(), dump=None):
    S_, t, d, PS = P.S, P.t, P.d, P.PS
    AY, AV, ASQ, AM, AX_ = t["y"], t["v"], t["sq"], t["misc"], t["epix"]
    miscb, filt = P.miscb, t["filt"]
    b1f, vf = P.b1f, P.vf
    for eng, sk, val in entry:
        S_.wait(eng, sk, val)

    def fcol(f, c, k):
        return filt[:, f * 32 + c * 4 + k: f * 32 + c * 4 + k + 1]

    def csyn(fs, c, s):
        return filt[:, 128 + fs * 24 + c * 3 + s: 128 + fs * 24 + c * 3 + s + 1]

    for fs in range(2):
        f = 2 + fs
        for c in range(8):
            base = 128 + fs * 24 + c * 3
            S_.em("v", "tensor_tensor", filt[:, base + 0:base + 1], fcol(f, c, 1), fcol(f, c, 2), ALU.add, inc="dve")
            S_.em("v", "tensor_tensor", filt[:, base + 1:base + 2], fcol(f, c, 0), fcol(f, c, 1), ALU.add, inc="dve")
            S_.em("v", "tensor_tensor", filt[:, base + 2:base + 3], fcol(f, c, 2), fcol(f, c, 3), ALU.add, inc="dve")
    dve_taps = S_.c["dve"]
    S_.wait("g", "dve", dve_taps)

    eng_inc = dict(v="dve", g="pool")
    pool_tmp = t["epix"][:, 1024:2048]

    def mac(eng, dst, srcap, f):
        if eng == "v":
            S_.em(eng, "scalar_tensor_tensor", dst, srcap, f, dst, ALU.mult, ALU.add, inc="dve")
        else:
            n = dst.ap[-1][1]
            S_.em(eng, "tensor_scalar_mul", pool_tmp[:, 0:n], srcap, f, inc="pool")
            S_.em(eng, "tensor_tensor", dst, dst, pool_tmp[:, 0:n], ALU.add, inc="pool")

    def conv_down(eng, src, dst, fidx, n_out):
        S_.em(eng, "tensor_scalar_mul", dst(0, n_out), src(0, n_out, 2), fcol(fidx, c, 1), inc=eng_inc[eng])
        mac(eng, dst(0, n_out), src(1, n_out, 2), fcol(fidx, c, 2))
        mac(eng, dst(1, n_out - 1), src(1, n_out - 1, 2), fcol(fidx, c, 0))
        mac(eng, dst(0, n_out - 1), src(2, n_out - 1, 2), fcol(fidx, c, 3))
        return S_.c[eng_inc[eng]]

    l0_ids, h0_ids = [None] * 8, [None] * 8
    for c in range(8):
        ysrc = lambda s0, n, st, c=c: AY[:, c * S + s0: c * S + s0 + (n - 1) * st + 1: st]
        l0d = lambda s0, n, c=c: b1f[:, c * 1024 + s0: c * 1024 + s0 + n]
        h0d = lambda s0, n, c=c: b1f[:, 8192 + c * 1024 + s0: 8192 + c * 1024 + s0 + n]
        l0_ids[c] = conv_down("v", ysrc, l0d, 0, 1024)
        h0_ids[c] = conv_down("v", ysrc, h0d, 1, 1024)
    l1_ids, h1_ids = [None] * 8, [None] * 8
    for c in range(8):
        lsrc = lambda s0, n, st, c=c: b1f[:, c * 1024 + s0: c * 1024 + s0 + (n - 1) * st + 1: st]
        l1d = lambda s0, n, c=c: vf[:, c * 512 + s0: c * 512 + s0 + n]
        h1d = lambda s0, n, c=c: vf[:, 4096 + c * 512 + s0: 4096 + c * 512 + s0 + n]
        l1_ids[c] = conv_down("v", lsrc, l1d, 0, 512)
        h1_ids[c] = conv_down("v", lsrc, h1d, 1, 512)
    if dump == "l1":
        return dict(dump_src=vf[:, 0:8192], done=("dve", l1_ids[7]), done2=("pool", h1_ids[7]))
    q_ids = [None] * 8
    qbf_ids = [None] * 8
    exp_prev = None
    pass
    for c in range(8):
        l1c = vf[:, c * 512:(c + 1) * 512]
        m0 = AM[:, 0:512]
        m1 = AM[:, 512:1024]
        if exp_prev is not None:
            S_.wait("v", "act", exp_prev)
        S_.em("v", "tensor_scalar_min", m0, l1c, 0.0, inc="dve")
        d0 = S_.c["dve"]
        S_.wait("s", "dve", d0)
        S_.em("s", "activation", m1, m0, AF.Exp, inc="act")
        exp_prev = S_.c["act"]
        S_.wait("v", "act", S_.c["act"])
        S_.em("v", "scalar_tensor_tensor", b1f[:, 4096 + c * 512:4096 + (c + 1) * 512], l1c, 0.0, m1,
              ALU.max, ALU.add, inc="dve")
        q_ids[c] = S_.c["dve"]
        S_.wait("s", "dve", q_ids[c])
        S_.em("s", "activation", ASQ[:, c * 512:(c + 1) * 512],
              b1f[:, 4096 + c * 512:4096 + (c + 1) * 512], AF.Copy, inc="act")
        qbf_ids[c] = S_.c["act"]

    S_.em("v", "memset", miscb[:, 256 + 64:256 + 260:65], 1.0, inc="dve")
    dve_vones = S_.c["dve"]
    la_done = []
    kvz_evs = []
    rb_evs = []
    prev_evs = {}
    import os as _os
    _nh = int(_os.environ.get('WAVH', str(H)))
    for h in range(_nh):
        hb = (h % 2) * 64
        hc = h // 2
        even = (h % 2 == 0)
        qh = ASQ[hb:hb + 64, hc * 512:(hc + 1) * 512]
        qhf = b1f[hb:hb + 64, 4096 + hc * 512:4096 + (hc + 1) * 512]
        l1h = vf[hb:hb + 64, hc * 512:(hc + 1) * 512]
        S_.wait("t", "dve", max(q_ids[hc], dve_vones))
        S_.wait("t", "act", qbf_ids[hc])
        ktm_ev, va_ev = [], []
        for ncc in range(4):
            psb = PS[ncc % 2][:, 0:64]
            if h > 0 or ncc >= 2:
                S_.wait("t", "act", prev_evs.get(("k", (h * 4 + ncc - 2)), 0))
            S_.em("t", "transpose", psb, qhf[:, ncc * 128:(ncc + 1) * 128],
                  t["idf"][hb:hb + 64, hb:hb + 64], inc="pe")
            pk = S_.c["pe"]
            S_.wait("s", "pe", pk)
            S_.em("s", "activation", miscb[:, ncc * 64:(ncc + 1) * 64], psb, AF.Copy, inc="act")
            ktm_ev.append(S_.c["act"])
            prev_evs[("k", h * 4 + ncc)] = S_.c["act"]
            psf = PS[2 + ncc % 2][:, 0:64]
            if h > 0 or ncc >= 2:
                S_.wait("t", "act", prev_evs.get(("v", (h * 4 + ncc - 2)), 0))
            S_.em("t", "transpose", psf, l1h[:, ncc * 128:(ncc + 1) * 128],
                  t["idf"][hb:hb + 64, hb:hb + 64], inc="pe")
            pv = S_.c["pe"]
            S_.wait("s", "pe", pv)
            S_.em("s", "activation", miscb[:, 256 + ncc * 65:256 + ncc * 65 + 64], psf, AF.Copy, inc="act")
            va_ev.append(S_.c["act"])
            prev_evs[("v", h * 4 + ncc)] = S_.c["act"]
        S_.wait("t", "act", va_ev[-1])
        S_.wait("t", "act", ktm_ev[-1])
        if h >= 2:
            S_.wait("t", "act", kvz_evs[h - 2])
        for ncc in range(4):
            S_.em("t", "matmul", PS[4][hb:hb + 64, 0:65], miscb[:, ncc * 64:(ncc + 1) * 64],
                  miscb[:, 256 + ncc * 65:256 + ncc * 65 + 65], start=(ncc == 0), stop=(ncc == 3),
                  inc="pe" if ncc == 3 else None)
        pe_kv = S_.c["pe"]
        S_.wait("s", "pe", pe_kv)
        S_.em("s", "activation", miscb[hb:hb + 64, 516:581], PS[4][hb:hb + 64, 0:65], AF.Copy, inc="act")
        act_kvz = S_.c["act"]
        kvz_evs.append(act_kvz)
        S_.wait("t", "act", act_kvz)
        if h >= 1:
            S_.wait("t", "dve", la_done[h - 1])
        if even:
            S_.em("t", "matmul", PS[5][0:65, :], miscb[0:64, 516:581], qh, start=True, stop=True, inc="pe")
            pe_nd = S_.c["pe"]
            denrow_ps = PS[5][64:65, :]
            db = 64
        else:
            S_.em("t", "matmul", PS[5][64:128, :], miscb[64:128, 516:580], qh, start=True, stop=True, inc="pe")
            S_.em("t", "matmul", PS[5][0:1, :], miscb[64:128, 580:581], qh, start=True, stop=True, inc="pe")
            pe_nd = S_.c["pe"]
            denrow_ps = PS[5][0:1, :]
            db = 0
        S_.wait("v", "pe", pe_nd)
        S_.em("v", "tensor_scalar_add", AX_[db:db + 1, 1024:1536], denrow_ps, 1e-6, inc="dve")
        dve_den = S_.c["dve"]
        S_.wait("s", "dve", dve_den)
        S_.em("s", "activation", AX_[db:db + 1, 1536:2048], AX_[db:db + 1, 1024:1536], AF.Ln, inc="act")
        S_.em("s", "activation", miscb[db:db + 1, 581:1093], AX_[db:db + 1, 1536:2048],
              AF.Exp, scale=-1.0, inc="act")
        dve_rec = ("act", S_.c["act"])
        S_.wait("t", dve_rec[0], dve_rec[1])
        if h >= 2:
            S_.wait("t", "act", rb_evs[h - 2])
        S_.em("t", "matmul", PS[6][hb:hb + 64, :], t["ones"][db:db + 1, 0:64],
              miscb[db:db + 1, 581:1093], start=True, stop=True, inc="pe")
        pe_bc = S_.c["pe"]
        S_.wait("s", "pe", pe_bc)
        S_.em("s", "activation", miscb[hb:hb + 64, 1093:1605], PS[6][hb:hb + 64, :], AF.Copy, inc="act")
        act_rb = S_.c["act"]
        rb_evs.append(act_rb)
        S_.wait("v", "act", act_rb)
        S_.em("v", "tensor_tensor", l1h, PS[5][hb:hb + 64, :], miscb[hb:hb + 64, 1093:1605],
              ALU.mult, inc="dve")
        la_done.append(S_.c["dve"])
    dve_la = S_.c["dve"]
    if dump == "lowa":
        return dict(dump_src=P.vf[:, 0:8192], done=("dve", dve_la))
    S_.wait("g", "dve", dve_la)

    syn1_v, syn1_g = [None] * 8, [None] * 8
    for c in range(8):
        la = lambda s0, n, c=c: vf[:, c * 512 + s0: c * 512 + s0 + n]
        hh = lambda s0, n, c=c: vf[:, 4096 + c * 512 + s0: 4096 + c * 512 + s0 + n]
        ev = lambda s0, n, c=c: b1f[:, c * 1024 + 2 * s0: c * 1024 + 2 * s0 + 2 * (n - 1) + 1: 2]
        od = lambda s0, n, c=c: b1f[:, c * 1024 + 1 + 2 * s0: c * 1024 + 1 + 2 * s0 + 2 * (n - 1) + 1: 2]
        S_.em("v", "tensor_scalar_mul", ev(0, 512), la(0, 512), csyn(0, c, 0), inc="dve")
        S_.em("v", "scalar_tensor_tensor", ev(1, 511), la(0, 511), fcol(2, c, 0), ev(1, 511), ALU.mult, ALU.add, inc="dve")
        S_.em("v", "scalar_tensor_tensor", ev(0, 511), la(1, 511), fcol(2, c, 3), ev(0, 511), ALU.mult, ALU.add, inc="dve")
        S_.em("v", "scalar_tensor_tensor", ev(0, 512), hh(0, 512), csyn(1, c, 0), ev(0, 512), ALU.mult, ALU.add, inc="dve")
        S_.em("v", "scalar_tensor_tensor", ev(1, 511), hh(0, 511), fcol(3, c, 0), ev(1, 511), ALU.mult, ALU.add, inc="dve")
        S_.em("v", "scalar_tensor_tensor", ev(0, 511), hh(1, 511), fcol(3, c, 3), ev(0, 511), ALU.mult, ALU.add, inc="dve")
        syn1_v[c] = S_.c["dve"]
        S_.em("v", "tensor_scalar_mul", od(0, 512), la(0, 512), csyn(0, c, 1), inc="dve")
        mac("v", od(0, 511), la(1, 511), csyn(0, c, 2))
        mac("v", od(0, 512), hh(0, 512), csyn(1, c, 1))
        mac("v", od(0, 511), hh(1, 511), csyn(1, c, 2))
        syn1_g[c] = S_.c["dve"]

    if dump == "low1":
        return dict(dump_src=b1f[:, 0:8192], done=("dve", syn1_v[7]), done2=("pool", syn1_g[7]))
    z_ids = [None] * 8
    pool_z = [None] * 8
    S_.wait("g", "dve", max(syn1_v[7], syn1_g[7]))
    for c in range(8):
        l1s = lambda s0, n, st=1, c=c: b1f[:, c * 1024 + s0: c * 1024 + s0 + (n - 1) * st + 1: st]
        h0s = lambda s0, n, st=1, c=c: b1f[:, 8192 + c * 1024 + s0: 8192 + c * 1024 + s0 + (n - 1) * st + 1: st]
        acc_e = AM[:, 0:1024]
        acc_o = AX_[:, 0:1024]
        if c > 0 and z_ids[c - 1] is not None:
            S_.wait("v", "act", z_ids[c - 1])
        S_.em("v", "tensor_copy", acc_e, AY[:, c * S: c * S + 2047: 2], inc="dve")
        S_.em("v", "scalar_tensor_tensor", acc_e, l1s(0, 1024), csyn(0, c, 0), acc_e, ALU.mult, ALU.add, inc="dve")
        S_.em("v", "scalar_tensor_tensor", acc_e[:, 1:1024], l1s(0, 1023), fcol(2, c, 0), acc_e[:, 1:1024], ALU.mult, ALU.add, inc="dve")
        S_.em("v", "scalar_tensor_tensor", acc_e[:, 0:1023], l1s(1, 1023), fcol(2, c, 3), acc_e[:, 0:1023], ALU.mult, ALU.add, inc="dve")
        S_.em("v", "scalar_tensor_tensor", acc_e, h0s(0, 1024), csyn(1, c, 0), acc_e, ALU.mult, ALU.add, inc="dve")
        S_.em("v", "scalar_tensor_tensor", acc_e[:, 1:1024], h0s(0, 1023), fcol(3, c, 0), acc_e[:, 1:1024], ALU.mult, ALU.add, inc="dve")
        S_.em("v", "scalar_tensor_tensor", acc_e[:, 0:1023], h0s(1, 1023), fcol(3, c, 3), acc_e[:, 0:1023], ALU.mult, ALU.add, inc="dve")
        dve_acc = S_.c["dve"]
        S_.wait("s", "dve", dve_acc)
        S_.em("s", "activation", AV[:, c * S: c * S + 2047: 2], acc_e, AF.Copy, inc="act")
        z_ids[c] = S_.c["act"]
        S_.em("g", "tensor_copy", acc_o, AY[:, c * S + 1: c * S + 2048: 2], inc="pool")
        S_.wait("v", "pool", S_.c["pool"])
        mac("v", acc_o, l1s(0, 1024), csyn(0, c, 1))
        mac("v", acc_o[:, 0:1023], l1s(1, 1023), csyn(0, c, 2))
        mac("v", acc_o, h0s(0, 1024), csyn(1, c, 1))
        mac("v", acc_o[:, 0:1023], h0s(1, 1023), csyn(1, c, 2))
        S_.wait("g", "dve", S_.c["dve"])
        S_.em("g", "tensor_copy", AV[:, c * S + 1: c * S + 2048: 2], acc_o, inc="pool")
        pool_z[c] = S_.c["pool"]
    act_z = z_ids[-1]
    pool_zl = pool_z[-1]
    if dump == "z":
        return dict(dump_src=AV[:, 0:16384], done=("act", act_z), done2=("pool", pool_zl), cast=True)

    oev = []
    sqs_ev = []
    sqmm_ids = []
    sum_pe = []
    rows_info = [None] * 4
    S_.wait("v", "pool", pool_zl)
    for sl in range(4):
        t0 = sl * 512
        for c in range(8):
            sqslot = ASQ[:, (c % 2) * 512:(c % 2) * 512 + 512]
            S_.wait("s", "act", act_z)
            S_.wait("s", "pool", pool_zl)
            if len(sqmm_ids) >= 2:
                S_.wait("s", "pe", sqmm_ids[-2])
            S_.em("s", "activation", sqslot, AV[:, c * S + t0: c * S + t0 + 512], AF.Square, inc="act")
            sqs_ev.append(S_.c["act"])
            S_.wait("t", "act", sqs_ev[-1])
            S_.wait("t", "pool", pool_zl)
            if sl > 0 and c == 0:
                S_.wait("t", "dve", rows_info[sl - 1][0])
            S_.em("t", "matmul", PS[0][:, :], t["ones"][:, :],
                  AV[:, c * S + t0: c * S + t0 + 512], start=(c == 0), stop=(c == 7), inc="pe")
            S_.em("t", "matmul", PS[1][:, :], t["ones"][:, :], sqslot,
                  start=(c == 0), stop=(c == 7), inc="pe")
            sqmm_ids.append(S_.c["pe"])
            if c == 7:
                sum_pe.append(S_.c["pe"])
        dve_mu, act_r, mu, rr = ln_rows(P, PS[0][0:1, :], PS[1][0:1, :], sl, sum_pe[-1])
        act_bf, mu_bf, r_bf = ln_rows_bf(P, sl, mu, rr, act_r)
        rows_info[sl] = (S_.c["dve"], act_bf)
        bmu = PS[2 + (sl % 2) * 2][:, :]
        brr = PS[3 + (sl % 2) * 2][:, :]
        S_.wait("t", "act", act_bf)
        if sl >= 2:
            S_.wait("t", "dve", oev[(sl - 2) * 8 + 7])
        S_.em("t", "matmul", bmu, t["ones"][0:1, :], mu_bf, start=True, stop=True, inc="pe")
        S_.em("t", "matmul", brr, t["ones"][0:1, :], r_bf, start=True, stop=True, inc="pe")
        pe_bc = S_.c["pe"]
        for c in range(8):
            zsl = AV[:, c * S + t0: c * S + t0 + 512]
            t1 = ASQ[:, 1024 + (c % 2) * 512:1024 + (c % 2) * 512 + 512]
            t2 = ASQ[:, 2048 + (c % 2) * 512:2048 + (c % 2) * 512 + 512]
            S_.wait("v", "pe", pe_bc)
            S_.em("v", "tensor_tensor", t1, zsl, bmu, ALU.subtract, inc="dve")
            S_.em("v", "tensor_tensor", t2, t1, brr, ALU.mult, inc="dve")
            S_.em("v", "tensor_scalar", zsl, t2, t["gb"][:, c:c + 1], t["gb"][:, 8 + c:8 + c + 1],
                  ALU.mult, ALU.add, inc="dve")
            oev.append(S_.c["dve"])
    return dict(o_done_dve=oev[-1])


def build_dev_wav(dump=None):
    nc = bass.Bass("TRN2", target_bir_lowering=False)
    P = Prog(nc, dev_io=True)
    S_ = P.S
    with contextlib.ExitStack() as stack:
        _make_sems(nc, S_, stack)
        block = stack.enter_context(nc.Block())
        load_constants(P)
        S_.em("g", "dma_start",
              P.t["y"].ap().rearrange("p (c t) -> p c t", c=NCH),
              P.d["dev_y"].rearrange("(c p) t -> p c t", p=128), inc="ldc", incv=16)
        n = S_.c["ldc"]
        ret = emit_wav(P, entry=[(e, "ldc", n) for e in ("v", "g", "s", "t")], dump=dump)
        if dump is not None:
            S_.wait("g", ret["done"][0], ret["done"][1])
            if "done2" in ret:
                S_.wait("g", ret["done2"][0], ret["done2"][1])
            n16 = ret["dump_src"].ap[1][1] if hasattr(ret["dump_src"], "ap") else 16384
            S_.em("g", "dma_start", P.d["dev_dump"][:, 0:(8192 if not ret.get("cast") else 16384)],
                  ret["dump_src"], inc="st", incv=16)
        else:
            S_.wait("g", "dve", ret["o_done_dve"])
            S_.em("g", "dma_start",
                  P.d["dev_o"].rearrange("(c p) t -> p c t", p=128),
                  P.t["v"][:, 0:NCH * S].rearrange("p (c t) -> p c t", c=NCH),
                  inc="st", incv=16)
        S_.wait("g", "st", S_.c["st"])
        _finish_queues(P, block)
    return nc



def emit_ffn(P, entry=(), tern=True):
    S_, t, d, PS = P.S, P.t, P.d, P.PS
    AY, AV, ASQ, AM, AX_ = t["y"], t["v"], t["sq"], t["misc"], t["epix"]
    b1f, wf = P.b1f, P.wf
    epixf = t["epix"]
    miscb = P.miscb
    stat = t["stat"]
    for eng, sk, val in entry:
        S_.wait(eng, sk, val)
    rr = t["b1"]

    S_.em("g", "dma_start", t["w"][:, 18432:18560].rearrange("p (c r) -> p c r", c=8),
          d["L1_ffn_vT"].rearrange("(c p) r -> p c r", p=128), inc="ldc", incv=16)
    S_.em("g", "dma_start", t["w"][0:16, 18816:18832], d["L1_ffn_fast_bT"], inc="ldc", incv=16)
    S_.em("g", "dma_start", t["g1"][0:1, 0:1], d["L1_ffn_gate"], inc="ldc", incv=16)
    ldc_ffn = S_.c["ldc"]

    S_.wait("t", "ldc", ldc_ffn)
    S_.em("t", "matmul", PS[3][0:16, 0:1], t["ones"][0:1, 0:16], t["g1"][0:1, 0:1],
          start=True, stop=True, inc="pe")
    S_.wait("s", "pe", S_.c["pe"])
    S_.em("s", "activation", stat[0:16, 99:100], PS[3][0:16, 0:1], AF.Copy, inc="act")
    act_g = S_.c["act"]

    oev = []
    last_abs_act = None
    stage_act = [None, None]
    last_mask_dve = None
    last_tr_pe = None
    trev_last = [None, None]
    lhsT_fc_pe = [None, None]
    last_rel_dve = [None, None]
    last_rel_act = [None, None]
    last_o_act = [None, None]
    last_proj_pe = None
    ld_fc = {}
    ld_uT = {}
    prev_t1ev = None
    wst_j = {}

    def fc_band_dma(j, cached):
        sem = "wa" if j % 2 == 0 else "wb"
        if cached:
            S_.wait("g", sem, S_.c[sem])
            S_.em("g", "dma_start", t["w"][:, 4096 + (j % 2) * 1024:4096 + (j % 2) * 1024 + 1024],
                  d["weff_fc"][j], inc=sem, incv=16)
        else:
            S_.em("g", "dma_start", wf[:, (j % 2) * 1024:(j % 2) * 1024 + 1024],
                  d["L1_ffn_w_fc"][j * 128:(j + 1) * 128, :], inc=sem, incv=16)
        ld_fc[j % 2] = (sem, S_.c[sem])

    def uT_dma(j):
        sem = "ma" if j % 2 == 0 else "mb"
        S_.em("g", "dma_start", t["w"][0:16, 18560 + (j % 2) * 128:18560 + (j % 2) * 128 + 128],
              d["L1_ffn_uT"][:, j * 128:(j + 1) * 128], inc=sem, incv=16)
        ld_uT[j % 2] = (sem, S_.c[sem])

    wst_total = [S_.c["wst"]]
    for half in range(2):
        use_tern = tern and half == 0
        th0 = half * 1024
        t2sb = miscb[0:16, 1024:2048]
        t1ev = []
        for th in range(2):
            if prev_t1ev is not None:
                S_.wait("t", "act", prev_t1ev)
            for i in range(NCH):
                S_.em("t", "matmul", PS[2][0:16, :], t["w"][:, 18432 + i * 16:18432 + (i + 1) * 16],
                      P.ych(i, th0 + th * 512), start=(i == 0), stop=(i == 7),
                      inc="pe" if i == 7 else None)
            S_.wait("s", "pe", S_.c["pe"])
            S_.em("s", "activation", ASQ[0:16, (th % 2) * 512:(th % 2) * 512 + 512],
                  PS[2][0:16, :], AF.Copy, inc="act")
            act_t1 = S_.c["act"]
            S_.wait("t", "act", act_t1)
            S_.em("t", "matmul", PS[3][0:16, :], t["w"][0:16, 18816:18832],
                  ASQ[0:16, (th % 2) * 512:(th % 2) * 512 + 512], start=True, stop=True, inc="pe")
            S_.wait("s", "pe", S_.c["pe"])
            S_.em("s", "activation", t2sb[:, th * 512:(th + 1) * 512], PS[3][0:16, :], AF.Copy,
                  scale=stat[0:16, 99:100], inc="act")
            t1ev.append(S_.c["act"])
        act_t2 = S_.c["act"]
        prev_t1ev = act_t2
        S_.wait("t", "act", act_t2)

        if not use_tern:
            S_.wait("g", "wst", wst_total[0])
        for j in range(NE):
            if j < 2:
                if half == 1:
                    S_.wait("g", "act", prev_abs_act if prev_abs_act is not None else 0)
                    S_.wait("g", "pe", last_proj_pe if last_proj_pe is not None else 0)
                if j == 0 and not use_tern and lhsT_fc_pe[0] is not None:
                    S_.wait("g", "pe", lhsT_fc_pe[0])
                    S_.wait("g", "pe", lhsT_fc_pe[1] or 0)
                fc_band_dma(j, not use_tern)
                uT_dma(j)
            sem, cnt = ld_fc[j % 2]
            if use_tern:
                S_.wait("s", sem, cnt)
                stage = wf[:, (j % 2) * 1024:(j % 2) * 1024 + 1024]
                absb = epixf[:, 0:1024]
                sgn = ASQ[:, 1024:2048]
                msk = AM.ap().bitcast(BF16)[:, 0:1024]
                weff = epixf[:, 1024:2048]
                if last_mask_dve is not None:
                    S_.wait("s", "dve", last_mask_dve)
                S_.em("s", "activation", absb, stage, AF.Abs, inc="act")
                S_.em("s", "activation", sgn, stage, AF.Sign, inc="act")
                act_abs = S_.c["act"]
                stage_act[j % 2] = act_abs
                S_.wait("v", "act", act_abs)
                S_.em("v", "reduce_sum", stat[:, 96:97], absb, axis=AX.X, inc="dve")
                S_.em("v", "tensor_scalar", stat[:, 97:98], stat[:, 96:97], 1.0 / 1024.0, 1e-5,
                      ALU.mult, ALU.max, inc="dve")
                S_.em("v", "tensor_scalar_mul", stat[:, 98:99], stat[:, 97:98], 0.5, inc="dve")
                S_.em("v", "tensor_scalar", msk, absb, stat[:, 98:99], None, ALU.is_ge, inc="dve")
                if last_tr_pe is not None:
                    S_.wait("v", "pe", last_tr_pe)
                S_.em("v", "scalar_tensor_tensor", weff, msk, stat[:, 97:98], sgn,
                      ALU.mult, ALU.mult, inc="dve")
                last_mask_dve = S_.c["dve"]
                nj = j + 2
                if nj < NE:
                    S_.wait("g", "act", act_abs)
                    fc_band_dma(nj, False)
                for i in range(NCH):
                    psx = PS[4 + i % 2][:, 0:128]
                    S_.wait("t", "dve", last_mask_dve)
                    if trev_last[i % 2] is not None:
                        S_.wait("t", "act", trev_last[i % 2])
                    S_.em("t", "transpose", psx, weff[:, i * 128:(i + 1) * 128], t["idf"][:, :], inc="pe")
                    last_tr_pe = S_.c["pe"]
                    S_.wait("s", "pe", last_tr_pe)
                    if lhsT_fc_pe[j % 2] is not None:
                        S_.wait("s", "pe", lhsT_fc_pe[j % 2])
                    if wst_j.get((0, j % 2)) is not None:
                        S_.wait("s", "wst", S_.c["wst"])
                    S_.em("s", "activation",
                          t["w"][:, 4096 + (j % 2) * 1024 + i * 128:4096 + (j % 2) * 1024 + (i + 1) * 128],
                          psx, AF.Copy, inc="act")
                    trev_last[i % 2] = S_.c["act"]
                S_.wait("sy", "act", trev_last[0])
                S_.wait("sy", "act", trev_last[1])
                S_.em("sy", "dma_start", d["weff_fc"][j],
                      t["w"][:, 4096 + (j % 2) * 1024:4096 + (j % 2) * 1024 + 1024], inc="wst", incv=16)
                wst_j[(0, j % 2)] = S_.c["wst"]
                wst_total[0] = S_.c["wst"]
            else:
                pass
            semu, cntu = ld_uT[j % 2]
            S_.wait("t", semu, cntu)
            if use_tern:
                S_.wait("t", "act", trev_last[0])
                S_.wait("t", "act", trev_last[1])
            else:
                S_.wait("t", sem, cnt)
            for th in range(2):
                ps = PS[th % 2]
                if last_rel_dve[th] is not None:
                    S_.wait("t", "dve", last_rel_dve[th])
                if j == 0 and last_o_act[th] is not None:
                    S_.wait("t", "act", last_o_act[th])
                for i in range(NCH):
                    S_.em("t", "matmul", ps[:, :],
                          t["w"][:, 4096 + (j % 2) * 1024 + i * 128:4096 + (j % 2) * 1024 + (i + 1) * 128],
                          P.ych(i, th0 + th * 512), start=(i == 0), stop=False)
                S_.em("t", "matmul", ps[:, :],
                      t["w"][0:16, 18560 + (j % 2) * 128:18560 + (j % 2) * 128 + 128],
                      t2sb[:, th * 512:(th + 1) * 512], start=False, stop=True, inc="pe")
                pe_h = S_.c["pe"]
                S_.wait("v", "pe", pe_h)
                S_.wait("v", "act", act_t2)
                rslot = ASQ[:, (th % 2) * 512:(th % 2) * 512 + 512]
                if last_rel_act[th] is not None:
                    S_.wait("v", "act", last_rel_act[th])
                S_.em("v", "tensor_scalar_max", rslot, ps[:, :], 0.0, inc="dve")
                last_rel_dve[th] = S_.c["dve"]
                S_.wait("s", "dve", last_rel_dve[th])
                S_.em("s", "activation", rr[:, j * 1024 + th * 512:j * 1024 + (th + 1) * 512],
                      rslot, AF.Square, inc="act")
                last_rel_act[th] = S_.c["act"]
            lhsT_fc_pe[j % 2] = S_.c["pe"]
            nj = j + 2
            if nj < NE:
                S_.wait("g", "pe", lhsT_fc_pe[j % 2])
                uT_dma(nj)
                if not use_tern:
                    fc_band_dma(nj, True)
        S_.wait("t", "act", last_rel_act[0])
        S_.wait("t", "act", last_rel_act[1])
        prev_tern_dve = None
        prev_abs_act = None
        tr_last_p = [None, None]
        PJ = [14336, 6144]
        for jd in range(NCH):
            sem = "wa" if jd % 2 == 0 else "wb"
            if not use_tern:
                if jd == 0:
                    S_.wait("g", "wst", wst_total[0])
                    for th_ in range(2):
                        if last_proj_pe is not None:
                            S_.wait("g", "pe", last_proj_pe)
                    S_.wait("g", "wa", S_.c["wa"])
                    S_.wait("g", "wb", S_.c["wb"])
                    S_.em("g", "dma_start", t["w"][:, PJ[0]:PJ[0] + 4096], d["weff_proj"][0], inc="wa", incv=16)
                    ld_pj0 = ("wa", S_.c["wa"])
                    S_.em("g", "dma_start", t["w"][:, PJ[1]:PJ[1] + 4096], d["weff_proj"][1], inc="wb", incv=16)
                    ld_pj1 = ("wb", S_.c["wb"])
                    ld_pj = [ld_pj0, ld_pj1]
                S_.wait("t", ld_pj[jd % 2][0], ld_pj[jd % 2][1])
                lhsT_base = PJ[jd % 2]
                for th in range(2):
                    ps = PS[th % 2]
                    if last_o_act[th] is not None:
                        S_.wait("t", "act", last_o_act[th])
                    elif last_rel_dve[th] is not None:
                        S_.wait("t", "dve", last_rel_dve[th])
                    for o in range(NE):
                        S_.em("t", "matmul", ps[:, :], t["w"][:, lhsT_base + o * 128:lhsT_base + (o + 1) * 128],
                              rr[:, o * 1024 + th * 512:o * 1024 + (th + 1) * 512],
                              start=(o == 0), stop=(o == NE - 1), inc="pe" if o == NE - 1 else None)
                    last_proj_pe = S_.c["pe"]
                    S_.wait("s", "pe", last_proj_pe)
                    S_.em("s", "activation", AV[:, jd * S + th0 + th * 512: jd * S + th0 + (th + 1) * 512],
                          ps[:, :], AF.Copy, inc="act")
                    oev.append(S_.c["act"])
                    last_o_act[th] = S_.c["act"]
                if jd + 2 < NCH:
                    S_.wait("g", "pe", last_proj_pe)
                    bsem = "wa" if jd % 2 == 0 else "wb"
                    S_.wait("g", bsem, S_.c[bsem])
                    S_.em("g", "dma_start", t["w"][:, PJ[jd % 2]:PJ[jd % 2] + 4096],
                          d["weff_proj"][jd + 2], inc=bsem, incv=16)
                    ld_pj[jd % 2] = (bsem, S_.c[bsem])
                continue
            if prev_abs_act is not None:
                S_.wait("g", "act", prev_abs_act)
            S_.wait("g", sem, S_.c[sem])
            S_.em("g", "dma_start", wf[:, 3072:7168], d["L1_ffn_w_proj"][jd * 128:(jd + 1) * 128, :],
                  inc=sem, incv=16)
            ldp = (sem, S_.c[sem])
            stage = wf[:, 3072:7168]
            S_.wait("s", ldp[0], ldp[1])
            if prev_tern_dve is not None:
                S_.wait("s", "dve", prev_tern_dve)
            S_.em("s", "activation", epixf[:, 0:2048], stage[:, 0:2048], AF.Abs, inc="act")
            S_.wait("v", "act", S_.c["act"])
            S_.em("v", "reduce_sum", stat[:, 96:97], epixf[:, 0:2048], axis=AX.X, inc="dve")
            S_.wait("s", "dve", S_.c["dve"])
            S_.em("s", "activation", epixf[:, 0:2048], stage[:, 2048:4096], AF.Abs, inc="act")
            S_.wait("v", "act", S_.c["act"])
            S_.em("v", "reduce_sum", stat[:, 95:96], epixf[:, 0:2048], axis=AX.X, inc="dve")
            S_.em("v", "tensor_tensor", stat[:, 96:97], stat[:, 96:97], stat[:, 95:96], ALU.add, inc="dve")
            S_.em("v", "tensor_scalar", stat[:, 97:98], stat[:, 96:97], 1.0 / 4096.0, 1e-5,
                  ALU.mult, ALU.max, inc="dve")
            S_.em("v", "tensor_scalar_mul", stat[:, 98:99], stat[:, 97:98], 0.5, inc="dve")
            dve_sc = S_.c["dve"]
            for pc in range(2):
                S_.wait("s", "dve", dve_sc)
                S_.em("s", "activation", epixf[:, 0:2048], stage[:, pc * 2048:(pc + 1) * 2048], AF.Abs, inc="act")
                S_.em("s", "activation", ASQ[:, 2048:4096], stage[:, pc * 2048:(pc + 1) * 2048], AF.Sign, inc="act")
                a3 = S_.c["act"]
                prev_abs_act = a3
                S_.wait("v", "act", a3)
                S_.em("v", "tensor_scalar", ASQ[:, 0:2048], epixf[:, 0:2048], stat[:, 98:99], None,
                      ALU.is_ge, inc="dve")
                if last_tr_pe is not None:
                    S_.wait("v", "pe", last_tr_pe)
                S_.em("v", "scalar_tensor_tensor", epixf[:, 0:2048], ASQ[:, 0:2048], stat[:, 97:98],
                      ASQ[:, 2048:4096], ALU.mult, ALU.mult, inc="dve")
                dve_we = S_.c["dve"]
                prev_tern_dve = dve_we
                for o2 in range(16):
                    o = pc * 16 + o2
                    psx = PS[4 + o % 2][:, 0:128]
                    S_.wait("t", "dve", dve_we)
                    if tr_last_p[o % 2] is not None:
                        S_.wait("t", "act", tr_last_p[o % 2])
                    elif trev_last[o % 2] is not None:
                        S_.wait("t", "act", trev_last[o % 2])
                    S_.em("t", "transpose", psx, epixf[:, o2 * 128:(o2 + 1) * 128], t["idf"][:, :], inc="pe")
                    last_tr_pe = S_.c["pe"]
                    S_.wait("s", "pe", last_tr_pe)
                    if last_proj_pe is not None and o < 2 and jd > 0:
                        S_.wait("s", "pe", last_proj_pe)
                    if o < 2 and jd > 0:
                        S_.wait("s", "wst", S_.c["wst"])
                    S_.em("s", "activation", t["w"][:, 14336 + o * 128:14336 + (o + 1) * 128], psx,
                          AF.Copy, inc="act")
                    tr_last_p[o % 2] = S_.c["act"]
            S_.wait("t", "act", tr_last_p[0])
            S_.wait("t", "act", tr_last_p[1])
            S_.wait("sy", "act", tr_last_p[0])
            S_.wait("sy", "act", tr_last_p[1])
            S_.em("sy", "dma_start", d["weff_proj"][jd], t["w"][:, 14336:14336 + 4096], inc="wst", incv=16)
            wst_total[0] = S_.c["wst"]
            wst_pj = S_.c["wst"]
            for th in range(2):
                ps = PS[th % 2]
                if last_o_act[th] is not None:
                    S_.wait("t", "act", last_o_act[th])
                elif last_rel_dve[th] is not None:
                    S_.wait("t", "dve", last_rel_dve[th])
                for o in range(NE):
                    S_.em("t", "matmul", ps[:, :], t["w"][:, 14336 + o * 128:14336 + (o + 1) * 128],
                          rr[:, o * 1024 + th * 512:o * 1024 + (th + 1) * 512],
                          start=(o == 0), stop=(o == NE - 1), inc="pe" if o == NE - 1 else None)
                last_proj_pe = S_.c["pe"]
                S_.wait("s", "pe", last_proj_pe)
                S_.em("s", "activation", AV[:, jd * S + th0 + th * 512: jd * S + th0 + (th + 1) * 512],
                      ps[:, :], AF.Copy, inc="act")
                oev.append(S_.c["act"])
                last_o_act[th] = S_.c["act"]
    return dict(o_done_act=oev[-1])


def build_dev_ffn():
    nc = bass.Bass("TRN2", target_bir_lowering=False)
    P = Prog(nc, dev_io=True)
    S_ = P.S
    with contextlib.ExitStack() as stack:
        _make_sems(nc, S_, stack)
        block = stack.enter_context(nc.Block())
        load_constants(P)
        S_.em("g", "dma_start",
              P.t["y"].ap().rearrange("p (c t) -> p c t", c=NCH),
              P.d["dev_y"].rearrange("(c p) t -> p c t", p=128), inc="ldc", incv=16)
        ret = emit_ffn(P)
        S_.wait("g", "act", ret["o_done_act"])
        S_.em("g", "dma_start",
              P.d["dev_o"].rearrange("(c p) t -> p c t", p=128),
              P.t["v"][:, 0:NCH * S].rearrange("p (c t) -> p c t", c=NCH),
              inc="st", incv=16)
        S_.wait("g", "st", S_.c["st"])
        _finish_queues(P, block)
    return nc



def phase_barrier(P, engines=("t", "s", "v", "g")):
    S_ = P.S
    snap = {k: S_.c[k] for k in ("pe", "act", "dve", "pool", "st", "x")}
    own = dict(t="pe", s="act", v="dve", g="pool")
    for eng in engines:
        for key, val in snap.items():
            if key != own[eng]:
                S_.wait(eng, key, val)
    return snap


def emit_prologue(P, x_src):
    S_, t, PS = P.S, P.t, P.PS
    AY, ASQ = t["y"], t["sq"]
    S_.em("g", "dma_start", AY.ap().rearrange("p (c t) -> p c t", c=NCH),
          x_src.rearrange("(c p) t -> p c t", p=128), inc="ldc", incv=16)
    ldx = S_.c["ldc"]
    S_.wait("s", "ldc", ldx)
    S_.wait("t", "ldc", ldx)
    S_.wait("v", "ldc", ldx)
    sqev = [None, None]
    mmlast = [None, None]
    rows_dve = None
    norm_ids = []
    for sl in range(4):
        t0 = sl * 512
        for c in range(NCH):
            slot = ASQ[:, (c % 2) * 512:(c % 2) * 512 + 512]
            if mmlast[c % 2] is not None:
                S_.wait("s", "pe", mmlast[c % 2])
            S_.em("s", "activation", slot, AY[:, c * S + t0:c * S + t0 + 512], AF.Square, inc="act")
            sqev[c % 2] = S_.c["act"]
            S_.wait("t", "act", sqev[c % 2])
            if sl > 0 and c == 0:
                S_.wait("t", "dve", rows_dve)
            S_.em("t", "matmul", PS[0][:, :], t["ones"][:, :], AY[:, c * S + t0:c * S + t0 + 512],
                  start=(c == 0), stop=(c == 7), inc="pe")
            S_.em("t", "matmul", PS[1][:, :], t["ones"][:, :], slot,
                  start=(c == 0), stop=(c == 7), inc="pe")
            mmlast[c % 2] = S_.c["pe"]
        dve_mu, act_r, mu, rr_ = ln_rows(P, PS[0][0:1, :], PS[1][0:1, :], sl, mmlast[1])
        rows_dve = S_.c["dve"]
        act_bf, mu_bf, r_bf = ln_rows_bf(P, sl, mu, rr_, act_r)
        bmu = PS[2 + (sl % 2) * 2][:, :]
        brr = PS[3 + (sl % 2) * 2][:, :]
        S_.wait("t", "act", act_bf)
        if sl >= 2:
            S_.wait("t", "dve", norm_ids[-1])
        S_.em("t", "matmul", bmu, t["ones"][0:1, :], mu_bf, start=True, stop=True, inc="pe")
        S_.em("t", "matmul", brr, t["ones"][0:1, :], r_bf, start=True, stop=True, inc="pe")
        pe_bc = S_.c["pe"]
        for c in range(NCH):
            ysl = AY[:, c * S + t0:c * S + t0 + 512]
            t1 = ASQ[:, 1024 + (c % 2) * 512:1024 + (c % 2) * 512 + 512]
            S_.wait("v", "pe", pe_bc)
            S_.em("v", "tensor_tensor", t1, ysl, bmu, ALU.subtract, inc="dve")
            S_.em("v", "tensor_tensor", ysl, t1, brr, ALU.mult, inc="dve")
            norm_ids.append(S_.c["dve"])
    return dict(y_done_dve=S_.c["dve"])


def emit_epilogue(P, x_src, xn_dst, want_y, y_hbm=None):
    S_, t, PS = P.S, P.t, P.PS
    AY, AV, ASQ = t["y"], t["v"], t["sq"]
    b1b = P.b1f.bitcast(BF16)
    epixf = t["epix"]
    b1f = P.b1f
    for key in ("pe", "act", "dve", "pool"):
        S_.wait("sy", key, S_.c[key])
    sqev = [None, None]
    mmlast = [None, None]
    rows_dve = None
    bc_ev = []
    for sl in range(4):
        t0 = sl * 512
        for c in range(NCH):
            slot = ASQ[:, (c % 2) * 512:(c % 2) * 512 + 512]
            if mmlast[c % 2] is not None:
                S_.wait("s", "pe", mmlast[c % 2])
            S_.em("s", "activation", slot, AV[:, c * S + t0:c * S + t0 + 512], AF.Square, inc="act")
            sqev[c % 2] = S_.c["act"]
            S_.wait("t", "act", sqev[c % 2])
            if sl > 0 and c == 0:
                S_.wait("t", "dve", rows_dve)
            S_.em("t", "matmul", PS[0][:, :], t["ones"][:, :], AV[:, c * S + t0:c * S + t0 + 512],
                  start=(c == 0), stop=(c == 7), inc="pe")
            S_.em("t", "matmul", PS[1][:, :], t["ones"][:, :], slot,
                  start=(c == 0), stop=(c == 7), inc="pe")
            mmlast[c % 2] = S_.c["pe"]
        dve_mu, act_r, mu, rr_ = ln_rows(P, PS[0][0:1, :], PS[1][0:1, :], sl, mmlast[1])
        rows_dve = S_.c["dve"]
        act_bf, mu_bf, r_bf = ln_rows_bf(P, sl, mu, rr_, act_r)
        bmu = PS[2 + (sl % 2) * 2][:, :]
        brr = PS[3 + (sl % 2) * 2][:, :]
        S_.wait("t", "act", act_bf)
        if sl >= 2:
            S_.wait("t", "act", bc_ev[sl - 2])
        S_.em("t", "matmul", bmu, t["ones"][0:1, :], mu_bf, start=True, stop=True, inc="pe")
        S_.em("t", "matmul", brr, t["ones"][0:1, :], r_bf, start=True, stop=True, inc="pe")
        pe_bc = S_.c["pe"]
        S_.wait("s", "pe", pe_bc)
        S_.em("s", "activation", b1b[:, 16384 + sl * 512:16384 + (sl + 1) * 512], bmu, AF.Copy, inc="act")
        S_.em("s", "activation", b1b[:, 20480 + sl * 512:20480 + (sl + 1) * 512], brr, AF.Copy, inc="act")
        bc_ev.append(S_.c["act"])
    act_bcast = S_.c["act"]
    S_.wait("t", "dve", rows_dve)
    S_.wait("t", "act", act_bcast)
    xn_adds = []
    s2stop = [None] * 4
    xn_bf_acts = []
    chunk_act = [None] * NCH
    chunk_dve = [None] * NCH
    s2mm = [None, None]
    st_ids = []
    xstage = [epixf[:, :], b1f[:, 11264:13312]]
    xnbuf = [b1f[:, 6144:8192], b1f[:, 13312:15360]]
    ldx_ids = [None] * NCH
    for c in (0, 1):
        S_.em("sy", "dma_start", xstage[c % 2], x_src[c * 128:(c + 1) * 128, :], inc="x", incv=16)
        ldx_ids[c] = S_.c["x"]
    for c in range(NCH):
        S_.wait("v", "x", ldx_ids[c])
        S_.wait("v", "act", act_bcast)
        if c >= 2:
            S_.wait("v", "st", st_ids[c - 2])
            if chunk_act[c - 2] is not None:
                S_.wait("v", "act", chunk_act[c - 2])
        for sl in range(4):
            t0 = sl * 512
            osl = AV[:, c * S + t0:c * S + t0 + 512]
            t1 = ASQ[:, 1024 + (sl % 2) * 512:1024 + (sl % 2) * 512 + 512]
            t2 = ASQ[:, 2048 + (sl % 2) * 512:2048 + (sl % 2) * 512 + 512]
            S_.em("v", "tensor_tensor", t1, osl, b1b[:, 16384 + sl * 512:16384 + (sl + 1) * 512],
                  ALU.subtract, inc="dve")
            S_.em("v", "tensor_tensor", t2, t1, b1b[:, 20480 + sl * 512:20480 + (sl + 1) * 512],
                  ALU.mult, inc="dve")
            S_.em("v", "tensor_tensor", xnbuf[c % 2][:, t0:t0 + 512], xstage[c % 2][:, t0:t0 + 512],
                  t2, ALU.add, inc="dve")
            xn_adds.append(S_.c["dve"])
            if want_y:
                S_.wait("s", "dve", xn_adds[-1])
                S_.em("s", "activation", AY[:, c * S + t0:c * S + t0 + 512],
                      xnbuf[c % 2][:, t0:t0 + 512], AF.Copy, inc="act")
                if s2mm[sl % 2] is not None:
                    S_.wait("s", "pe", s2mm[sl % 2])
                S_.em("s", "activation", ASQ[:, (sl % 2) * 512:(sl % 2) * 512 + 512],
                      AY[:, c * S + t0:c * S + t0 + 512], AF.Square, inc="act")
                xn_bf_acts.append(S_.c["act"])
                S_.wait("t", "act", xn_bf_acts[-1])
                S_.em("t", "matmul", PS[sl][:, :], t["ones"][:, :],
                      AY[:, c * S + t0:c * S + t0 + 512], start=(c == 0), stop=(c == 7), inc="pe")
                S_.em("t", "matmul", PS[4 + sl][:, :], t["ones"][:, :],
                      ASQ[:, (sl % 2) * 512:(sl % 2) * 512 + 512], start=(c == 0), stop=(c == 7), inc="pe")
                s2mm[sl % 2] = S_.c["pe"]
                if c == 7:
                    s2stop[sl] = S_.c["pe"]
        chunk_dve[c] = xn_adds[-1]
        chunk_act[c] = S_.c["act"] if want_y else None
        S_.wait("sy", "dve", chunk_dve[c])
        S_.em("sy", "dma_start", xn_dst[c * 128:(c + 1) * 128, :], xnbuf[c % 2], inc="st", incv=16)
        st_ids.append(S_.c["st"])
        if c + 2 < NCH:
            S_.em("sy", "dma_start", xstage[c % 2], x_src[(c + 2) * 128:(c + 3) * 128, :],
                  inc="x", incv=16)
            ldx_ids[c + 2] = S_.c["x"]
    st_xn = S_.c["st"]
    if not want_y:
        return dict(st_done=st_xn)
    ydone = []
    for sl in range(4):
        t0 = sl * 512
        dve_mu, act_r, mu, rr_ = ln_rows(P, PS[sl][0:1, :], PS[4 + sl][0:1, :], sl, s2stop[sl])
        act_bf, mu_bf, r_bf = ln_rows_bf(P, sl, mu, rr_, act_r)
        bmu = PS[sl][:, :]
        brr = PS[4 + sl][:, :]
        S_.wait("t", "act", act_bf)
        S_.wait("t", "dve", S_.c["dve"])
        S_.em("t", "matmul", bmu, t["ones"][0:1, :], mu_bf, start=True, stop=True, inc="pe")
        S_.em("t", "matmul", brr, t["ones"][0:1, :], r_bf, start=True, stop=True, inc="pe")
        pe_bc = S_.c["pe"]
        for c in range(NCH):
            ysl = AY[:, c * S + t0:c * S + t0 + 512]
            t1 = ASQ[:, 1024 + (c % 2) * 512:1024 + (c % 2) * 512 + 512]
            S_.wait("v", "pe", pe_bc)
            S_.em("v", "tensor_tensor", t1, ysl, bmu, ALU.subtract, inc="dve")
            S_.em("v", "tensor_tensor", ysl, t1, brr, ALU.mult, inc="dve")
            ydone.append(S_.c["dve"])
    ret = dict(st_done=st_xn, y_done_dve=ydone[-1])
    if y_hbm is not None:
        S_.wait("sy", "dve", ydone[-1])
        S_.em("sy", "dma_start", y_hbm.rearrange("(c p) t -> p c t", p=128),
              AY.ap().rearrange("p (c t) -> p c t", c=NCH), inc="st", incv=16)
        ret["st_done"] = S_.c["st"]
    return ret



BLOCK_EMIT = dict(att=None, moe=None, wav=None, ffn=None)


def build_stage(stage):
    nc = bass.Bass("TRN2", target_bir_lowering=False)
    P = Prog(nc, stage=stage)
    S_ = P.S
    d = P.d
    with contextlib.ExitStack() as stack:
        _make_sems(nc, S_, stack)
        block = stack.enter_context(nc.Block())
        load_constants(P)
        for b in range(2):
            if stage == "att":
                emit_prologue(P, d["xin"][b])
                phase_barrier(P)
                emit_att(P)
                phase_barrier(P)
                ret = emit_epilogue(P, d["xin"][b], d["s_xn"][b], True, y_hbm=d["s_yout"][b])
            else:
                gate = S_.c["dve"]
                S_.wait("g", "dve", gate)
                S_.em("g", "dma_start", P.t["y"].ap().rearrange("p (c t) -> p c t", c=NCH),
                      d["s_y"][b].rearrange("(c p) t -> p c t", p=128), inc="ldc", incv=16)
                ldy = S_.c["ldc"]
                if stage == "moe":
                    emit_moe(P)
                elif stage == "wav":
                    ret0 = emit_wav(P, entry=[(e, "ldc", ldy) for e in ("v", "g", "s", "t")])
                    phase_barrier(P)
                elif stage == "ffn":
                    emit_ffn(P)
                phase_barrier(P)
                if stage == "ffn":
                    ret = emit_epilogue(P, d["s_xprev"][b], d["xout"][b], False)
                else:
                    ret = emit_epilogue(P, d["s_xprev"][b], d["s_xn"][b], True, y_hbm=d["s_yout"][b])
            phase_barrier(P)
            S_.wait("g", "st", S_.c["st"])
            S_.wait("g", "x", S_.c["x"])
        S_.wait("sy", "st", S_.c["st"])
        _finish_queues(P, block)
    return nc


def build_full():
    nc = bass.Bass("TRN2", target_bir_lowering=False)
    P = Prog(nc)
    S_ = P.S
    d = P.d
    with contextlib.ExitStack() as stack:
        _make_sems(nc, S_, stack)
        block = stack.enter_context(nc.Block())
        load_constants(P)
        for b in range(2):
            emit_prologue(P, d["xin"][b])
            phase_barrier(P)
            ratt = emit_att(P)
            phase_barrier(P)
            emit_epilogue(P, d["xin"][b], d["xscr"][0], True)
            phase_barrier(P)
            emit_moe(P, gates=[("g", "dve", ratt["rope_dve"])])
            phase_barrier(P)
            emit_epilogue(P, d["xscr"][0], d["xscr"][1], True)
            phase_barrier(P)
            emit_wav(P)
            phase_barrier(P)
            emit_epilogue(P, d["xscr"][1], d["xscr"][0], True)
            phase_barrier(P)
            emit_ffn(P, tern=(b == 0))
            phase_barrier(P)
            emit_epilogue(P, d["xscr"][0], d["xout"][b], False)
            phase_barrier(P)
        S_.wait("sy", "st", S_.c["st"])
        S_.wait("sy", "x", S_.c["x"])
        _finish_queues(P, block)
    return nc


def kernel_fused(**inputs):
    import ml_dtypes
    consts = _host_constants()
    w = _pack_weights(inputs)
    maps = [_core_inputs(inputs, c, consts, w) for c in range(_CORES)]
    nc = build_full()
    res = run_bass_kernel_spmd(nc, maps, core_ids=list(range(_CORES)))
    out = np.empty((16, S, D), dtype=np.float32)
    for c in range(_CORES):
        xo = np.asarray(res.results[c]["xout"])
        out[2 * c:2 * c + 2] = xo.transpose(0, 2, 1)
    return out


_CORES = 8


def _core_inputs(inputs, core, consts, w):
    x = np.asarray(inputs["L0_attn_x"])
    xb = np.ascontiguousarray(x[2 * core:2 * core + 2].transpose(0, 2, 1)).astype(np.float32)
    m = {"xin": xb}
    m.update(consts)
    m.update(w)
    return m


def kernel(**inputs):
    return kernel_fused(**inputs)


def kernel_staged(**inputs):
    import ml_dtypes
    consts = _host_constants()
    w = _pack_weights(inputs)
    maps = [_core_inputs(inputs, c, consts, w) for c in range(_CORES)]
    state = [dict() for _ in range(_CORES)]
    for stage in ("att", "moe", "wav", "ffn"):
        nc = build_stage(stage)
        ins = []
        for c in range(_CORES):
            mm = dict(maps[c])
            if stage != "att":
                mm["s_y"] = state[c]["s_yout"]
                mm["s_xprev"] = state[c]["s_xn"]
            ins.append(mm)
        res = run_bass_kernel_spmd(nc, ins, core_ids=list(range(_CORES)))
        for c in range(_CORES):
            state[c] = {k: np.asarray(v) for k, v in res.results[c].items()}
    out = np.empty((16, S, D), dtype=np.float32)
    for c in range(_CORES):
        xo = state[c]["xout"]
        out[2 * c:2 * c + 2] = xo.transpose(0, 2, 1)
    return out



def _host_constants():
    c = {}
    t = np.arange(S, dtype=np.float64)
    invf = 1.0 / (10000.0 ** (np.arange(0, DH, 2, dtype=np.float64) / DH))
    ang = np.outer(np.concatenate([invf, invf]), t)
    cos = np.cos(ang).astype(np.float32)
    sin = np.sin(ang).astype(np.float32)
    c["c_cos"] = np.concatenate([cos, cos], 0)
    c["c_sin"] = np.concatenate([sin, sin], 0)
    p2 = np.zeros((128, 128), dtype=np.float32)
    for half in range(2):
        o = half * 64
        for p in range(32):
            p2[o + p + 32, o + p] = -1.0
            p2[o + p, o + p + 32] = 1.0
    c["c_p2"] = p2
    c["c_id"] = np.eye(128, dtype=np.float32)
    c["c_ones"] = np.ones((128, 128), dtype=np.float32)
    oh = np.zeros((32, NE * 128), dtype=np.float32)
    for e in range(NE):
        oh[e, e * 128:(e + 1) * 128] = 1.0
    c["c_oh"] = oh
    return c


def _pack_weights(inputs):
    w = {}
    w["L0_attn_Wqkv"] = np.ascontiguousarray(inputs["L0_attn_Wqkv"])
    w["L0_attn_Wo"] = np.ascontiguousarray(inputs["L0_attn_Wo"])
    w["L0_ffn_Wr"] = np.ascontiguousarray(inputs["L0_ffn_Wr"])
    w["L0_ffn_br"] = np.ascontiguousarray(inputs["L0_ffn_br"].reshape(NE, 1))
    for k in ("Wg_s", "Wu_s", "Wd_s"):
        w[f"L0_ffn_{k}"] = np.ascontiguousarray(inputs[f"L0_ffn_{k}"])
    for k in ("bg_s", "bu_s"):
        w[f"L0_ffn_{k}"] = np.ascontiguousarray(inputs[f"L0_ffn_{k}"].reshape(4, 128).T)
    w["L0_ffn_bd_s"] = np.ascontiguousarray(inputs["L0_ffn_bd_s"].reshape(8, 128).T)
    for k in ("Wg", "Wu", "Wd"):
        w[f"L0_ffn_{k}"] = np.ascontiguousarray(inputs[f"L0_ffn_{k}"])
    for k in ("bg", "bu"):
        b = inputs[f"L0_ffn_{k}"].reshape(NE, 4, 128)
        w[f"L0_ffn_{k}"] = np.ascontiguousarray(b.transpose(2, 0, 1).reshape(128, NE * 4))
    w["L0_ffn_bd"] = np.ascontiguousarray(inputs["L0_ffn_bd"])
    filt = np.zeros((128, 128), dtype=np.float32)
    for fi, key in enumerate(["L1_attn_h_filter", "L1_attn_g_filter", "L1_attn_h_synth", "L1_attn_g_synth"]):
        f = inputs[key]
        for c in range(8):
            for k in range(4):
                filt[:, fi * 32 + c * 4 + k] = f[k, c * 128:(c + 1) * 128]
    w["L1_attn_filt"] = filt
    w["L1_attn_gamma"] = np.ascontiguousarray(inputs["L1_attn_norm_gamma"].reshape(8, 128).T)
    w["L1_attn_beta"] = np.ascontiguousarray(inputs["L1_attn_norm_beta"].reshape(8, 128).T)
    w["L1_ffn_fast_bT"] = np.ascontiguousarray(inputs["L1_ffn_fast_b"].T)
    w["L1_ffn_w_fc"] = np.ascontiguousarray(inputs["L1_ffn_w_fc"])
    w["L1_ffn_w_proj"] = np.ascontiguousarray(inputs["L1_ffn_w_proj"])
    w["L1_ffn_uT"] = np.ascontiguousarray(inputs["L1_ffn_u"].T)
    w["L1_ffn_vT"] = np.ascontiguousarray(inputs["L1_ffn_v"].T)
    w["L1_ffn_gate"] = np.asarray(inputs["L1_ffn_gate"], dtype=np.float32).reshape(1, 1)
    return w


def _finish_queues(P, block):
    S_ = P.S
    import contextlib
    lp = P.nc.allow_low_precision(reason="bf16 operands with f32 psum accumulation by design")
    lp.__enter__()
    if S_.q["g"]:
        @block.gpsimd
        def _(g):
            for fn in S_.q["g"]:
                fn(g)
    if S_.q["t"]:
        @block.tensor
        def _(t_):
            for fn in S_.q["t"]:
                fn(t_)
    if S_.q["s"]:
        @block.scalar
        def _(s_):
            for fn in S_.q["s"]:
                fn(s_)
    if S_.q["v"]:
        @block.vector
        def _(v_):
            for fn in S_.q["v"]:
                fn(v_)
    if S_.q["sy"]:
        @block.sync
        def _(sy_):
            for fn in S_.q["sy"]:
                fn(sy_)
    lp.__exit__(None, None, None)


import contextlib


def _make_sems(nc, S_, stack):
    keys = ("ldc", "wa", "wb", "ma", "mb", "x", "st", "wst", "pe", "act", "dve", "pool")
    S_.sem = {k: stack.enter_context(nc.semaphore(f"s_{k}")) for k in keys}


def build_dev_att():
    nc = bass.Bass("TRN2", target_bir_lowering=False)
    P = Prog(nc, dev_io=True)
    S_ = P.S
    with contextlib.ExitStack() as stack:
        _make_sems(nc, S_, stack)
        block = stack.enter_context(nc.Block())
        load_constants(P)
        S_.em("g", "dma_start",
              P.t["y"].ap().rearrange("p (c t) -> p c t", c=NCH),
              P.d["dev_y"].rearrange("(c p) t -> p c t", p=128), inc="ldc", incv=16)
        ldy = S_.c["ldc"]
        ret = emit_att(P)
        S_.wait("g", "act", ret["o_done_act"])
        S_.em("g", "dma_start",
              P.d["dev_o"].rearrange("(c p) t -> p c t", p=128),
              P.t["v"][:, 0:NCH * S].rearrange("p (c t) -> p c t", c=NCH),
              inc="st", incv=16)
        S_.wait("g", "st", S_.c["st"])
        _finish_queues(P, block)
    return nc



